# version 18
# baseline (speedup 1.0000x reference)
import numpy as np
import ml_dtypes
from contextlib import ExitStack

import concourse.bass as bass
import concourse.mybir as mybir
from concourse.bass_utils import run_bass_kernel_spmd

F32 = mybir.dt.float32
BF16 = mybir.dt.bfloat16
AF = mybir.ActivationFunctionType
ALU = mybir.AluOpType

B = 4
T = 4096
D = 1024
LT = 256
NC = 8
EPS = 1e-5


def build_program():
    nc = bass.Bass("TRN2", target_bir_lowering=False, debug=False,
                   detect_race_conditions=False)

    d_xkv = nc.dram_tensor("xkv", [B, D, 512], F32, kind="ExternalInput").ap()
    d_xq = nc.dram_tensor("xq", [B, D, 512], F32, kind="ExternalInput").ap()
    d_wq = nc.dram_tensor("wq", [D, D], F32, kind="ExternalInput").ap()
    d_wdkv = nc.dram_tensor("wdkv", [D, LT], F32, kind="ExternalInput").ap()
    d_wuk = nc.dram_tensor("wuk", [LT, D], F32, kind="ExternalInput").ap()
    d_wuv = nc.dram_tensor("wuv", [LT, D], F32, kind="ExternalInput").ap()
    d_wo = nc.dram_tensor("wo", [D, D], F32, kind="ExternalInput").ap()
    d_fwT = nc.dram_tensor("fwT", [8, D, D], F32, kind="ExternalInput").ap()
    d_bff = nc.dram_tensor("bff", [128, 32], F32, kind="ExternalInput").ap()
    d_incol = nc.dram_tensor("incol", [128, 8], F32, kind="ExternalInput").ap()
    d_gcol = nc.dram_tensor("gcol", [128, 2], F32, kind="ExternalInput").ap()
    d_bcol = nc.dram_tensor("bcol", [128, 2], F32, kind="ExternalInput").ap()
    d_ropek = nc.dram_tensor("ropek", [4, 128, 1024], F32, kind="ExternalInput").ap()
    d_ropeq = nc.dram_tensor("ropeq", [4, 128, 1024], F32, kind="ExternalInput").ap()
    d_maskb = nc.dram_tensor("maskb", [128, 128], BF16, kind="ExternalInput").ap()
    d_idt = nc.dram_tensor("idt", [128, 128], F32, kind="ExternalInput").ap()

    d_kvsh = nc.dram_tensor("kvsh", [B, 8, 2, 128, 4, 128], BF16).ap()
    d_kvfull = nc.dram_tensor("kvfull", [B, NC, 8, 2, 128, 4, 128], BF16,
                              addr_space="Shared").ap()
    d_stsh = nc.dram_tensor("stsh", [B, 128], F32).ap()
    d_stfull = nc.dram_tensor("stfull", [B, NC, 128], F32).ap()
    d_fcache = nc.dram_tensor("fcache", [8, 8, 128, 8, 128], BF16).ap()
    d_xmid32 = nc.dram_tensor("xmid32", [B, 8, 128, 512], F32).ap()
    d_xmid16 = nc.dram_tensor("xmid16", [B, 8, 128, 512], BF16).ap()
    d_yT = nc.dram_tensor("yT", [B, 8, 128, 512], F32, kind="ExternalOutput").ap()

    ctx = ExitStack()
    sb = lambda name, shape, dt: ctx.enter_context(nc.sbuf_tensor(name, shape, dt))
    ps = lambda name, shape: ctx.enter_context(nc.psum_tensor(name, shape, F32))

    on_mat = sb("on_mat", [128, 64], BF16)
    on_row = sb("on_row", [1, 128], F32)
    on_col = sb("on_col", [128, 1], F32)
    w_o = sb("w_o", [128, 8, 1024], BF16)
    bffs = sb("bffs", [128, 32], F32)
    incs = sb("incs", [128, 8], F32)
    gcs = sb("gcs", [128, 2], F32)
    bcs = sb("bcs", [128, 2], F32)
    masks = sb("masks", [128, 2, 2, 32], BF16)
    biasB = sb("biasB", [128, 32], F32)
    Mq = sb("Mq", [128, 32], F32)
    Mk = sb("Mk", [128, 4], F32)
    MqR = sb("MqR", [128, 32], F32)
    MkR = sb("MkR", [128, 4], F32)
    sqMq = sb("sqMq", [128, 32], F32)
    kmb = sb("kmb", [128, 4], F32)
    strd = sb("strd", [1, 64], F32)
    kmaxg = sb("kmaxg", [1, 8], F32)
    mrow = sb("mrow", [1, 40], F32)
    idt = sb("idt_s", [128, 128], F32)
    lnscr = sb("lnscr", [1, 1024], F32)
    tmpa = sb("tmpa", [128, 512], F32)
    tmpb = sb("tmpb", [128, 512], F32)
    cvals = sb("cvals", [128, 2], F32)
    zrow = sb("zrow", [1, 512], BF16)
    on_row16 = sb("on_row16", [1, 128], BF16)
    qhat = sb("qhat", [128, B, 8, 4, 128], BF16)

    bctx = ExitStack()
    bsb = lambda name, shape, dt: bctx.enter_context(nc.sbuf_tensor(name, shape, dt))
    w_q = bsb("w_q", [128, 8, 1024], BF16)
    w_dkv = bsb("w_dkv", [128, 8, 256], BF16)
    w_uk = bsb("w_uk", [128, 2, 1024], BF16)
    w_uv = bsb("w_uv", [128, 2, 1024], BF16)
    xkv_s = bsb("xkv_s", [128, 2, 8, 512], BF16)
    xq_s = bsb("xq_s", [128, 2, 8, 512], BF16)
    ltf = bsb("ltf", [128, 2, 512], F32)
    ltsq = bsb("ltsq", [128, 2, 512], F32)
    latn = bsb("latn", [128, 2, 512], BF16)
    ropbk = bsb("ropbk", [128, 2, 1024], F32)
    ropbq = bsb("ropbq", [128, 2, 1024], F32)
    krop = bsb("krop", [128, 2, 1024], BF16)
    qrop = bsb("qrop", [128, 2, 1024], BF16)
    kts = bsb("kts", [128, 2, 1024], BF16)
    vst = bsb("vst", [128, 2, 1024], BF16)
    sqk = bsb("sqk", [128, 1024], F32)
    sqq = bsb("sqq", [128, 1024], F32)
    n16 = bsb("n16", [128, 16], F32)
    n8 = bsb("n8", [128, 8], F32)
    ncol = bsb("ncol", [128, 4], F32)

    PS = ps("PS", [128, 3072])
    PO = ps("PO", [128, 512])
    PD = ps("PD", [128, 512])

    SEMNAMES = [
        "dW",
        "dWa",
        "dX0", "dX1",
        "dC",
        "dT0", "dT1",
        "dK0", "dK1",
        "dQ0", "dQ1",
        "dH0", "dH1",
        "dSS",
        "dV0", "dV1",
        "dR0", "dR1",
        "dN0", "dN1",
        "dM",
        "dO0", "dO1",
        "dF0", "dF1",
        "P", "A", "V", "G", "CC",
    ]
    SEMS = {}
    for nm in SEMNAMES:
        SEMS[nm] = ctx.enter_context(nc.semaphore("s_" + nm))
    C = {nm: 0 for nm in SEMNAMES}
    lastw = {}

    def wt(eng, ename, key, val):
        if val <= 0:
            return
        if lastw.get((ename, key), -1) >= val:
            return
        eng.wait_ge(SEMS[key], val)
        lastw[(ename, key)] = val

    def dma(eng, key, out, in_, transpose=False):
        if transpose:
            eng.dma_start(out=out, in_=in_, transpose=True).then_inc(SEMS[key], 16)
        else:
            eng.dma_start(out=out, in_=in_).then_inc(SEMS[key], 16)
        C[key] += 16
        return C[key]

    block = ctx.enter_context(nc.Block())

    SCALAR_OPS = []

    def defer_scalar(fn):
        SCALAR_OPS.append(fn)
        return None

    @block.gpsimd
    def _(g):
        g.memset(on_mat[:, :], 1.0)
        g.memset(on_row[:, :], 1.0)
        g.memset(on_col[:, :], 1.0)
        g.memset(Mk[:, :], 0.0)
        g.memset(Mq[:, :], 0.0)
        g.memset(cvals[:, 0:1], EPS)
        g.memset(cvals[:, 1:2], 1e-30)
        g.memset(zrow[:, :], 0.0)
        g.memset(on_row16[:, :], 1.0).then_inc(SEMS["G"], 1)
        C["G"] += 1
        for dst, src in (
            (w_dkv[:, :, :], d_wdkv), (w_q[:, :, :], d_wq),
            (w_uk[:, :, :], d_wuk), (w_uv[:, :, :], d_wuv),
        ):
            dma(g, "dWa", dst, src.rearrange("(ic p) d -> p ic d", p=128))
        dma(g, "dW", w_o[:, :, :], d_wo.rearrange("(ic p) d -> p ic d", p=128))
        dma(g, "dW", masks[:, :, :, :], d_maskb.rearrange("p (par two c) -> p par two c", par=2, c=32))
    WA_DONE = C["dWa"]

    @block.sync
    def _(s):
        dma(s, "dC", idt[:, :], d_idt)
        dma(s, "dC", bffs[:, :], d_bff)
        dma(s, "dC", incs[:, :], d_incol)
        dma(s, "dC", gcs[:, :], d_gcol)
        dma(s, "dC", bcs[:, :], d_bcol)
    CONST_DONE = C["dC"]

    @block.gpsimd
    def _(g):
        dma(g, "dX0", xkv_s[:, 0, :, :], d_xkv[0].rearrange("(ic p) t -> p ic t", p=128))
        dma(g, "dX0", xq_s[:, 0, :, :], d_xq[0].rearrange("(ic p) t -> p ic t", p=128))
    XLOAD = {0: ("dX0", C["dX0"])}

    slot_units = {}
    lat_units = {}
    lat_norm = 0
    ph2_V = {}
    SQMQ_B = {}
    CC_STATS = {}
    CC_KV = {}
    BIAS_B = {}

    for b in range(B):
        bb = b % 2

        if b + 1 < B:
            @block.gpsimd
            def _(g, b=b):
                nb = (b + 1) % 2
                key = "dX" + str(nb)
                if b + 1 >= 2:
                    wt(g, "g", "P", slot_units[(b - 1, 3)]["q_mm"])
                dma(g, key, xkv_s[:, nb, :, :],
                    d_xkv[b + 1].rearrange("(ic p) t -> p ic t", p=128))
                dma(g, key, xq_s[:, nb, :, :],
                    d_xq[b + 1].rearrange("(ic p) t -> p ic t", p=128))
                XLOAD[b + 1] = (key, C[key])

        @block.tensor
        def _(t, b=b, bb=bb):
            wt(t, "t", "dWa", WA_DONE)
            wt(t, "t", "G", 1)
            wt(t, "t", XLOAD[b][0], XLOAD[b][1])
            if b > 0:
                wt(t, "t", "V", slot_units[(b - 1, 3)]["ropeq_done"])
                wt(t, "t", "V", ph2_V[b - 1])
            for lh in range(2):
                for ic in range(8):
                    t.matmul(PS[:, lh * 512:(lh + 1) * 512],
                             w_dkv[:, ic, lh * 128:(lh + 1) * 128],
                             xkv_s[:, bb, ic, :],
                             start=(ic == 0), stop=(ic == 7)).then_maybe_inc(
                        (SEMS["P"], 1) if (lh == 1 and ic == 7) else None)
            pass
            C["P"] += 1
        lat_mm = C["P"]

        @block.vector
        def _(v, b=b, lat_mm=lat_mm):
            wt(v, "v", "P", lat_mm)
            v.tensor_copy(ltf[:, 0, :], PS[:, 0:512])
            v.tensor_copy(ltf[:, 1, :], PS[:, 512:1024]).then_inc(SEMS["V"], 1)
            C["V"] += 1
        lat_evac = C["V"]

        @block.vector
        def _(v, lat_evac=lat_evac):
            v.tensor_mul(ltsq[:, 0, :], ltf[:, 0, :], ltf[:, 0, :])
            v.tensor_mul(ltsq[:, 1, :], ltf[:, 1, :],
                         ltf[:, 1, :]).then_inc(SEMS["V"], 1)
            C["V"] += 1
        lat_sq = C["V"]

        @block.tensor
        def _(t, b=b, lat_sq=lat_sq):
            wt(t, "t", "V", lat_sq)
            if b > 0:
                wt(t, "t", "V", lat_units[b - 1]["norm_done"])
            for lh in range(2):
                t.matmul(PD[0:1, 0:512], on_col[:, 0:1], ltf[:, lh, :],
                         start=(lh == 0), stop=(lh == 1))
            for lh in range(2):
                t.matmul(PO[0:1, 0:512], on_col[:, 0:1], ltsq[:, lh, :],
                         start=(lh == 0), stop=(lh == 1)).then_maybe_inc(
                    (SEMS["P"], 1) if lh == 1 else None)
            pass
            C["P"] += 1
        lat_stats = C["P"]

        mu_s = lnscr[0:1, 0:512]
        var_s = lnscr[0:1, 512:1024]
        m2_s = tmpa[0:1, 0:512]
        sd_s = tmpb[0:1, 0:512]

        @block.vector
        def _(v, lat_stats=lat_stats):
            wt(v, "v", "P", lat_stats)
            v.tensor_scalar(mu_s, PD[0:1, 0:512], 1.0 / LT, None, ALU.mult)
            v.tensor_scalar(var_s, PO[0:1, 0:512], 1.0 / LT, None, ALU.mult)
            v.tensor_mul(m2_s, mu_s, mu_s)
            v.tensor_sub(var_s, var_s, m2_s).then_inc(SEMS["V"], 1)
            C["V"] += 1
        lat_v1 = C["V"]

        @defer_scalar
        def _(a, lat_v1=lat_v1):
            wt(a, "a", "V", lat_v1)
            a.activation(sd_s, var_s, AF.Ln, bias=cvals[0:1, 0:1])
            a.activation(var_s, sd_s, AF.Exp, scale=-0.5).then_inc(SEMS["A"], 1)
        C["A"] += 1
        lat_rstd = C["A"]

        @block.tensor
        def _(t, lat_rstd=lat_rstd):
            wt(t, "t", "A", lat_rstd)
            t.matmul(PS[:, 2048:2560], on_row[0:1, 0:128], mu_s, start=True, stop=True)
            t.matmul(PS[:, 2560:3072], on_row[0:1, 0:128], var_s, start=True,
                     stop=True).then_inc(SEMS["P"], 1)
            C["P"] += 1
        lat_bc = C["P"]

        @block.vector
        def _(v, lat_bc=lat_bc):
            wt(v, "v", "P", lat_bc)
            wt(v, "v", "dC", CONST_DONE)
            for lh in range(2):
                v.tensor_sub(tmpa[:, :], ltf[:, lh, :], PS[:, 2048:2560])
                v.tensor_mul(tmpb[:, :], tmpa[:, :], PS[:, 2560:3072])
                v.tensor_scalar(latn[:, lh, :], tmpb[:, :],
                                gcs[:, lh:lh + 1], bcs[:, lh:lh + 1],
                                ALU.mult, ALU.add).then_maybe_inc(
                    (SEMS["V"], 1) if lh == 1 else None)
            pass
            C["V"] += 1
        lat_units[b] = {"norm_done": C["V"]}
        lat_norm = C["V"]

        for s_ in range(4):
            par = s_ % 2
            tk = "dT" + str(par)
            prev = slot_units.get((b, s_ - 1)) if s_ > 0 else \
                (slot_units.get((b - 1, 3)) if b > 0 else None)
            prev2 = slot_units.get((b, s_ - 2)) if s_ > 1 else \
                (slot_units.get((b - 1, 2 + s_)) if b > 0 else None)

            @block.sync
            def _(s, s_=s_, par=par, tk=tk, prev2=prev2):
                if prev2 is not None:
                    wt(s, "s", "V", prev2["ropeq_done"])
                dma(s, tk, ropbk[:, par, :], d_ropek[s_])
                dma(s, tk, ropbq[:, par, :], d_ropeq[s_])
            tbl_done = C[tk]

            @block.tensor
            def _(t, s_=s_, prev=prev, lat_norm=lat_norm):
                wt(t, "t", "V", lat_norm)
                if prev is not None:
                    wt(t, "t", "V", prev["vevac_done"])
                for dh in range(2):
                    for lh in range(2):
                        t.matmul(PS[:, dh * 512:(dh + 1) * 512],
                                 latn[:, lh, s_ * 128:s_ * 128 + 128],
                                 w_uk[:, lh, dh * 512:(dh + 1) * 512],
                                 start=(lh == 0), stop=(lh == 1))
                for dh in range(2):
                    for lh in range(2):
                        t.matmul(PS[:, 1024 + dh * 512:1024 + (dh + 1) * 512],
                                 latn[:, lh, s_ * 128:s_ * 128 + 128],
                                 w_uv[:, lh, dh * 512:(dh + 1) * 512],
                                 start=(lh == 0), stop=(lh == 1)).then_maybe_inc(
                            (SEMS["P"], 1) if (dh == 1 and lh == 1) else None)
                pass
                C["P"] += 1
            kv_mm = C["P"]

            @block.tensor
            def _(t, bb=bb, s_=s_, prev=prev):
                if prev is not None:
                    wt(t, "t", "V", prev["ropeq_done"])
                for dh in range(2):
                    for ic in range(8):
                        t.matmul(PS[:, 2048 + dh * 512:2048 + (dh + 1) * 512],
                                 xq_s[:, bb, ic, s_ * 128:s_ * 128 + 128],
                                 w_q[:, ic, dh * 512:(dh + 1) * 512],
                                 start=(ic == 0), stop=(ic == 7)).then_maybe_inc(
                            (SEMS["P"], 1) if (dh == 1 and ic == 7) else None)
                pass
                C["P"] += 1
            q_mm = C["P"]

            def rope_emit(v, psoff, tblbuf, tbl, dst, dstbuf):
                ap = PS[:, psoff:psoff + 1024].rearrange(
                    "p (h two q) -> p h two q", h=16, two=2)
                x1 = ap[:, :, 0, :]
                x2 = ap[:, :, 1, :]
                cos = tbl[:, tblbuf, 0:512].rearrange("p (h q) -> p h q", h=16)
                sin = tbl[:, tblbuf, 512:1024].rearrange("p (h q) -> p h q", h=16)
                ta = tmpa[:, :].rearrange("p (h q) -> p h q", h=16)
                tb = tmpb[:, :].rearrange("p (h q) -> p h q", h=16)
                dap = dst[:, dstbuf, :].rearrange("p (h two q) -> p h two q", h=16, two=2)
                v.tensor_mul(ta, x1, cos)
                v.tensor_mul(tb, x2, sin)
                v.tensor_sub(dap[:, :, 0, :], ta, tb)
                v.tensor_mul(ta, x1, sin)
                v.tensor_mul(tb, x2, cos)
                return v.tensor_add(dap[:, :, 1, :], ta, tb)

            @block.vector
            def _(v, par=par, tk=tk, kv_mm=kv_mm, tbl_done=tbl_done, prev2=prev2):
                wt(v, "v", "P", kv_mm)
                wt(v, "v", tk, tbl_done)
                if prev2 is not None:
                    wt(v, "v", "dK" + str(par), prev2["ktr_done"])
                rope_emit(v, 0, par, ropbk, krop, par).then_inc(SEMS["V"], 1)
                C["V"] += 1
            ropek_done = C["V"]

            @block.vector
            def _(v, par=par, prev2=prev2):
                if prev2 is not None:
                    wt(v, "v", "dH" + str(par), prev2["shard_done"])
                v.tensor_copy(vst[:, par, :], PS[:, 1024:2048]).then_inc(SEMS["V"], 1)
                C["V"] += 1
            vevac_done = C["V"]

            @block.vector
            def _(v, par=par, tk=tk, q_mm=q_mm, tbl_done=tbl_done, prev2=prev2):
                wt(v, "v", "P", q_mm)
                wt(v, "v", tk, tbl_done)
                if prev2 is not None:
                    wt(v, "v", "dQ" + str(par), prev2["qtr_done"])
                rope_emit(v, 2048, par, ropbq, qrop, par).then_inc(SEMS["V"], 1)
                C["V"] += 1
            ropeq_done = C["V"]

            @block.vector
            def _(v, par=par):
                v.tensor_mul(sqk[:, :], krop[:, par, :], krop[:, par, :]).then_inc(SEMS["V"], 1)
                C["V"] += 1
                v.tensor_mul(sqq[:, :], qrop[:, par, :], qrop[:, par, :]).then_inc(SEMS["V"], 1)
                C["V"] += 1
            sq_done = C["V"]

            @block.vector
            def _(v, b=b, sq_done=sq_done):
                v.reduce_sum(n16[:, :], sqk[:, :].rearrange("p (h q) -> p h q", h=16),
                             axis=mybir.AxisListType.X)
                v.reduce_max(ncol[:, 0:1], n16[:, :], axis=mybir.AxisListType.X)
                v.tensor_max(Mk[:, b:b + 1], Mk[:, b:b + 1], ncol[:, 0:1])
                v.reduce_sum(n16[:, :], sqq[:, :].rearrange("p (h q) -> p h q", h=16),
                             axis=mybir.AxisListType.X)
                v.reduce_max(n8[:, :], n16[:, :].rearrange("p (pr two) -> p pr two", two=2),
                             axis=mybir.AxisListType.X)
                v.tensor_max(Mq[:, b * 8:(b + 1) * 8], Mq[:, b * 8:(b + 1) * 8],
                             n8[:, :]).then_inc(SEMS["V"], 1)
                C["V"] += 1
            norms_done = C["V"]

            @block.sync
            def _(s, par=par, ropek_done=ropek_done, prev2=prev2):
                wt(s, "s", "V", ropek_done)
                if prev2 is not None:
                    wt(s, "s", "dH" + str(par), prev2["shard_done"])
                for pr in range(8):
                    dma(s, "dK" + str(par), kts[:, par, pr * 128:(pr + 1) * 128],
                        krop[:, par, pr * 128:(pr + 1) * 128], transpose=True)
            ktr_done = C["dK" + str(par)]

            @block.sync
            def _(s, b=b, s_=s_, par=par, ropeq_done=ropeq_done):
                wt(s, "s", "V", ropeq_done)
                for pr in range(8):
                    dma(s, "dQ" + str(par), qhat[:, b, pr, s_, :],
                        qrop[:, par, pr * 128:(pr + 1) * 128], transpose=True)
            qtr_done = C["dQ" + str(par)]

            @block.gpsimd
            def _(g, b=b, s_=s_, par=par, ktr_done=ktr_done, vevac_done=vevac_done):
                wt(g, "g", "dK" + str(par), ktr_done)
                dma(g, "dH" + str(par), d_kvsh[b, :, 0, :, s_, :].transpose([1, 0, 2]),
                    kts[:, par, :].rearrange("p (pr c) -> p pr c", c=128))
                wt(g, "g", "V", vevac_done)
                dma(g, "dH" + str(par), d_kvsh[b, :, 1, :, s_, :].transpose([1, 0, 2]),
                    vst[:, par, :].rearrange("p (pr c) -> p pr c", c=128))
            shard_done = C["dH" + str(par)]

            slot_units[(b, s_)] = dict(
                ropek_done=ropek_done, ropeq_done=ropeq_done, vevac_done=vevac_done,
                norms_done=norms_done, ktr_done=ktr_done, qtr_done=qtr_done,
                shard_done=shard_done, q_mm=q_mm,
            )

        nd = slot_units[(b, 3)]["norms_done"]

        @block.tensor
        def _(t, b=b, nd=nd):
            wt(t, "t", "V", nd)
            wt(t, "t", "dC", CONST_DONE)
            t.transpose(PS[0:8, 0:128], Mq[:, 8 * b:8 * b + 8], idt[:, :])
            t.transpose(PS[0:1, 1536:1664], Mk[:, b:b + 1],
                        idt[:, :]).then_inc(SEMS["P"], 1)
            C["P"] += 1
        tr1 = C["P"]

        @block.vector
        def _(v, b=b, tr1=tr1):
            wt(v, "v", "P", tr1)
            v.reduce_max(tmpa[0:8, 0:1], PS[0:8, 0:128], axis=mybir.AxisListType.X)
            v.reduce_max(tmpb[0:1, 0:1], PS[0:1, 1536:1664],
                         axis=mybir.AxisListType.X)
            v.tensor_copy(mrow[0:1, 32 + b:33 + b],
                          tmpb[0:1, 0:1]).then_inc(SEMS["V"], 1)
            C["V"] += 1
        rmx = C["V"]

        @block.tensor
        def _(t, rmx=rmx):
            wt(t, "t", "V", rmx)
            t.transpose(PS[0:1, 512:520], tmpa[0:8, 0:1],
                        idt[0:8, 0:8]).then_inc(SEMS["P"], 1)
            C["P"] += 1
        tr2 = C["P"]

        @block.vector
        def _(v, tr2=tr2):
            wt(v, "v", "P", tr2)
            v.tensor_copy(mrow[0:1, 0:8], PS[0:1, 512:520]).then_inc(SEMS["V"], 1)
            C["V"] += 1
        cpy = C["V"]

        @block.tensor
        def _(t, cpy=cpy):
            wt(t, "t", "V", cpy)
            t.matmul(PS[:, 1024:1032], on_row[0:1, 0:128], mrow[0:1, 0:8],
                     start=True, stop=True).then_inc(SEMS["P"], 1)
            C["P"] += 1
        bcm = C["P"]

        @block.vector
        def _(v, b=b, bcm=bcm):
            wt(v, "v", "P", bcm)
            v.tensor_copy(MqR[:, 8 * b:8 * b + 8],
                          PS[:, 1024:1032]).then_inc(SEMS["V"], 1)
            C["V"] += 1
        ph2_V[b] = C["V"]

        @defer_scalar
        def _(a, b=b, pv=ph2_V[b]):
            wt(a, "a", "V", pv)
            a.activation(sqMq[:, 8 * b:8 * b + 8], MqR[:, 8 * b:8 * b + 8],
                         AF.Ln, bias=cvals[:, 1:2])
            a.activation(sqMq[:, 8 * b:8 * b + 8], sqMq[:, 8 * b:8 * b + 8],
                         AF.Exp, scale=0.5).then_inc(SEMS["A"], 1)
        C["A"] += 1
        SQMQ_B[b] = C["A"]

        @block.gpsimd
        def _(g, b=b, rmx=rmx, kvdh=(C["dH0"], C["dH1"])):
            wt(g, "g", "V", rmx)
            dma(g, "dSS", d_stsh[b:b + 1, 1:128], on_row[0:1, 1:128])
            dma(g, "dSS", d_stsh[b:b + 1, 0:1], mrow[0:1, 32 + b:33 + b])
            wt(g, "g", "dSS", C["dSS"])
            g.collective_compute(
                "AllGather", ALU.bypass, replica_groups=[list(range(NC))],
                ins=[d_stsh[b:b + 1, :]], outs=[d_stfull[b]],
            ).then_inc(SEMS["CC"], 1)
            C["CC"] += 1
            if b == 0:
                wt(g, "g", "dH0", kvdh[0])
                wt(g, "g", "dH1", kvdh[1])
                g.collective_compute(
                    "AllGather", ALU.bypass, replica_groups=[list(range(NC))],
                    ins=[d_kvsh[0]], outs=[d_kvfull[0]],
                ).then_inc(SEMS["CC"], 1)
                C["CC"] += 1
        CC_STATS[b] = C["CC"] - (1 if b == 0 else 0)
        if b == 0:
            CC_KV[b] = C["CC"]

            @block.gpsimd
            def _(g):
                for fm in range(8):
                    for foc in range(8):
                        dma(g, "dW", d_fcache[fm, foc],
                            d_fwT[fm].rearrange("(ic p) (oc c) -> oc p ic c",
                                                p=128, c=128)[foc])

    @block.gpsimd
    def _(g):
        wt(g, "g", "dH0", C["dH0"])
        wt(g, "g", "dH1", C["dH1"])
        for bq in range(1, B):
            g.collective_compute(
                "AllGather", ALU.bypass, replica_groups=[list(range(NC))],
                ins=[d_kvsh[bq]], outs=[d_kvfull[bq]],
            ).then_inc(SEMS["CC"], 1)
            C["CC"] += 1
            CC_KV[bq] = C["CC"]

    W_DONE = C["dW"]
    LAST_NORMS = slot_units[(B - 1, 3)]["norms_done"]

    kmb_done = 0
    for b in range(B):
        @block.sync
        def _(s, b=b, cs=CC_STATS[b]):
            wt(s, "s", "CC", cs)
            s.dma_start(out=strd[0:1, b * 8:b * 8 + 8],
                        in_=d_stfull[b, :, 0:1].transpose([1, 0]),
                        allow_slow_non_contiguous=True).then_inc(SEMS["dC"], 16)
            C["dC"] += 16

        @block.vector
        def _(v, b=b, sl=C["dC"]):
            wt(v, "v", "dC", sl)
            v.reduce_max(kmaxg[0:1, b:b + 1], strd[0:1, b * 8:b * 8 + 8],
                         axis=mybir.AxisListType.X).then_inc(SEMS["V"], 1)
            C["V"] += 1
        kmg = C["V"]

        @defer_scalar
        def _(a, b=b, kmg=kmg):
            wt(a, "a", "V", kmg)
            a.activation(kmaxg[0:1, 4 + b:5 + b], kmaxg[0:1, b:b + 1],
                         AF.Ln, bias=cvals[0:1, 1:2])
            a.activation(kmaxg[0:1, b:b + 1], kmaxg[0:1, 4 + b:5 + b],
                         AF.Exp, scale=0.5).then_inc(SEMS["A"], 1)
        C["A"] += 1
        ksq = C["A"]

        @block.tensor
        def _(t, b=b, ksq=ksq, pk=kmb_done):
            wt(t, "t", "A", ksq)
            if b == 0:
                wt(t, "t", "V", ph2_V[B - 1])
            if pk:
                wt(t, "t", "V", pk)
            t.matmul(PS[:, 1024 + b:1025 + b], on_row[0:1, 0:128],
                     kmaxg[0:1, b:b + 1], start=True,
                     stop=True).then_inc(SEMS["P"], 1)
            C["P"] += 1
        kmm = C["P"]

        @block.vector
        def _(v, b=b, kmm=kmm, sq=SQMQ_B[b]):
            wt(v, "v", "P", kmm)
            v.tensor_copy(kmb[:, b:b + 1], PS[:, 1024 + b:1025 + b])
            wt(v, "v", "A", sq)
            v.tensor_scalar(biasB[:, b * 8:(b + 1) * 8], sqMq[:, b * 8:(b + 1) * 8],
                            kmb[:, b:b + 1], -1.0, ALU.mult,
                            ALU.mult).then_inc(SEMS["V"], 1)
            C["V"] += 1
        kmb_done = C["V"]
        BIAS_B[b] = C["V"]
    BIAS_DONE = BIAS_B[B - 1]

    BAR = dict(C)

    def barrier_waits(eng, ename):
        for k in SEMNAMES:
            if k != "CC":
                wt(eng, ename, k, BAR[k])

    @block.tensor
    def _(t):
        barrier_waits(t, "t")

    @block.vector
    def _(v):
        barrier_waits(v, "v")

    @defer_scalar
    def _(a):
        barrier_waits(a, "a")

    @block.sync
    def _(s):
        barrier_waits(s, "s")

    @block.gpsimd
    def _(g):
        barrier_waits(g, "g")

    bctx.close()

    kvK = sb("kvK", [128, 2, 32, 128], BF16)
    kvV = sb("kvV", [128, 2, 32, 128], BF16)
    pt = sb("pt", [128, 3, 1024], BF16)
    attnT = sb("attnT", [128, 8, 512], BF16)
    rcp = sb("rcp", [128, 512], F32)
    xrc = sb("xrc", [128, 2, 512], F32)
    rbuf = sb("rbuf", [128, 8, 512], F32)
    rsq = sb("rsq", [128, 512], F32)
    nrm = sb("nrm", [128, 2, 512], F32)
    xmt = sb("xmt", [128, 8, 512], BF16)
    xm32c = sb("xm32c", [128, 2, 512], F32)
    cT32 = sb("cT32", [128, 8, 512], F32)
    cT16 = sb("cT16", [128, 2, 8, 512], BF16)
    fwc = sb("fwc", [128, 2, 2, 8, 128], BF16)
    gf = sb("gf", [128, 512], BF16)
    gi = sb("gi", [128, 512], BF16)
    gg = sb("gg", [128, 512], BF16)
    hc = sb("hc", [128, 2, 512], F32)

    @defer_scalar
    def _(a):
        wt(a, "a", "dC", CONST_DONE)
        for j in range(8):
            a.activation(cT32[:, j, :], cT32[:, j, :], AF.Identity,
                         bias=incs[:, j:j + 1], scale=0.0).then_maybe_inc(
                (SEMS["A"], 1) if j == 7 else None)
        pass
    C["A"] += 1
    CINIT_A = C["A"]

    @block.vector
    def _(v):
        wt(v, "v", "A", CINIT_A)
        for j in range(8):
            v.tensor_copy(cT16[:, 0, j, :], cT32[:, j, :]).then_maybe_inc(
                (SEMS["V"], 1) if j == 7 else None)
        pass
        C["V"] += 1
    cready = C["V"]

    FCHUNKS = []
    for _t in range(B):
        for _mo in range(8):
            for _g in range(3):
                FCHUNKS.append((2 * _g, 2 * _g + 1, _mo))
    for _t in range(B):
        for _oc in range(8):
            FCHUNKS.append((6, 7, _oc))
    chunk_mm = {}
    chunk_ld = {}
    chunk_act = {}
    cupd_last = {-1: cready}
    ffn_state = {"last_cupd": cready}

    def emit_fwc_load(ci):
        if ci >= len(FCHUNKS):
            return
        wm, um, oc_ = FCHUNKS[ci]
        buf = ci % 2
        key = "dF" + str(buf)

        @block.gpsimd
        def _(g, ci=ci, wm=wm, um=um, oc_=oc_, buf=buf, key=key):
            wt(g, "g", "dW", W_DONE)
            if ci - 2 in chunk_mm:
                wt(g, "g", "P", chunk_mm[ci - 2])
            dma(g, key, fwc[:, buf, 0, :, :], d_fcache[wm, oc_])
            dma(g, key, fwc[:, buf, 1, :, :], d_fcache[um, oc_])
        chunk_ld[ci] = (key, C[key])

    emit_fwc_load(0)
    emit_fwc_load(1)

    def emit_xmt_load(step):
        @block.sync
        def _(s, step=step):
            wt(s, "s", "dN0", C["dN0"])
            wt(s, "s", "dN1", C["dN1"])
            if step > 0:
                wt(s, "s", "P", chunk_mm[step * 24 - 1])
            dma(s, "dM", xmt[:, :, :], d_xmid16[step].transpose([1, 0, 2]))
        return C["dM"]

    def emit_gate_part(ci, step, mo, gg_i, part, xmt_ld, extraV=0):
        buf = ci % 2
        bank = 4 + (ci % 2)
        func = (AF.Sigmoid, AF.Sigmoid, AF.Tanh)[gg_i]

        @block.tensor
        def _(t, ci=ci, buf=buf, bank=bank, xmt_ld=xmt_ld, step=step,
              part=part, extraV=extraV, cw=cupd_last[step - 1]):
            if part == 0:
                wt(t, "t", chunk_ld[ci][0], chunk_ld[ci][1])
                wt(t, "t", "dM", xmt_ld)
                if extraV:
                    wt(t, "t", "V", extraV)
                if ci - 2 in chunk_act:
                    wt(t, "t", "A", chunk_act[ci - 2])
            elif part == 2:
                wt(t, "t", "V", cw)
            w_u = part // 2
            src = xmt if w_u == 0 else None
            for j in range(4):
                ic = (part % 2) * 4 + j
                rhs = xmt[:, ic, :] if w_u == 0 else cT16[:, step % 2, ic, :]
                t.matmul(PS[:, bank * 512:(bank + 1) * 512],
                         fwc[:, buf, w_u, ic, :], rhs,
                         start=(part == 0 and j == 0),
                         stop=(part == 3 and j == 3)).then_maybe_inc(
                    (SEMS["P"], 1) if (part == 3 and j == 3) else None)
            pass
            if part == 3:
                C["P"] += 1
        if part == 3:
            chunk_mm[ci] = C["P"]
            emit_fwc_load(ci + 2)

            @defer_scalar
            def _(a, bank=bank, gg_i=gg_i, mo=mo, func=func, gmm=chunk_mm[ci],
                  pcu=(ffn_state["last_cupd"] if gg_i == 0 else 0)):
                wt(a, "a", "P", gmm)
                if pcu:
                    wt(a, "a", "V", pcu)
                dstb = (gf, gi, gg)[gg_i]
                a.activation(dstb[:, :], PS[:, bank * 512:(bank + 1) * 512],
                             func, bias=bffs[:, gg_i * 8 + mo:gg_i * 8 + mo + 1]
                             ).then_inc(SEMS["A"], 1)
            C["A"] += 1
            chunk_act[ci] = C["A"]

    FFN_TILES = [3, 5, 8, 10, 13, 15, 18, 20, 23, 25, 28, 30]

    def emit_gate_unit(ci, step, mo, gg_i, xmt_ld, extraV=0):
        for part in range(4):
            emit_gate_part(ci, step, mo, gg_i, part, xmt_ld,
                           extraV=(extraV if part == 0 else 0))

    def emit_cupdate(step, mo):
        nxt = (step + 1) % 2

        @block.vector
        def _(v, mo=mo, nxt=nxt, gact=chunk_act[step * 24 + mo * 3 + 2]):
            wt(v, "v", "A", gact)
            v.tensor_mul(tmpa[:, :], gf[:, :], cT32[:, mo, :])
            v.tensor_mul(tmpb[:, :], gi[:, :], gg[:, :])
            v.tensor_add(cT32[:, mo, :], tmpa[:, :], tmpb[:, :])
            v.tensor_copy(cT16[:, nxt, mo, :],
                          cT32[:, mo, :]).then_inc(SEMS["V"], 1)
            C["V"] += 1
        ffn_state["last_cupd"] = C["V"]

    xmt_lds = {}

    def kv_prefetch(s, b, pair, buf):
        key = "dV" + str(buf)
        for c in range(NC):
            dma(s, key, kvK[:, buf, 4 * c:4 * (c + 1), :], d_kvfull[b, c, pair, 0])
            dma(s, key, kvV[:, buf, 4 * c:4 * (c + 1), :], d_kvfull[b, c, pair, 1])
        return (key, C[key])

    kvready = {}

    @block.sync
    def _(s):
        wt(s, "s", "CC", CC_KV[0])
        kvready[(0, 0)] = kv_prefetch(s, 0, 0, 0)

    prev_epilogue_V = 0
    prev_pvden_P = 0
    wo_done_P = 0
    attn_done_V = {}
    wo_radds = {}
    wo_nrm_V = {}
    exp_slot_wait = {0: 0, 1: 0}

    for b in range(B):
        for pair in range(8):
            bp = b * 8 + pair
            buf = bp % 2

            @block.sync
            def _(s, bp=bp):
                if bp + 1 < 32:
                    nb, npair = divmod(bp + 1, 8)
                    if bp >= 1:
                        wt(s, "s", "P", prev_pvden_P)
                    wt(s, "s", "CC", CC_KV[nb])
                    kvready[(nb, npair)] = kv_prefetch(s, nb, npair, (bp + 1) % 2)

            if b >= 1 and pair == 0:
                xmt_lds[b - 1] = emit_xmt_load(b - 1)

            P_sT = {}
            A_exp = {}
            V_mask = {}

            def emit_dummies(epiV=prev_epilogue_V):
                @block.tensor
                def _(t, epiV=epiV):
                    wt(t, "t", "V", epiV)
                    t.matmul(PO[:, :], on_row16[0:1, 0:128], zrow[0:1, :],
                             start=True, stop=True)
                    t.matmul(PD[:, :], on_row16[0:1, 0:128], zrow[0:1, :],
                             start=True, stop=True)

            def emit_pv(tt, buf=buf):
                slot = tt % 3
                n0 = 32 * (tt // 2)
                N = 512 - n0

                @block.tensor
                def _(t, buf=buf, tt=tt, slot=slot, n0=n0, N=N, vm=V_mask[tt]):
                    wt(t, "t", "V", vm)
                    t.matmul(PO[0:64, n0:512], kvV[:, buf, tt, 0:64],
                             pt[:, slot, 0:N], start=False, stop=False,
                             tile_position=(0, 0), skip_group_check=True)
                    t.matmul(PO[64:128, n0:512], kvV[:, buf, tt, 64:128],
                             pt[:, slot, 512:512 + N], start=False, stop=False,
                             tile_position=(0, 64), skip_group_check=True)
                    t.matmul(PD[0:64, n0:512], on_mat[:, :],
                             pt[:, slot, 0:N], start=False, stop=False,
                             tile_position=(0, 0), skip_group_check=True)
                    t.matmul(PD[64:128, n0:512], on_mat[:, :],
                             pt[:, slot, 512:512 + N], start=False, stop=False,
                             tile_position=(0, 64), skip_group_check=True
                             ).then_inc(SEMS["P"], 1)
                C["P"] += 1

            for tt in range(32):
                slot = tt % 2
                n0 = 32 * (tt // 2)
                N = 512 - n0
                if tt == 0:
                    emit_dummies()
                if tt >= 3:
                    emit_pv(tt - 3)
                if b >= 1 and tt in FFN_TILES:
                    idx = FFN_TILES.index(tt)
                    g = idx // 4
                    part = idx % 4
                    ci = (b - 1) * 24 + pair * 3 + g
                    emit_gate_part(ci, b - 1, pair, g, part, xmt_lds[b - 1],
                                   extraV=(wo_nrm_V[b - 1]
                                           if (pair == 0 and g == 0 and part == 0)
                                           else 0))

                @block.tensor
                def _(t, b=b, pair=pair, buf=buf, tt=tt, slot=slot, n0=n0, N=N,
                      ew=exp_slot_wait[slot],
                      wonrm=(wo_radds[b - 1][7] if (pair == 0 and b > 0) else None)):
                    if tt == 0:
                        wt(t, "t", kvready[(b, pair)][0], kvready[(b, pair)][1])
                        wt(t, "t", "V", BIAS_B[b])
                        if wonrm is not None:
                            wt(t, "t", "V", wonrm)
                    if ew:
                        wt(t, "t", "A", ew)
                    qfe = qhat[0:64, b, pair, :, :].rearrange(
                        "p a c -> p (a c)")[:, n0:512]
                    qfo = qhat[64:128, b, pair, :, :].rearrange(
                        "p a c -> p (a c)")[:, n0:512]
                    t.matmul(PS[:, slot * 1024:slot * 1024 + N],
                             kvK[0:64, buf, tt, :], qfe,
                             start=True, stop=True)
                    t.matmul(PS[:, slot * 1024 + 512:slot * 1024 + 512 + N],
                             kvK[64:128, buf, tt, :], qfo,
                             start=True, stop=True,
                             tile_position=(64, 0)).then_inc(SEMS["P"], 1)
                C["P"] += 1
                P_sT[tt] = C["P"]

                @defer_scalar
                def _(a, b=b, pair=pair, slot=slot, pslot=tt % 3, N=N, stp=P_sT[tt]):
                    wt(a, "a", "P", stp)
                    inap = PS[:, slot * 1024:(slot + 1) * 1024].rearrange(
                        "p (two n) -> p two n", two=2)[:, :, 0:N]
                    outap = pt[:, pslot, :].rearrange(
                        "p (two n) -> p two n", two=2)[:, :, 0:N]
                    a.activation(outap, inap, AF.Exp,
                                 bias=biasB[:, b * 8 + pair:b * 8 + pair + 1]
                                 ).then_inc(SEMS["A"], 1)
                C["A"] += 1
                A_exp[tt] = C["A"]
                exp_slot_wait[slot] = C["A"]

                @block.vector
                def _(v, tt=tt, pslot=tt % 3, N=N, ae=A_exp[tt]):
                    if tt == 0:
                        wt(v, "v", "dW", W_DONE)
                    wt(v, "v", "A", ae)
                    pap = bass.AP(pt.handle if hasattr(pt, "handle") else pt,
                                  pslot * 1024, [[3072, 128], [512, 2], [1, 32]])
                    v.tensor_mul(pap, pap, masks[:, tt % 2, :, :]).then_inc(
                        SEMS["V"], 1)
                C["V"] += 1
                V_mask[tt] = C["V"]

            emit_pv(29)
            emit_pv(30)
            emit_pv(31)
            prev_pvden_P = C["P"]

            @defer_scalar
            def _(a, pv=prev_pvden_P):
                wt(a, "a", "P", pv)
                a.activation(rsq[:, :], PD[:, :], AF.Ln, bias=cvals[:, 1:2])
                a.activation(rcp[:, :], rsq[:, :], AF.Exp, scale=-1.0).then_inc(SEMS["A"], 1)
            C["A"] += 1
            epi_a = C["A"]

            @block.vector
            def _(v, pair=pair, epi_a=epi_a, b=b, wod=wo_done_P):
                wt(v, "v", "A", epi_a)
                if b > 0:
                    wt(v, "v", "P", wod)
                v.tensor_mul(attnT[:, pair, :], PO[:, :], rcp[:, :]).then_inc(SEMS["V"], 1)
                C["V"] += 1
            prev_epilogue_V = C["V"]

            if b >= 1:
                emit_cupdate(b - 1, pair)
                if pair == 7:
                    cupd_last[b - 1] = C["V"]

        attn_done_V[b] = prev_epilogue_V

        @block.sync
        def _(s, b=b):
            if b > 0:
                wt(s, "s", "V", wo_radds[b - 1][6])
                wt(s, "s", "V", wo_radds[b - 1][7])
            dma(s, "dR0", xrc[:, 0, :], d_xq[b, 0:128, :])
            dma(s, "dR1", xrc[:, 1, :], d_xq[b, 128:256, :])
        xrc_loads = {0: ("dR0", C["dR0"]), 1: ("dR1", C["dR1"])}

        wo_units = []
        radds = {}
        for oc in range(8):
            bank = oc % 2

            @block.tensor
            def _(t, b=b, oc=oc, bank=bank):
                wt(t, "t", "V", attn_done_V[b])
                if oc >= 2:
                    wt(t, "t", "V", wo_units[oc - 2]["radd"])
                for cc in range(8):
                    t.matmul(PS[:, bank * 512:(bank + 1) * 512],
                             w_o[:, cc, oc * 128:(oc + 1) * 128],
                             attnT[:, cc, :],
                             start=(cc == 0), stop=(cc == 7)).then_maybe_inc(
                        (SEMS["P"], 1) if cc == 7 else None)
                pass
                C["P"] += 1
            womm = C["P"]

            @block.vector
            def _(v, oc=oc, bank=bank, womm=womm, xl=xrc_loads[oc]):
                wt(v, "v", "P", womm)
                wt(v, "v", xl[0], xl[1])
                v.tensor_add(rbuf[:, oc, :], PS[:, bank * 512:(bank + 1) * 512],
                             xrc[:, oc % 2, :]).then_inc(SEMS["V"], 1)
                C["V"] += 1
            radd = C["V"]
            radds[oc] = radd

            @block.sync
            def _(s, b=b, oc=oc, radd=radd):
                if oc + 2 < 8:
                    wt(s, "s", "V", radd)
                    key = "dR" + str(oc % 2)
                    dma(s, key, xrc[:, oc % 2, :], d_xq[b, (oc + 2) * 128:(oc + 3) * 128, :])
                    xrc_loads[oc + 2] = (key, C[key])

            @block.vector
            def _(v, oc=oc):
                v.tensor_mul(rsq[:, :], rbuf[:, oc, :], rbuf[:, oc, :]).then_inc(SEMS["V"], 1)
                C["V"] += 1
            rsq_a = C["V"]

            @block.tensor
            def _(t, oc=oc, radd=radd, rsq_a=rsq_a):
                wt(t, "t", "V", rsq_a)
                t.matmul(PD[0:1, 0:512], on_col[:, 0:1], rbuf[:, oc, :],
                         start=(oc == 0), stop=(oc == 7))
                t.matmul(PO[0:1, 0:512], on_col[:, 0:1], rsq[:, :],
                         start=(oc == 0), stop=(oc == 7)).then_inc(SEMS["P"], 1)
                C["P"] += 1
            wo_units.append(dict(radd=radd, stats=C["P"]))

        wo_radds[b] = radds
        ln_stats = wo_units[-1]["stats"]

        @block.vector
        def _(v, ln_stats=ln_stats):
            wt(v, "v", "P", ln_stats)
            v.tensor_scalar(mu_s, PD[0:1, 0:512], 1.0 / D, None, ALU.mult)
            v.tensor_scalar(var_s, PO[0:1, 0:512], 1.0 / D, None, ALU.mult)
            v.tensor_mul(m2_s, mu_s, mu_s)
            v.tensor_sub(var_s, var_s, m2_s).then_inc(SEMS["V"], 1)
            C["V"] += 1
        ln_v = C["V"]

        @defer_scalar
        def _(a, ln_v=ln_v):
            wt(a, "a", "V", ln_v)
            a.activation(sd_s, var_s, AF.Ln, bias=cvals[0:1, 0:1])
            a.activation(var_s, sd_s, AF.Exp, scale=-0.5).then_inc(SEMS["A"], 1)
        C["A"] += 1
        ln_r = C["A"]

        @block.tensor
        def _(t, ln_r=ln_r, ga=chunk_act.get(b * 24 - 1, 0)):
            wt(t, "t", "A", ln_r)
            if ga:
                wt(t, "t", "A", ga)
            t.matmul(PS[:, 2048:2560], on_row[0:1, 0:128], mu_s, start=True, stop=True)
            t.matmul(PS[:, 2560:3072], on_row[0:1, 0:128], var_s, start=True,
                     stop=True).then_inc(SEMS["P"], 1)
            C["P"] += 1
        ln_bc = C["P"]

        nrm_dmas = {}
        for oc in range(8):
            @block.vector
            def _(v, oc=oc, ln_bc=ln_bc, nrm_dmas=nrm_dmas):
                wt(v, "v", "P", ln_bc)
                if oc >= 2:
                    wt(v, "v", nrm_dmas[oc - 2][0], nrm_dmas[oc - 2][1])
                v.tensor_sub(tmpa[:, :], rbuf[:, oc, :], PS[:, 2048:2560])
                v.tensor_mul(nrm[:, oc % 2, :], tmpa[:, :],
                             PS[:, 2560:3072]).then_inc(SEMS["V"], 1)
                C["V"] += 1
            nrm_v = C["V"]

            @block.gpsimd
            def _(g, b=b, oc=oc, nrm_v=nrm_v, nrm_dmas=nrm_dmas):
                wt(g, "g", "V", nrm_v)
                key = "dN" + str(oc % 2)
                dma(g, key, d_xmid32[b, oc], nrm[:, oc % 2, :])
                dma(g, key, d_xmid16[b, oc], nrm[:, oc % 2, :])
                nrm_dmas[oc] = (key, C[key])
        wo_nrm_V[b] = C["V"]
        wo_done_P = ln_bc

    xmt_lds[3] = emit_xmt_load(3)
    for mo in range(8):
        for g in range(3):
            ci = 3 * 24 + mo * 3 + g
            emit_gate_unit(ci, 3, mo, g, xmt_lds[3],
                           extraV=(wo_nrm_V[3] if (mo == 0 and g == 0) else 0))
        emit_cupdate(3, mo)
    cupd_last[3] = C["V"]

    CFIN = B % 2
    ffn_c_done = C["V"]
    CI = 96

    h_radds = {}
    nrm_dmas5 = {}
    for tt in range(B):
        @block.sync
        def _(s, tt=tt, lastmm=chunk_mm[CI - 1]):
            wt(s, "s", "P", lastmm)
            dma(s, "dM", xmt[:, :, :], d_xmid16[tt].transpose([1, 0, 2]))
        hx_load = C["dM"]

        hu = []
        for oc in range(8):
            ci = CI
            CI += 1
            buf = ci % 2

            @block.tensor
            def _(t, oc=oc, ci=ci, buf=buf, hx_load=hx_load, ffn_c_done=ffn_c_done):
                wt(t, "t", chunk_ld[ci][0], chunk_ld[ci][1])
                wt(t, "t", "dM", hx_load)
                wt(t, "t", "V", ffn_c_done)
                bank = oc % 2
                if oc >= 2:
                    wt(t, "t", "V", hu[oc - 2]["radd"])
                for ic in range(8):
                    t.matmul(PS[:, bank * 512:(bank + 1) * 512],
                             fwc[:, buf, 0, ic, :], xmt[:, ic, :],
                             start=(ic == 0), stop=False)
                for ic in range(8):
                    t.matmul(PS[:, bank * 512:(bank + 1) * 512],
                             fwc[:, buf, 1, ic, :], cT16[:, CFIN, ic, :],
                             start=False, stop=(ic == 7)).then_maybe_inc(
                        (SEMS["P"], 1) if ic == 7 else None)
                pass
                C["P"] += 1
            hmm_ = C["P"]
            chunk_mm[ci] = hmm_
            emit_fwc_load(ci + 2)

            @defer_scalar
            def _(a, oc=oc, hmm_=hmm_):
                wt(a, "a", "P", hmm_)
                a.activation(hc[:, oc % 2, :], PS[:, (oc % 2) * 512:(oc % 2 + 1) * 512],
                             AF.Tanh, bias=bffs[:, 24 + oc:25 + oc]).then_inc(SEMS["A"], 1)
            C["A"] += 1
            hact = C["A"]

            @block.sync
            def _(s, tt=tt, oc=oc):
                key = "dO" + str(oc % 2)
                tenant = h_radds.get((tt, oc - 2)) if oc >= 2 else \
                    (h_radds.get((tt - 1, oc + 6)) if tt > 0 else None)
                if tenant is not None:
                    wt(s, "s", "V", tenant)
                dma(s, key, xm32c[:, oc % 2, :], d_xmid32[tt, oc])
            xm_ld = (("dO" + str(oc % 2)), C["dO" + str(oc % 2)])

            @block.vector
            def _(v, oc=oc, hact=hact, xm_ld=xm_ld):
                wt(v, "v", "A", hact)
                wt(v, "v", xm_ld[0], xm_ld[1])
                v.tensor_add(rbuf[:, oc, :], hc[:, oc % 2, :],
                             xm32c[:, oc % 2, :]).then_inc(SEMS["V"], 1)
                C["V"] += 1
            radd = C["V"]
            h_radds[(tt, oc)] = radd

            @block.vector
            def _(v, oc=oc):
                v.tensor_mul(rsq[:, :], rbuf[:, oc, :], rbuf[:, oc, :]).then_inc(SEMS["V"], 1)
                C["V"] += 1
            rsq_a = C["V"]

            @block.tensor
            def _(t, oc=oc, radd=radd, rsq_a=rsq_a):
                wt(t, "t", "V", rsq_a)
                t.matmul(PD[0:1, 0:512], on_col[:, 0:1], rbuf[:, oc, :],
                         start=(oc == 0), stop=(oc == 7))
                t.matmul(PO[0:1, 0:512], on_col[:, 0:1], rsq[:, :],
                         start=(oc == 0), stop=(oc == 7)).then_inc(SEMS["P"], 1)
                C["P"] += 1
            hu.append(dict(radd=radd, stats=C["P"]))

        @block.vector
        def _(v, hu=hu):
            wt(v, "v", "P", hu[-1]["stats"])
            v.tensor_scalar(mu_s, PD[0:1, 0:512], 1.0 / D, None, ALU.mult)
            v.tensor_scalar(var_s, PO[0:1, 0:512], 1.0 / D, None, ALU.mult)
            v.tensor_mul(m2_s, mu_s, mu_s)
            v.tensor_sub(var_s, var_s, m2_s).then_inc(SEMS["V"], 1)
            C["V"] += 1
        ln_v = C["V"]

        @defer_scalar
        def _(a, ln_v=ln_v):
            wt(a, "a", "V", ln_v)
            a.activation(sd_s, var_s, AF.Ln, bias=cvals[0:1, 0:1])
            a.activation(var_s, sd_s, AF.Exp, scale=-0.5).then_inc(SEMS["A"], 1)
        C["A"] += 1
        ln_r = C["A"]

        @block.tensor
        def _(t, ln_r=ln_r):
            wt(t, "t", "A", ln_r)
            t.matmul(PS[:, 2048:2560], on_row[0:1, 0:128], mu_s, start=True, stop=True)
            t.matmul(PS[:, 2560:3072], on_row[0:1, 0:128], var_s, start=True,
                     stop=True).then_inc(SEMS["P"], 1)
            C["P"] += 1
        ln_bc = C["P"]

        for oc in range(8):
            @block.vector
            def _(v, tt=tt, oc=oc, ln_bc=ln_bc):
                wt(v, "v", "P", ln_bc)
                tenant = nrm_dmas5.get((tt, oc - 2)) if oc >= 2 else \
                    (nrm_dmas5.get((tt - 1, oc + 6)) if tt > 0 else None)
                if tenant is not None:
                    wt(v, "v", tenant[0], tenant[1])
                v.tensor_sub(tmpa[:, :], rbuf[:, oc, :], PS[:, 2048:2560])
                v.tensor_mul(nrm[:, oc % 2, :], tmpa[:, :],
                             PS[:, 2560:3072]).then_inc(SEMS["V"], 1)
                C["V"] += 1
            nrm_v = C["V"]

            @block.gpsimd
            def _(g, tt=tt, oc=oc, nrm_v=nrm_v):
                wt(g, "g", "V", nrm_v)
                key = "dN" + str(oc % 2)
                dma(g, key, d_yT[tt, oc], nrm[:, oc % 2, :])
                nrm_dmas5[(tt, oc)] = (key, C[key])

    @block.gpsimd
    def _(g):
        wt(g, "g", "dN0", C["dN0"])
        wt(g, "g", "dN1", C["dN1"])

    @block.scalar
    def _(a):
        for fn in SCALAR_OPS:
            fn(a)

    ctx.close()
    return nc


def _host_prep(inputs):
    f32 = np.float32
    x = np.asarray(inputs["L0_attn_x"], f32)
    xT = np.ascontiguousarray(x.transpose(0, 2, 1))

    half = 32
    inv = (1.0 / (np.float32(10000.0) ** (np.arange(half, dtype=f32) / np.float32(half)))).astype(f32)

    def tables(pos, scale):
        ang = (pos.astype(f32)[:, None] * inv[None, :]).astype(f32)
        cos = (np.cos(ang).astype(f32) * f32(scale)).astype(f32)
        sin = (np.sin(ang).astype(f32) * f32(scale)).astype(f32)
        out = np.zeros((4, 128, 1024), f32)
        for s in range(4):
            out[s, :, 0:512] = np.tile(cos[s * 128:(s + 1) * 128], (1, 16))
            out[s, :, 512:1024] = np.tile(sin[s * 128:(s + 1) * 128], (1, 16))
        return out

    wq = np.asarray(inputs["L0_attn_W_q"], f32)
    wdkv = np.asarray(inputs["L0_attn_W_dkv"], f32)
    wuk = np.asarray(inputs["L0_attn_W_uk"], f32)
    wuv = np.asarray(inputs["L0_attn_W_uv"], f32)
    wo = np.asarray(inputs["L0_attn_W_o"], f32)
    gam = np.asarray(inputs["L0_attn_ln_gamma"], f32)
    bet = np.asarray(inputs["L0_attn_ln_beta"], f32)
    fwT = np.ascontiguousarray(np.stack([
        np.asarray(inputs["L0_ffn_wf"], f32).T, np.asarray(inputs["L0_ffn_uf"], f32).T,
        np.asarray(inputs["L0_ffn_wi"], f32).T, np.asarray(inputs["L0_ffn_ui"], f32).T,
        np.asarray(inputs["L0_ffn_wc"], f32).T, np.asarray(inputs["L0_ffn_uc"], f32).T,
        np.asarray(inputs["L0_ffn_wo"], f32).T, np.asarray(inputs["L0_ffn_uo"], f32).T,
    ]))
    bias4 = np.stack([np.asarray(inputs["L0_ffn_" + k], f32) for k in ("bf", "bi", "bc", "bo")])
    bff = np.ascontiguousarray(bias4.reshape(4, 8, 128).transpose(2, 0, 1).reshape(128, 32))
    incol = np.ascontiguousarray(np.asarray(inputs["L0_ffn_in_c"], f32).reshape(8, 128).T)
    gcol = np.ascontiguousarray(gam.reshape(2, 128).T)
    bcol = np.ascontiguousarray(bet.reshape(2, 128).T)

    shared = dict(wq=wq, wdkv=wdkv, wuk=wuk, wuv=wuv, wo=wo, fwT=fwT,
                  bff=bff, incol=incol, gcol=gcol, bcol=bcol,
                  idt=np.eye(128, dtype=f32))

    in_maps = []
    for i in range(NC):
        xq_i = np.ascontiguousarray(xT[:, :, i::8])
        xkv_i = np.ascontiguousarray(xT[:, :, 512 * i:512 * (i + 1)])
        pos_q = np.arange(i, T, 8)
        pos_kv = np.arange(512 * i, 512 * (i + 1))
        k_loc = np.arange(128)
        j_loc = np.arange(16)
        mm_ = (k_loc[:, None] <= (8 * j_loc[None, :] + i))
        maskb = np.zeros((128, 2, 2, 32), np.float32)
        maskb[:, 0, :, 0:16] = mm_[:, None, :]
        maskb[:, 0, :, 16:32] = 1.0
        maskb[:, 1, :, 0:16] = 0.0
        maskb[:, 1, :, 16:32] = mm_[:, None, :]
        m = dict(shared)
        m["xq"] = xq_i
        m["xkv"] = xkv_i
        m["ropek"] = tables(pos_kv, 1.0)
        m["ropeq"] = tables(pos_q, 0.125)
        m["maskb"] = np.ascontiguousarray(maskb.reshape(128, 128)).astype(ml_dtypes.bfloat16)
        in_maps.append(m)
    return in_maps


def kernel(**inputs):
    in_maps = _host_prep(inputs)
    nc = build_program()
    res = run_bass_kernel_spmd(nc, in_maps, core_ids=list(range(NC)))
    outT = np.empty((B, D, T), np.float32)
    for i in range(NC):
        outT[:, :, i::8] = res.results[i]["yT"].reshape(B, D, 512)
    return np.ascontiguousarray(outT.transpose(0, 2, 1))


# version 22
# speedup vs baseline: 1.0683x; 1.0683x over previous
import numpy as np
import ml_dtypes
from contextlib import ExitStack

import concourse.bass as bass
import concourse.mybir as mybir
from concourse.bass_utils import run_bass_kernel_spmd

F32 = mybir.dt.float32
BF16 = mybir.dt.bfloat16
AF = mybir.ActivationFunctionType
ALU = mybir.AluOpType

B = 4
T = 4096
D = 1024
LT = 256
NC = 8
EPS = 1e-5


def build_program():
    nc = bass.Bass("TRN2", target_bir_lowering=False, debug=False,
                   detect_race_conditions=False)

    d_xkv = nc.dram_tensor("xkv", [B, D, 512], F32, kind="ExternalInput").ap()
    d_xq = nc.dram_tensor("xq", [B, D, 512], F32, kind="ExternalInput").ap()
    d_wq = nc.dram_tensor("wq", [D, D], F32, kind="ExternalInput").ap()
    d_wdkv = nc.dram_tensor("wdkv", [D, LT], F32, kind="ExternalInput").ap()
    d_wuk = nc.dram_tensor("wuk", [LT, D], F32, kind="ExternalInput").ap()
    d_wuv = nc.dram_tensor("wuv", [LT, D], F32, kind="ExternalInput").ap()
    d_wo = nc.dram_tensor("wo", [D, D], F32, kind="ExternalInput").ap()
    d_fwT = nc.dram_tensor("fwT", [8, D, D], F32, kind="ExternalInput").ap()
    d_bff = nc.dram_tensor("bff", [128, 32], F32, kind="ExternalInput").ap()
    d_incol = nc.dram_tensor("incol", [128, 8], F32, kind="ExternalInput").ap()
    d_gcol = nc.dram_tensor("gcol", [128, 2], F32, kind="ExternalInput").ap()
    d_bcol = nc.dram_tensor("bcol", [128, 2], F32, kind="ExternalInput").ap()
    d_ropek = nc.dram_tensor("ropek", [4, 128, 1024], F32, kind="ExternalInput").ap()
    d_ropeq = nc.dram_tensor("ropeq", [4, 128, 1024], F32, kind="ExternalInput").ap()
    d_maskb = nc.dram_tensor("maskb", [128, 128], BF16, kind="ExternalInput").ap()
    d_idt = nc.dram_tensor("idt", [128, 128], F32, kind="ExternalInput").ap()

    d_kvsh = nc.dram_tensor("kvsh", [B, 8, 2, 128, 4, 128], BF16).ap()
    d_kvfull = nc.dram_tensor("kvfull", [B, NC, 8, 2, 128, 4, 128], BF16,
                              addr_space="Shared").ap()
    d_stsh = nc.dram_tensor("stsh", [B, 128], F32).ap()
    d_stfull = nc.dram_tensor("stfull", [B, NC, 128], F32).ap()
    d_fcache = nc.dram_tensor("fcache", [8, 8, 128, 8, 128], BF16).ap()
    d_xmid32 = nc.dram_tensor("xmid32", [B, 8, 128, 512], F32).ap()
    d_xmid16 = nc.dram_tensor("xmid16", [B, 8, 128, 512], BF16).ap()
    d_yT = nc.dram_tensor("yT", [B, 8, 128, 512], F32, kind="ExternalOutput").ap()

    ctx = ExitStack()
    sb = lambda name, shape, dt: ctx.enter_context(nc.sbuf_tensor(name, shape, dt))
    ps = lambda name, shape: ctx.enter_context(nc.psum_tensor(name, shape, F32))

    on_mat = sb("on_mat", [128, 64], BF16)
    on_row = sb("on_row", [1, 128], F32)
    on_col = sb("on_col", [128, 1], F32)
    w_o = sb("w_o", [128, 8, 1024], BF16)
    bffs = sb("bffs", [128, 32], F32)
    incs = sb("incs", [128, 8], F32)
    gcs = sb("gcs", [128, 2], F32)
    bcs = sb("bcs", [128, 2], F32)
    masks = sb("masks", [128, 2, 2, 32], BF16)
    biasB = sb("biasB", [128, 32], F32)
    Mq = sb("Mq", [128, 32], F32)
    Mk = sb("Mk", [128, 4], F32)
    MqR = sb("MqR", [128, 32], F32)
    MkR = sb("MkR", [128, 4], F32)
    sqMq = sb("sqMq", [128, 32], F32)
    kmb = sb("kmb", [128, 4], F32)
    strd = sb("strd", [1, 64], F32)
    kmaxg = sb("kmaxg", [1, 8], F32)
    mrow = sb("mrow", [1, 40], F32)
    idt = sb("idt_s", [128, 128], F32)
    lnscr = sb("lnscr", [1, 1024], F32)
    tmpa = sb("tmpa", [128, 512], F32)
    tmpb = sb("tmpb", [128, 512], F32)
    cvals = sb("cvals", [128, 2], F32)
    zrow = sb("zrow", [1, 512], BF16)
    on_row16 = sb("on_row16", [1, 128], BF16)
    qhat = sb("qhat", [128, B, 8, 4, 128], BF16)

    bctx = ExitStack()
    bsb = lambda name, shape, dt: bctx.enter_context(nc.sbuf_tensor(name, shape, dt))
    w_q = bsb("w_q", [128, 8, 1024], BF16)
    w_dkv = bsb("w_dkv", [128, 8, 256], BF16)
    w_uk = bsb("w_uk", [128, 2, 1024], BF16)
    w_uv = bsb("w_uv", [128, 2, 1024], BF16)
    xkv_s = bsb("xkv_s", [128, 2, 8, 512], BF16)
    xq_s = bsb("xq_s", [128, 2, 8, 512], BF16)
    ltf = bsb("ltf", [128, 2, 512], F32)
    ltsq = bsb("ltsq", [128, 2, 512], F32)
    latn = bsb("latn", [128, 2, 2, 512], BF16)
    ropbk = bsb("ropbk", [128, 4, 1024], F32)
    ropbq = bsb("ropbq", [128, 4, 1024], F32)
    krop = bsb("krop", [128, 2, 1024], BF16)
    qrop = bsb("qrop", [128, 2, 1024], BF16)
    kts = bsb("kts", [128, 2, 1024], BF16)
    vst = bsb("vst", [128, 2, 1024], BF16)
    sqk = bsb("sqk", [128, 1024], F32)
    sqq = bsb("sqq", [128, 1024], F32)
    n16 = bsb("n16", [128, 16], F32)
    n8 = bsb("n8", [128, 8], F32)
    ncol = bsb("ncol", [128, 4], F32)

    PS = ps("PS", [128, 3072])
    PO = ps("PO", [128, 512])
    PD = ps("PD", [128, 512])

    SEMNAMES = [
        "dW",
        "dWa",
        "dX0", "dX1",
        "dC",
        "dT0", "dT1",
        "dK0", "dK1",
        "dQ0", "dQ1",
        "dH0", "dH1",
        "dSS",
        "dV0", "dV1",
        "dR0", "dR1",
        "dN0", "dN1", "dN2", "dN3",
        "dM",
        "dO0", "dO1",
        "dF0", "dF1",
        "P", "A", "V", "G", "CC",
    ]
    SEMS = {}
    for nm in SEMNAMES:
        SEMS[nm] = ctx.enter_context(nc.semaphore("s_" + nm))
    C = {nm: 0 for nm in SEMNAMES}
    lastw = {}

    def wt(eng, ename, key, val):
        if val <= 0:
            return
        if lastw.get((ename, key), -1) >= val:
            return
        eng.wait_ge(SEMS[key], val)
        lastw[(ename, key)] = val

    def dma(eng, key, out, in_, transpose=False):
        if transpose:
            eng.dma_start(out=out, in_=in_, transpose=True).then_inc(SEMS[key], 16)
        else:
            eng.dma_start(out=out, in_=in_).then_inc(SEMS[key], 16)
        C[key] += 16
        return C[key]

    block = ctx.enter_context(nc.Block())

    SCALAR_OPS = []

    def defer_scalar(fn):
        SCALAR_OPS.append(fn)
        return None

    @block.gpsimd
    def _(g):
        g.memset(on_mat[:, :], 1.0)
        g.memset(on_row[:, :], 1.0)
        g.memset(on_col[:, :], 1.0)
        g.memset(Mk[:, :], 0.0)
        g.memset(Mq[:, :], 0.0)
        g.memset(cvals[:, 0:1], EPS)
        g.memset(cvals[:, 1:2], 1e-30)
        g.memset(zrow[:, :], 0.0)
        g.memset(on_row16[:, :], 1.0).then_inc(SEMS["G"], 1)
        C["G"] += 1
        for dst, src in (
            (w_dkv[:, :, :], d_wdkv), (w_q[:, :, :], d_wq),
            (w_uk[:, :, :], d_wuk), (w_uv[:, :, :], d_wuv),
        ):
            dma(g, "dWa", dst, src.rearrange("(ic p) d -> p ic d", p=128))
        dma(g, "dW", w_o[:, :, :], d_wo.rearrange("(ic p) d -> p ic d", p=128))
        dma(g, "dW", masks[:, :, :, :], d_maskb.rearrange("p (par two c) -> p par two c", par=2, c=32))
    WA_DONE = C["dWa"]

    @block.sync
    def _(s):
        dma(s, "dC", idt[:, :], d_idt)
        dma(s, "dC", bffs[:, :], d_bff)
        dma(s, "dC", incs[:, :], d_incol)
        dma(s, "dC", gcs[:, :], d_gcol)
        dma(s, "dC", bcs[:, :], d_bcol)
        dma(s, "dC", ropbk[:, :, :], d_ropek.transpose([1, 0, 2]))
        dma(s, "dC", ropbq[:, :, :], d_ropeq.transpose([1, 0, 2]))
    CONST_DONE = C["dC"]

    @block.gpsimd
    def _(g):
        dma(g, "dX0", xkv_s[:, 0, :, :], d_xkv[0].rearrange("(ic p) t -> p ic t", p=128))
        dma(g, "dX0", xq_s[:, 0, :, :], d_xq[0].rearrange("(ic p) t -> p ic t", p=128))
    XLOAD = {0: ("dX0", C["dX0"])}

    slot_units = {}
    lat_units = {}
    lat_norm = 0
    ph2_V = {}
    SQMQ_B = {}
    CC_STATS = {}
    CC_KV = {}
    BIAS_B = {}

    mu_s = lnscr[0:1, 0:512]
    var_s = lnscr[0:1, 512:1024]
    m2_s = tmpa[0:1, 0:512]
    sd_s = tmpb[0:1, 0:512]

    def emit_lat_chain(b):
        bb = b % 2

        @block.tensor
        def _(t, b=b, bb=bb):
            wt(t, "t", "dWa", WA_DONE)
            wt(t, "t", "G", 1)
            wt(t, "t", XLOAD[b][0], XLOAD[b][1])
            if b > 0:
                wt(t, "t", "V", lat_units[b - 1]["norm_done"])
            for lh in range(2):
                for ic in range(8):
                    t.matmul(PS[:, 1024 + lh * 512:1024 + (lh + 1) * 512],
                             w_dkv[:, ic, lh * 128:(lh + 1) * 128],
                             xkv_s[:, bb, ic, :],
                             start=(ic == 0), stop=(ic == 7)).then_maybe_inc(
                        (SEMS["P"], 1) if (lh == 1 and ic == 7) else None)
            pass
            C["P"] += 1
        lat_mm = C["P"]

        @block.vector
        def _(v, lat_mm=lat_mm):
            wt(v, "v", "P", lat_mm)
            v.tensor_copy(ltf[:, 0, :], PS[:, 1024:1536])
            v.tensor_copy(ltf[:, 1, :], PS[:, 1536:2048]).then_inc(SEMS["V"], 1)
            C["V"] += 1

        @block.vector
        def _(v):
            v.tensor_mul(ltsq[:, 0, :], ltf[:, 0, :], ltf[:, 0, :])
            v.tensor_mul(ltsq[:, 1, :], ltf[:, 1, :],
                         ltf[:, 1, :]).then_inc(SEMS["V"], 1)
            C["V"] += 1
        lat_sq = C["V"]

        @block.tensor
        def _(t, lat_sq=lat_sq):
            wt(t, "t", "V", lat_sq)
            for lh in range(2):
                t.matmul(PS[0:1, 1024:1536], on_col[:, 0:1], ltf[:, lh, :],
                         start=(lh == 0), stop=(lh == 1))
            for lh in range(2):
                t.matmul(PS[0:1, 1536:2048], on_col[:, 0:1], ltsq[:, lh, :],
                         start=(lh == 0), stop=(lh == 1)).then_maybe_inc(
                    (SEMS["P"], 1) if lh == 1 else None)
            pass
            C["P"] += 1
        lat_stats = C["P"]

        @block.vector
        def _(v, lat_stats=lat_stats):
            wt(v, "v", "P", lat_stats)
            v.tensor_scalar(mu_s, PS[0:1, 1024:1536], 1.0 / LT, None, ALU.mult)
            v.tensor_scalar(var_s, PS[0:1, 1536:2048], 1.0 / LT, None, ALU.mult)
            v.tensor_mul(m2_s, mu_s, mu_s)
            v.tensor_sub(var_s, var_s, m2_s).then_inc(SEMS["V"], 1)
            C["V"] += 1
        lat_v1 = C["V"]

        @defer_scalar
        def _(a, lat_v1=lat_v1):
            wt(a, "a", "V", lat_v1)
            a.activation(sd_s, var_s, AF.Ln, bias=cvals[0:1, 0:1])
            a.activation(var_s, sd_s, AF.Exp, scale=-0.5).then_inc(SEMS["A"], 1)
        C["A"] += 1
        lat_rstd = C["A"]

        @block.tensor
        def _(t, lat_rstd=lat_rstd):
            wt(t, "t", "A", lat_rstd)
            t.matmul(PS[:, 1024:1536], on_row[0:1, 0:128], mu_s, start=True, stop=True)
            t.matmul(PS[:, 1536:2048], on_row[0:1, 0:128], var_s, start=True,
                     stop=True).then_inc(SEMS["P"], 1)
            C["P"] += 1
        lat_bc = C["P"]

        @block.vector
        def _(v, bb=bb, lat_bc=lat_bc):
            wt(v, "v", "P", lat_bc)
            wt(v, "v", "dC", CONST_DONE)
            for lh in range(2):
                v.tensor_sub(tmpa[:, :], ltf[:, lh, :], PS[:, 1024:1536])
                v.tensor_mul(tmpb[:, :], tmpa[:, :], PS[:, 1536:2048])
                v.tensor_scalar(latn[:, bb, lh, :], tmpb[:, :],
                                gcs[:, lh:lh + 1], bcs[:, lh:lh + 1],
                                ALU.mult, ALU.add).then_maybe_inc(
                    (SEMS["V"], 1) if lh == 1 else None)
            pass
            C["V"] += 1
        lat_units[b] = {"norm_done": C["V"]}

    for b in range(B):
        bb = b % 2

        if b + 1 < B:
            @block.gpsimd
            def _(g, b=b):
                nb = (b + 1) % 2
                key = "dX" + str(nb)
                if b + 1 >= 2:
                    wt(g, "g", "P", slot_units[(b - 1, 3)]["q_mm"])
                dma(g, key, xkv_s[:, nb, :, :],
                    d_xkv[b + 1].rearrange("(ic p) t -> p ic t", p=128))
                dma(g, key, xq_s[:, nb, :, :],
                    d_xq[b + 1].rearrange("(ic p) t -> p ic t", p=128))
                XLOAD[b + 1] = (key, C[key])

        if b == 0:
            emit_lat_chain(0)
        lat_norm = lat_units[b]["norm_done"]

        for s_ in range(4):
            par = s_ % 2
            tk = "dT" + str(par)
            prev = slot_units.get((b, s_ - 1)) if s_ > 0 else \
                (slot_units.get((b - 1, 3)) if b > 0 else None)
            prev2 = slot_units.get((b, s_ - 2)) if s_ > 1 else \
                (slot_units.get((b - 1, 2 + s_)) if b > 0 else None)

            @block.tensor
            def _(t, bb=bb, s_=s_, prev=prev, lat_norm=lat_norm):
                wt(t, "t", "V", lat_norm)
                if prev is not None:
                    wt(t, "t", "V", prev["vevac_done"])
                for dh in range(2):
                    for lh in range(2):
                        t.matmul(PS[:, dh * 512:(dh + 1) * 512],
                                 latn[:, bb, lh, s_ * 128:s_ * 128 + 128],
                                 w_uk[:, lh, dh * 512:(dh + 1) * 512],
                                 start=(lh == 0), stop=(lh == 1))
                for dh in range(2):
                    dst = PO if dh == 0 else PD
                    for lh in range(2):
                        t.matmul(dst[:, :],
                                 latn[:, bb, lh, s_ * 128:s_ * 128 + 128],
                                 w_uv[:, lh, dh * 512:(dh + 1) * 512],
                                 start=(lh == 0), stop=(lh == 1)).then_maybe_inc(
                            (SEMS["P"], 1) if (dh == 1 and lh == 1) else None)
                pass
                C["P"] += 1
            kv_mm = C["P"]

            @block.tensor
            def _(t, b=b, bb=bb, s_=s_, prev=prev):
                if prev is not None:
                    wt(t, "t", "V", prev["ropeq_done"])
                if s_ == 0 and b > 0:
                    wt(t, "t", "V", ph2_V[b - 1])
                for dh in range(2):
                    for ic in range(8):
                        t.matmul(PS[:, 2048 + dh * 512:2048 + (dh + 1) * 512],
                                 xq_s[:, bb, ic, s_ * 128:s_ * 128 + 128],
                                 w_q[:, ic, dh * 512:(dh + 1) * 512],
                                 start=(ic == 0), stop=(ic == 7)).then_maybe_inc(
                            (SEMS["P"], 1) if (dh == 1 and ic == 7) else None)
                pass
                C["P"] += 1
            q_mm = C["P"]

            def rope_emit(v, psoff, tblbuf, tbl, dst, dstbuf):
                ap = PS[:, psoff:psoff + 1024].rearrange(
                    "p (h two q) -> p h two q", h=16, two=2)
                x1 = ap[:, :, 0, :]
                x2 = ap[:, :, 1, :]
                cos = tbl[:, tblbuf, 0:512].rearrange("p (h q) -> p h q", h=16)
                sin = tbl[:, tblbuf, 512:1024].rearrange("p (h q) -> p h q", h=16)
                ta = tmpa[:, :].rearrange("p (h q) -> p h q", h=16)
                tb = tmpb[:, :].rearrange("p (h q) -> p h q", h=16)
                dap = dst[:, dstbuf, :].rearrange("p (h two q) -> p h two q", h=16, two=2)
                v.tensor_mul(ta, x1, cos)
                v.tensor_mul(tb, x2, sin)
                v.tensor_sub(dap[:, :, 0, :], ta, tb)
                v.tensor_mul(ta, x1, sin)
                v.tensor_mul(tb, x2, cos)
                return v.tensor_add(dap[:, :, 1, :], ta, tb)

            @block.vector
            def _(v, s_=s_, par=par, kv_mm=kv_mm, prev2=prev2):
                wt(v, "v", "P", kv_mm)
                wt(v, "v", "dC", CONST_DONE)
                if prev2 is not None:
                    wt(v, "v", "dK" + str(par), prev2["ktr_done"])
                rope_emit(v, 0, s_, ropbk, krop, par).then_inc(SEMS["V"], 1)
                C["V"] += 1
            ropek_done = C["V"]

            @block.vector
            def _(v, par=par, prev2=prev2):
                if prev2 is not None:
                    wt(v, "v", "dH" + str(par), prev2["shard_done"])
                v.tensor_copy(vst[:, par, 0:512], PO[:, :])
                v.tensor_copy(vst[:, par, 512:1024], PD[:, :]).then_inc(SEMS["V"], 1)
                C["V"] += 1
            vevac_done = C["V"]

            @block.vector
            def _(v, s_=s_, par=par, q_mm=q_mm, prev2=prev2):
                wt(v, "v", "P", q_mm)
                if prev2 is not None:
                    wt(v, "v", "dQ" + str(par), prev2["qtr_done"])
                rope_emit(v, 2048, s_, ropbq, qrop, par).then_inc(SEMS["V"], 1)
                C["V"] += 1
            ropeq_done = C["V"]

            @block.vector
            def _(v, par=par):
                v.tensor_mul(sqk[:, :], krop[:, par, :], krop[:, par, :]).then_inc(SEMS["V"], 1)
                C["V"] += 1
                v.tensor_mul(sqq[:, :], qrop[:, par, :], qrop[:, par, :]).then_inc(SEMS["V"], 1)
                C["V"] += 1
            sq_done = C["V"]

            @block.vector
            def _(v, b=b, sq_done=sq_done):
                v.reduce_sum(n16[:, :], sqk[:, :].rearrange("p (h q) -> p h q", h=16),
                             axis=mybir.AxisListType.X)
                v.reduce_max(ncol[:, 0:1], n16[:, :], axis=mybir.AxisListType.X)
                v.tensor_max(Mk[:, b:b + 1], Mk[:, b:b + 1], ncol[:, 0:1])
                v.reduce_sum(n16[:, :], sqq[:, :].rearrange("p (h q) -> p h q", h=16),
                             axis=mybir.AxisListType.X)
                v.reduce_max(n8[:, :], n16[:, :].rearrange("p (pr two) -> p pr two", two=2),
                             axis=mybir.AxisListType.X)
                v.tensor_max(Mq[:, b * 8:(b + 1) * 8], Mq[:, b * 8:(b + 1) * 8],
                             n8[:, :]).then_inc(SEMS["V"], 1)
                C["V"] += 1
            norms_done = C["V"]

            @block.sync
            def _(s, par=par, ropek_done=ropek_done, prev2=prev2):
                wt(s, "s", "V", ropek_done)
                if prev2 is not None:
                    wt(s, "s", "dH" + str(par), prev2["shard_done"])
                for pr in range(8):
                    dma(s, "dK" + str(par), kts[:, par, pr * 128:(pr + 1) * 128],
                        krop[:, par, pr * 128:(pr + 1) * 128], transpose=True)
            ktr_done = C["dK" + str(par)]

            @block.sync
            def _(s, b=b, s_=s_, par=par, ropeq_done=ropeq_done):
                wt(s, "s", "V", ropeq_done)
                for pr in range(8):
                    dma(s, "dQ" + str(par), qhat[:, b, pr, s_, :],
                        qrop[:, par, pr * 128:(pr + 1) * 128], transpose=True)
            qtr_done = C["dQ" + str(par)]

            @block.gpsimd
            def _(g, b=b, s_=s_, par=par, ktr_done=ktr_done, vevac_done=vevac_done):
                wt(g, "g", "dK" + str(par), ktr_done)
                dma(g, "dH" + str(par), d_kvsh[b, :, 0, :, s_, :].transpose([1, 0, 2]),
                    kts[:, par, :].rearrange("p (pr c) -> p pr c", c=128))
                wt(g, "g", "V", vevac_done)
                dma(g, "dH" + str(par), d_kvsh[b, :, 1, :, s_, :].transpose([1, 0, 2]),
                    vst[:, par, :].rearrange("p (pr c) -> p pr c", c=128))
            shard_done = C["dH" + str(par)]

            slot_units[(b, s_)] = dict(
                ropek_done=ropek_done, ropeq_done=ropeq_done, vevac_done=vevac_done,
                norms_done=norms_done, ktr_done=ktr_done, qtr_done=qtr_done,
                shard_done=shard_done, q_mm=q_mm,
            )

            if s_ == 2 and b + 1 < B:
                emit_lat_chain(b + 1)

        nd = slot_units[(b, 3)]["norms_done"]

        @block.tensor
        def _(t, b=b, nd=nd):
            wt(t, "t", "V", nd)
            wt(t, "t", "dC", CONST_DONE)
            t.transpose(PS[0:8, 2048:2176], Mq[:, 8 * b:8 * b + 8], idt[:, :])
            t.transpose(PS[0:1, 2176:2304], Mk[:, b:b + 1],
                        idt[:, :]).then_inc(SEMS["P"], 1)
            C["P"] += 1
        tr1 = C["P"]

        @block.vector
        def _(v, b=b, tr1=tr1):
            wt(v, "v", "P", tr1)
            v.reduce_max(tmpa[0:8, 0:1], PS[0:8, 2048:2176], axis=mybir.AxisListType.X)
            v.reduce_max(tmpb[0:1, 0:1], PS[0:1, 2176:2304],
                         axis=mybir.AxisListType.X)
            v.tensor_copy(mrow[0:1, 32 + b:33 + b],
                          tmpb[0:1, 0:1]).then_inc(SEMS["V"], 1)
            C["V"] += 1
        rmx = C["V"]

        @block.tensor
        def _(t, rmx=rmx):
            wt(t, "t", "V", rmx)
            t.transpose(PS[0:1, 2560:2568], tmpa[0:8, 0:1],
                        idt[0:8, 0:8]).then_inc(SEMS["P"], 1)
            C["P"] += 1
        tr2 = C["P"]

        @block.vector
        def _(v, tr2=tr2):
            wt(v, "v", "P", tr2)
            v.tensor_copy(mrow[0:1, 0:8], PS[0:1, 2560:2568]).then_inc(SEMS["V"], 1)
            C["V"] += 1
        cpy = C["V"]

        @block.tensor
        def _(t, cpy=cpy):
            wt(t, "t", "V", cpy)
            t.matmul(PS[:, 2568:2576], on_row[0:1, 0:128], mrow[0:1, 0:8],
                     start=True, stop=True).then_inc(SEMS["P"], 1)
            C["P"] += 1
        bcm = C["P"]

        @block.vector
        def _(v, b=b, bcm=bcm):
            wt(v, "v", "P", bcm)
            v.tensor_copy(MqR[:, 8 * b:8 * b + 8],
                          PS[:, 2568:2576]).then_inc(SEMS["V"], 1)
            C["V"] += 1
        ph2_V[b] = C["V"]

        @defer_scalar
        def _(a, b=b, pv=ph2_V[b]):
            wt(a, "a", "V", pv)
            a.activation(sqMq[:, 8 * b:8 * b + 8], MqR[:, 8 * b:8 * b + 8],
                         AF.Ln, bias=cvals[:, 1:2])
            a.activation(sqMq[:, 8 * b:8 * b + 8], sqMq[:, 8 * b:8 * b + 8],
                         AF.Exp, scale=0.5).then_inc(SEMS["A"], 1)
        C["A"] += 1
        SQMQ_B[b] = C["A"]

        @block.gpsimd
        def _(g, b=b, rmx=rmx, kvdh=(C["dH0"], C["dH1"])):
            wt(g, "g", "V", rmx)
            dma(g, "dSS", d_stsh[b:b + 1, 1:128], on_row[0:1, 1:128])
            dma(g, "dSS", d_stsh[b:b + 1, 0:1], mrow[0:1, 32 + b:33 + b])
            wt(g, "g", "dSS", C["dSS"])
            g.collective_compute(
                "AllGather", ALU.bypass, replica_groups=[list(range(NC))],
                ins=[d_stsh[b:b + 1, :]], outs=[d_stfull[b]],
            ).then_inc(SEMS["CC"], 1)
            C["CC"] += 1
            if b == 0:
                wt(g, "g", "dH0", kvdh[0])
                wt(g, "g", "dH1", kvdh[1])
                g.collective_compute(
                    "AllGather", ALU.bypass, replica_groups=[list(range(NC))],
                    ins=[d_kvsh[0]], outs=[d_kvfull[0]],
                ).then_inc(SEMS["CC"], 1)
                C["CC"] += 1
        CC_STATS[b] = C["CC"] - (1 if b == 0 else 0)
        if b == 0:
            CC_KV[b] = C["CC"]

            @block.gpsimd
            def _(g):
                for fm in range(8):
                    for foc in range(8):
                        dma(g, "dW", d_fcache[fm, foc],
                            d_fwT[fm].rearrange("(ic p) (oc c) -> oc p ic c",
                                                p=128, c=128)[foc])

    @block.gpsimd
    def _(g):
        wt(g, "g", "dH0", C["dH0"])
        wt(g, "g", "dH1", C["dH1"])
        for bq in range(1, B):
            g.collective_compute(
                "AllGather", ALU.bypass, replica_groups=[list(range(NC))],
                ins=[d_kvsh[bq]], outs=[d_kvfull[bq]],
            ).then_inc(SEMS["CC"], 1)
            C["CC"] += 1
            CC_KV[bq] = C["CC"]

    W_DONE = C["dW"]
    LAST_NORMS = slot_units[(B - 1, 3)]["norms_done"]

    kmb_done = 0
    for b in range(B):
        @block.sync
        def _(s, b=b, cs=CC_STATS[b]):
            wt(s, "s", "CC", cs)
            s.dma_start(out=strd[0:1, b * 8:b * 8 + 8],
                        in_=d_stfull[b, :, 0:1].transpose([1, 0]),
                        allow_slow_non_contiguous=True).then_inc(SEMS["dC"], 16)
            C["dC"] += 16

        @block.vector
        def _(v, b=b, sl=C["dC"]):
            wt(v, "v", "dC", sl)
            v.reduce_max(kmaxg[0:1, b:b + 1], strd[0:1, b * 8:b * 8 + 8],
                         axis=mybir.AxisListType.X).then_inc(SEMS["V"], 1)
            C["V"] += 1
        kmg = C["V"]

        @defer_scalar
        def _(a, b=b, kmg=kmg):
            wt(a, "a", "V", kmg)
            a.activation(kmaxg[0:1, 4 + b:5 + b], kmaxg[0:1, b:b + 1],
                         AF.Ln, bias=cvals[0:1, 1:2])
            a.activation(kmaxg[0:1, b:b + 1], kmaxg[0:1, 4 + b:5 + b],
                         AF.Exp, scale=0.5).then_inc(SEMS["A"], 1)
        C["A"] += 1
        ksq = C["A"]

        @block.tensor
        def _(t, b=b, ksq=ksq, pk=kmb_done):
            wt(t, "t", "A", ksq)
            if b == 0:
                wt(t, "t", "V", ph2_V[B - 1])
            if pk:
                wt(t, "t", "V", pk)
            t.matmul(PS[:, 1024 + b:1025 + b], on_row[0:1, 0:128],
                     kmaxg[0:1, b:b + 1], start=True,
                     stop=True).then_inc(SEMS["P"], 1)
            C["P"] += 1
        kmm = C["P"]

        @block.vector
        def _(v, b=b, kmm=kmm, sq=SQMQ_B[b]):
            wt(v, "v", "P", kmm)
            v.tensor_copy(kmb[:, b:b + 1], PS[:, 1024 + b:1025 + b])
            wt(v, "v", "A", sq)
            v.tensor_scalar(biasB[:, b * 8:(b + 1) * 8], sqMq[:, b * 8:(b + 1) * 8],
                            kmb[:, b:b + 1], -1.0, ALU.mult,
                            ALU.mult).then_inc(SEMS["V"], 1)
            C["V"] += 1
        kmb_done = C["V"]
        BIAS_B[b] = C["V"]
    BIAS_DONE = BIAS_B[B - 1]

    BAR = dict(C)

    def barrier_waits(eng, ename):
        for k in SEMNAMES:
            if k != "CC":
                wt(eng, ename, k, BAR[k])

    @block.tensor
    def _(t):
        barrier_waits(t, "t")

    @block.vector
    def _(v):
        barrier_waits(v, "v")

    @defer_scalar
    def _(a):
        barrier_waits(a, "a")

    @block.sync
    def _(s):
        barrier_waits(s, "s")

    @block.gpsimd
    def _(g):
        barrier_waits(g, "g")

    bctx.close()

    kvK = sb("kvK", [128, 2, 32, 128], BF16)
    kvV = sb("kvV", [128, 2, 32, 128], BF16)
    pt = sb("pt", [128, 3, 1024], BF16)
    attnT = sb("attnT", [128, 8, 512], BF16)
    rcp = sb("rcp", [128, 512], F32)
    xrc = sb("xrc", [128, 2, 512], F32)
    rbuf = sb("rbuf", [128, 8, 512], F32)
    rsq = sb("rsq", [128, 512], F32)
    nrm = sb("nrm", [128, 4, 512], F32)
    xmt = sb("xmt", [128, 8, 512], BF16)
    xm32c = sb("xm32c", [128, 2, 512], F32)
    cT32 = sb("cT32", [128, 8, 512], F32)
    cT16 = sb("cT16", [128, 2, 8, 512], BF16)
    fwc = sb("fwc", [128, 2, 2, 8, 128], BF16)
    gf = sb("gf", [128, 512], BF16)
    gi = sb("gi", [128, 512], BF16)
    gg = sb("gg", [128, 512], BF16)
    hc = sb("hc", [128, 2, 512], F32)

    @defer_scalar
    def _(a):
        wt(a, "a", "dC", CONST_DONE)
        for j in range(8):
            a.activation(cT32[:, j, :], cT32[:, j, :], AF.Identity,
                         bias=incs[:, j:j + 1], scale=0.0).then_maybe_inc(
                (SEMS["A"], 1) if j == 7 else None)
        pass
    C["A"] += 1
    CINIT_A = C["A"]

    @block.vector
    def _(v):
        wt(v, "v", "A", CINIT_A)
        for j in range(8):
            v.tensor_copy(cT16[:, 0, j, :], cT32[:, j, :]).then_maybe_inc(
                (SEMS["V"], 1) if j == 7 else None)
        pass
        C["V"] += 1
    cready = C["V"]

    FCHUNKS = []
    for _t in range(B):
        for _mo in range(8):
            for _g in range(3):
                FCHUNKS.append((2 * _g, 2 * _g + 1, _mo))
    for _t in range(B):
        for _oc in range(8):
            FCHUNKS.append((6, 7, _oc))
    chunk_mm = {}
    chunk_ld = {}
    chunk_act = {}
    cupd_last = {-1: cready}
    ffn_state = {"last_cupd": cready}

    def emit_fwc_load(ci):
        if ci >= len(FCHUNKS):
            return
        wm, um, oc_ = FCHUNKS[ci]
        buf = ci % 2
        key = "dF" + str(buf)

        @block.gpsimd
        def _(g, ci=ci, wm=wm, um=um, oc_=oc_, buf=buf, key=key):
            wt(g, "g", "dW", W_DONE)
            if ci - 2 in chunk_mm:
                wt(g, "g", "P", chunk_mm[ci - 2])
            dma(g, key, fwc[:, buf, 0, :, :], d_fcache[wm, oc_])
            dma(g, key, fwc[:, buf, 1, :, :], d_fcache[um, oc_])
        chunk_ld[ci] = (key, C[key])

    emit_fwc_load(0)
    emit_fwc_load(1)

    def emit_xmt_load(step):
        @block.sync
        def _(s, step=step):
            for dk in ("dN0", "dN1", "dN2", "dN3"):
                wt(s, "s", dk, C[dk])
            if step > 0:
                wt(s, "s", "P", chunk_mm[step * 24 - 1])
            dma(s, "dM", xmt[:, :, :], d_xmid16[step].transpose([1, 0, 2]))
        return C["dM"]

    def emit_gate_part(ci, step, mo, gg_i, part, xmt_ld, extraV=0):
        buf = ci % 2
        bank = 4 + (ci % 2)
        func = (AF.Sigmoid, AF.Sigmoid, AF.Tanh)[gg_i]

        @block.tensor
        def _(t, ci=ci, buf=buf, bank=bank, xmt_ld=xmt_ld, step=step,
              part=part, extraV=extraV, cw=cupd_last[step - 1]):
            if part == 0:
                wt(t, "t", chunk_ld[ci][0], chunk_ld[ci][1])
                wt(t, "t", "dM", xmt_ld)
                if extraV:
                    wt(t, "t", "V", extraV)
                if ci - 2 in chunk_act:
                    wt(t, "t", "A", chunk_act[ci - 2])
            elif part == 2:
                wt(t, "t", "V", cw)
            w_u = part // 2
            src = xmt if w_u == 0 else None
            for j in range(4):
                ic = (part % 2) * 4 + j
                rhs = xmt[:, ic, :] if w_u == 0 else cT16[:, step % 2, ic, :]
                t.matmul(PS[:, bank * 512:(bank + 1) * 512],
                         fwc[:, buf, w_u, ic, :], rhs,
                         start=(part == 0 and j == 0),
                         stop=(part == 3 and j == 3)).then_maybe_inc(
                    (SEMS["P"], 1) if (part == 3 and j == 3) else None)
            pass
            if part == 3:
                C["P"] += 1
        if part == 3:
            chunk_mm[ci] = C["P"]
            emit_fwc_load(ci + 2)

            @defer_scalar
            def _(a, bank=bank, gg_i=gg_i, mo=mo, func=func, gmm=chunk_mm[ci],
                  pcu=(ffn_state["last_cupd"] if gg_i == 0 else 0)):
                wt(a, "a", "P", gmm)
                if pcu:
                    wt(a, "a", "V", pcu)
                dstb = (gf, gi, gg)[gg_i]
                a.activation(dstb[:, :], PS[:, bank * 512:(bank + 1) * 512],
                             func, bias=bffs[:, gg_i * 8 + mo:gg_i * 8 + mo + 1]
                             ).then_inc(SEMS["A"], 1)
            C["A"] += 1
            chunk_act[ci] = C["A"]

    FFN_TILES = [3, 5, 8, 10, 13, 15, 18, 20, 23, 25, 28, 30]
    FFN_TILES_P0 = [8, 10, 12, 14, 17, 19, 22, 24, 26, 28, 29, 30]

    def emit_gate_unit(ci, step, mo, gg_i, xmt_ld, extraV=0):
        for part in range(4):
            emit_gate_part(ci, step, mo, gg_i, part, xmt_ld,
                           extraV=(extraV if part == 0 else 0))

    def emit_cupdate(step, mo):
        nxt = (step + 1) % 2

        @block.vector
        def _(v, mo=mo, nxt=nxt, gact=chunk_act[step * 24 + mo * 3 + 2]):
            wt(v, "v", "A", gact)
            v.tensor_mul(tmpa[:, :], gf[:, :], cT32[:, mo, :])
            v.tensor_mul(tmpb[:, :], gi[:, :], gg[:, :])
            v.tensor_add(cT32[:, mo, :], tmpa[:, :], tmpb[:, :])
            v.tensor_copy(cT16[:, nxt, mo, :],
                          cT32[:, mo, :]).then_inc(SEMS["V"], 1)
            C["V"] += 1
        ffn_state["last_cupd"] = C["V"]

    xmt_lds = {}

    def kv_prefetch(s, b, pair, buf):
        key = "dV" + str(buf)
        for c in range(NC):
            dma(s, key, kvK[:, buf, 4 * c:4 * (c + 1), :], d_kvfull[b, c, pair, 0])
            dma(s, key, kvV[:, buf, 4 * c:4 * (c + 1), :], d_kvfull[b, c, pair, 1])
        return (key, C[key])

    kvready = {}

    @block.sync
    def _(s):
        wt(s, "s", "CC", CC_KV[0])
        kvready[(0, 0)] = kv_prefetch(s, 0, 0, 0)

    prev_epilogue_V = 0
    prev_pvden_P = 0
    wo_done_P = 0
    attn_done_V = {}
    wo_radds = {}
    wo_nrm_V = {}
    exp_slot_wait = {0: 0, 1: 0}

    for b in range(B):
        for pair in range(8):
            bp = b * 8 + pair
            buf = bp % 2

            @block.sync
            def _(s, bp=bp):
                if bp + 1 < 32:
                    nb, npair = divmod(bp + 1, 8)
                    if bp >= 1:
                        wt(s, "s", "P", prev_pvden_P)
                    wt(s, "s", "CC", CC_KV[nb])
                    kvready[(nb, npair)] = kv_prefetch(s, nb, npair, (bp + 1) % 2)

            if b >= 1 and pair == 0:
                xmt_lds[b - 1] = emit_xmt_load(b - 1)

            P_sT = {}
            A_exp = {}
            V_mask = {}

            def emit_dummies(epiV=prev_epilogue_V):
                @block.tensor
                def _(t, epiV=epiV):
                    wt(t, "t", "V", epiV)
                    t.matmul(PO[:, :], on_row16[0:1, 0:128], zrow[0:1, :],
                             start=True, stop=True)
                    t.matmul(PD[:, :], on_row16[0:1, 0:128], zrow[0:1, :],
                             start=True, stop=True)

            def emit_pv(tt, buf=buf):
                slot = tt % 3
                n0 = 32 * (tt // 2)
                N = 512 - n0

                @block.tensor
                def _(t, buf=buf, tt=tt, slot=slot, n0=n0, N=N, vm=V_mask[tt]):
                    wt(t, "t", "V", vm)
                    t.matmul(PO[0:64, n0:512], kvV[:, buf, tt, 0:64],
                             pt[:, slot, 0:N], start=False, stop=False,
                             tile_position=(0, 0), skip_group_check=True)
                    t.matmul(PO[64:128, n0:512], kvV[:, buf, tt, 64:128],
                             pt[:, slot, 512:512 + N], start=False, stop=False,
                             tile_position=(0, 64), skip_group_check=True)
                    t.matmul(PD[0:64, n0:512], on_mat[:, :],
                             pt[:, slot, 0:N], start=False, stop=False,
                             tile_position=(0, 0), skip_group_check=True)
                    t.matmul(PD[64:128, n0:512], on_mat[:, :],
                             pt[:, slot, 512:512 + N], start=False, stop=False,
                             tile_position=(0, 64), skip_group_check=True
                             ).then_inc(SEMS["P"], 1)
                C["P"] += 1

            for tt in range(32):
                slot = tt % 2
                n0 = 32 * (tt // 2)
                N = 512 - n0
                if tt == 0:
                    emit_dummies()
                if tt >= 3:
                    emit_pv(tt - 3)
                ftl = FFN_TILES_P0 if pair == 0 else FFN_TILES
                if b >= 1 and tt in ftl:
                    idx = ftl.index(tt)
                    g = idx // 4
                    part = idx % 4
                    ci = (b - 1) * 24 + pair * 3 + g
                    emit_gate_part(ci, b - 1, pair, g, part, xmt_lds[b - 1],
                                   extraV=(wo_nrm_V[b - 1]
                                           if (pair == 0 and g == 0 and part == 0)
                                           else 0))

                @block.tensor
                def _(t, b=b, pair=pair, buf=buf, tt=tt, slot=slot, n0=n0, N=N,
                      ew=exp_slot_wait[slot],
                      wonrm=(wo_radds[b - 1][7] if (pair == 0 and b > 0) else None)):
                    if tt == 0:
                        wt(t, "t", kvready[(b, pair)][0], kvready[(b, pair)][1])
                        wt(t, "t", "V", BIAS_B[b])
                        if wonrm is not None:
                            wt(t, "t", "V", wonrm)
                    if ew:
                        wt(t, "t", "A", ew)
                    qfe = qhat[0:64, b, pair, :, :].rearrange(
                        "p a c -> p (a c)")[:, n0:512]
                    qfo = qhat[64:128, b, pair, :, :].rearrange(
                        "p a c -> p (a c)")[:, n0:512]
                    t.matmul(PS[:, slot * 1024:slot * 1024 + N],
                             kvK[0:64, buf, tt, :], qfe,
                             start=True, stop=True)
                    t.matmul(PS[:, slot * 1024 + 512:slot * 1024 + 512 + N],
                             kvK[64:128, buf, tt, :], qfo,
                             start=True, stop=True,
                             tile_position=(64, 0)).then_inc(SEMS["P"], 1)
                C["P"] += 1
                P_sT[tt] = C["P"]

                @defer_scalar
                def _(a, b=b, pair=pair, slot=slot, pslot=tt % 3, N=N, stp=P_sT[tt]):
                    wt(a, "a", "P", stp)
                    inap = PS[:, slot * 1024:(slot + 1) * 1024].rearrange(
                        "p (two n) -> p two n", two=2)[:, :, 0:N]
                    outap = pt[:, pslot, :].rearrange(
                        "p (two n) -> p two n", two=2)[:, :, 0:N]
                    a.activation(outap, inap, AF.Exp,
                                 bias=biasB[:, b * 8 + pair:b * 8 + pair + 1]
                                 ).then_inc(SEMS["A"], 1)
                C["A"] += 1
                A_exp[tt] = C["A"]
                exp_slot_wait[slot] = C["A"]

                @block.vector
                def _(v, tt=tt, pslot=tt % 3, N=N, ae=A_exp[tt]):
                    if tt == 0:
                        wt(v, "v", "dW", W_DONE)
                    wt(v, "v", "A", ae)
                    pap = bass.AP(pt.handle if hasattr(pt, "handle") else pt,
                                  pslot * 1024, [[3072, 128], [512, 2], [1, 32]])
                    v.tensor_mul(pap, pap, masks[:, tt % 2, :, :]).then_inc(
                        SEMS["V"], 1)
                C["V"] += 1
                V_mask[tt] = C["V"]

            emit_pv(29)
            emit_pv(30)
            emit_pv(31)
            prev_pvden_P = C["P"]

            @defer_scalar
            def _(a, pv=prev_pvden_P):
                wt(a, "a", "P", pv)
                a.activation(rsq[:, :], PD[:, :], AF.Ln, bias=cvals[:, 1:2])
                a.activation(rcp[:, :], rsq[:, :], AF.Exp, scale=-1.0).then_inc(SEMS["A"], 1)
            C["A"] += 1
            epi_a = C["A"]

            @block.vector
            def _(v, pair=pair, epi_a=epi_a, b=b, wod=wo_done_P):
                wt(v, "v", "A", epi_a)
                if b > 0:
                    wt(v, "v", "P", wod)
                v.tensor_mul(attnT[:, pair, :], PO[:, :], rcp[:, :]).then_inc(SEMS["V"], 1)
                C["V"] += 1
            prev_epilogue_V = C["V"]

            if b >= 1:
                emit_cupdate(b - 1, pair)
                if pair == 7:
                    cupd_last[b - 1] = C["V"]

        attn_done_V[b] = prev_epilogue_V

        @block.sync
        def _(s, b=b):
            if b > 0:
                wt(s, "s", "V", wo_radds[b - 1][6])
                wt(s, "s", "V", wo_radds[b - 1][7])
            dma(s, "dR0", xrc[:, 0, :], d_xq[b, 0:128, :])
            dma(s, "dR1", xrc[:, 1, :], d_xq[b, 128:256, :])
        xrc_loads = {0: ("dR0", C["dR0"]), 1: ("dR1", C["dR1"])}

        wo_units = []
        radds = {}
        for oc in range(8):
            bank = oc % 2

            @block.tensor
            def _(t, b=b, oc=oc, bank=bank):
                wt(t, "t", "V", attn_done_V[b])
                if oc >= 2:
                    wt(t, "t", "V", wo_units[oc - 2]["radd"])
                for cc in range(8):
                    t.matmul(PS[:, bank * 512:(bank + 1) * 512],
                             w_o[:, cc, oc * 128:(oc + 1) * 128],
                             attnT[:, cc, :],
                             start=(cc == 0), stop=(cc == 7)).then_maybe_inc(
                        (SEMS["P"], 1) if cc == 7 else None)
                pass
                C["P"] += 1
            womm = C["P"]

            @block.vector
            def _(v, oc=oc, bank=bank, womm=womm, xl=xrc_loads[oc]):
                wt(v, "v", "P", womm)
                wt(v, "v", xl[0], xl[1])
                v.tensor_add(rbuf[:, oc, :], PS[:, bank * 512:(bank + 1) * 512],
                             xrc[:, oc % 2, :]).then_inc(SEMS["V"], 1)
                C["V"] += 1
            radd = C["V"]
            radds[oc] = radd

            @block.sync
            def _(s, b=b, oc=oc, radd=radd):
                if oc + 2 < 8:
                    wt(s, "s", "V", radd)
                    key = "dR" + str(oc % 2)
                    dma(s, key, xrc[:, oc % 2, :], d_xq[b, (oc + 2) * 128:(oc + 3) * 128, :])
                    xrc_loads[oc + 2] = (key, C[key])

            @block.vector
            def _(v, oc=oc):
                v.tensor_mul(rsq[:, :], rbuf[:, oc, :], rbuf[:, oc, :]).then_inc(SEMS["V"], 1)
                C["V"] += 1
            rsq_a = C["V"]

            @block.tensor
            def _(t, oc=oc, radd=radd, rsq_a=rsq_a):
                wt(t, "t", "V", rsq_a)
                t.matmul(PD[0:1, 0:512], on_col[:, 0:1], rbuf[:, oc, :],
                         start=(oc == 0), stop=(oc == 7))
                t.matmul(PO[0:1, 0:512], on_col[:, 0:1], rsq[:, :],
                         start=(oc == 0), stop=(oc == 7)).then_inc(SEMS["P"], 1)
                C["P"] += 1
            wo_units.append(dict(radd=radd, stats=C["P"]))

        wo_radds[b] = radds
        ln_stats = wo_units[-1]["stats"]

        @block.vector
        def _(v, ln_stats=ln_stats):
            wt(v, "v", "P", ln_stats)
            v.tensor_scalar(mu_s, PD[0:1, 0:512], 1.0 / D, None, ALU.mult)
            v.tensor_scalar(var_s, PO[0:1, 0:512], 1.0 / D, None, ALU.mult)
            v.tensor_mul(m2_s, mu_s, mu_s)
            v.tensor_sub(var_s, var_s, m2_s).then_inc(SEMS["V"], 1)
            C["V"] += 1
        ln_v = C["V"]

        @defer_scalar
        def _(a, ln_v=ln_v):
            wt(a, "a", "V", ln_v)
            a.activation(sd_s, var_s, AF.Ln, bias=cvals[0:1, 0:1])
            a.activation(var_s, sd_s, AF.Exp, scale=-0.5).then_inc(SEMS["A"], 1)
        C["A"] += 1
        ln_r = C["A"]

        @block.tensor
        def _(t, ln_r=ln_r, ga=chunk_act.get(b * 24 - 1, 0)):
            wt(t, "t", "A", ln_r)
            if ga:
                wt(t, "t", "A", ga)
            t.matmul(PS[:, 2048:2560], on_row[0:1, 0:128], mu_s, start=True, stop=True)
            t.matmul(PS[:, 2560:3072], on_row[0:1, 0:128], var_s, start=True,
                     stop=True).then_inc(SEMS["P"], 1)
            C["P"] += 1
        ln_bc = C["P"]

        nrm_dmas = {}
        for oc in range(8):
            @block.vector
            def _(v, oc=oc, ln_bc=ln_bc, nrm_dmas=nrm_dmas):
                wt(v, "v", "P", ln_bc)
                if oc >= 4:
                    wt(v, "v", nrm_dmas[oc - 4][0], nrm_dmas[oc - 4][1])
                v.tensor_sub(tmpa[:, :], rbuf[:, oc, :], PS[:, 2048:2560])
                v.tensor_mul(nrm[:, oc % 4, :], tmpa[:, :],
                             PS[:, 2560:3072]).then_inc(SEMS["V"], 1)
                C["V"] += 1
            nrm_v = C["V"]

            @block.gpsimd
            def _(g, b=b, oc=oc, nrm_v=nrm_v, nrm_dmas=nrm_dmas):
                wt(g, "g", "V", nrm_v)
                key = "dN" + str(oc % 4)
                dma(g, key, d_xmid32[b, oc], nrm[:, oc % 4, :])
                dma(g, key, d_xmid16[b, oc], nrm[:, oc % 4, :])
                nrm_dmas[oc] = (key, C[key])
        wo_nrm_V[b] = C["V"]
        wo_done_P = ln_bc

    xmt_lds[3] = emit_xmt_load(3)
    for mo in range(8):
        for g in range(3):
            ci = 3 * 24 + mo * 3 + g
            emit_gate_unit(ci, 3, mo, g, xmt_lds[3],
                           extraV=(wo_nrm_V[3] if (mo == 0 and g == 0) else 0))
        emit_cupdate(3, mo)
    cupd_last[3] = C["V"]

    CFIN = B % 2
    ffn_c_done = C["V"]
    CI = 96

    h_radds = {}
    nrm_dmas5 = {}
    for tt in range(B):
        @block.sync
        def _(s, tt=tt, lastmm=chunk_mm[CI - 1],
              dn=tuple((dk, C[dk]) for dk in ("dN0", "dN1", "dN2", "dN3"))):
            for dk, dv in dn:
                wt(s, dk, dk, dv) if False else wt(s, "s", dk, dv)
            wt(s, "s", "P", lastmm)
            dma(s, "dM", xmt[:, :, :], d_xmid16[tt].transpose([1, 0, 2]))
        hx_load = C["dM"]

        hu = []
        for oc in range(8):
            ci = CI
            CI += 1
            buf = ci % 2

            @block.tensor
            def _(t, oc=oc, ci=ci, buf=buf, hx_load=hx_load, ffn_c_done=ffn_c_done):
                wt(t, "t", chunk_ld[ci][0], chunk_ld[ci][1])
                wt(t, "t", "dM", hx_load)
                wt(t, "t", "V", ffn_c_done)
                bank = oc % 2
                if oc >= 2:
                    wt(t, "t", "V", hu[oc - 2]["radd"])
                for ic in range(8):
                    t.matmul(PS[:, bank * 512:(bank + 1) * 512],
                             fwc[:, buf, 0, ic, :], xmt[:, ic, :],
                             start=(ic == 0), stop=False)
                for ic in range(8):
                    t.matmul(PS[:, bank * 512:(bank + 1) * 512],
                             fwc[:, buf, 1, ic, :], cT16[:, CFIN, ic, :],
                             start=False, stop=(ic == 7)).then_maybe_inc(
                        (SEMS["P"], 1) if ic == 7 else None)
                pass
                C["P"] += 1
            hmm_ = C["P"]
            chunk_mm[ci] = hmm_
            emit_fwc_load(ci + 2)

            @defer_scalar
            def _(a, oc=oc, hmm_=hmm_):
                wt(a, "a", "P", hmm_)
                a.activation(hc[:, oc % 2, :], PS[:, (oc % 2) * 512:(oc % 2 + 1) * 512],
                             AF.Tanh, bias=bffs[:, 24 + oc:25 + oc]).then_inc(SEMS["A"], 1)
            C["A"] += 1
            hact = C["A"]

            @block.sync
            def _(s, tt=tt, oc=oc):
                key = "dO" + str(oc % 2)
                tenant = h_radds.get((tt, oc - 2)) if oc >= 2 else \
                    (h_radds.get((tt - 1, oc + 6)) if tt > 0 else None)
                if tenant is not None:
                    wt(s, "s", "V", tenant)
                dma(s, key, xm32c[:, oc % 2, :], d_xmid32[tt, oc])
            xm_ld = (("dO" + str(oc % 2)), C["dO" + str(oc % 2)])

            @block.vector
            def _(v, oc=oc, hact=hact, xm_ld=xm_ld):
                wt(v, "v", "A", hact)
                wt(v, "v", xm_ld[0], xm_ld[1])
                v.tensor_add(rbuf[:, oc, :], hc[:, oc % 2, :],
                             xm32c[:, oc % 2, :]).then_inc(SEMS["V"], 1)
                C["V"] += 1
            radd = C["V"]
            h_radds[(tt, oc)] = radd

            @block.vector
            def _(v, oc=oc):
                v.tensor_mul(rsq[:, :], rbuf[:, oc, :], rbuf[:, oc, :]).then_inc(SEMS["V"], 1)
                C["V"] += 1
            rsq_a = C["V"]

            @block.tensor
            def _(t, oc=oc, radd=radd, rsq_a=rsq_a):
                wt(t, "t", "V", rsq_a)
                t.matmul(PD[0:1, 0:512], on_col[:, 0:1], rbuf[:, oc, :],
                         start=(oc == 0), stop=(oc == 7))
                t.matmul(PO[0:1, 0:512], on_col[:, 0:1], rsq[:, :],
                         start=(oc == 0), stop=(oc == 7)).then_inc(SEMS["P"], 1)
                C["P"] += 1
            hu.append(dict(radd=radd, stats=C["P"]))

        @block.vector
        def _(v, hu=hu):
            wt(v, "v", "P", hu[-1]["stats"])
            v.tensor_scalar(mu_s, PD[0:1, 0:512], 1.0 / D, None, ALU.mult)
            v.tensor_scalar(var_s, PO[0:1, 0:512], 1.0 / D, None, ALU.mult)
            v.tensor_mul(m2_s, mu_s, mu_s)
            v.tensor_sub(var_s, var_s, m2_s).then_inc(SEMS["V"], 1)
            C["V"] += 1
        ln_v = C["V"]

        @defer_scalar
        def _(a, ln_v=ln_v):
            wt(a, "a", "V", ln_v)
            a.activation(sd_s, var_s, AF.Ln, bias=cvals[0:1, 0:1])
            a.activation(var_s, sd_s, AF.Exp, scale=-0.5).then_inc(SEMS["A"], 1)
        C["A"] += 1
        ln_r = C["A"]

        @block.tensor
        def _(t, ln_r=ln_r):
            wt(t, "t", "A", ln_r)
            t.matmul(PS[:, 2048:2560], on_row[0:1, 0:128], mu_s, start=True, stop=True)
            t.matmul(PS[:, 2560:3072], on_row[0:1, 0:128], var_s, start=True,
                     stop=True).then_inc(SEMS["P"], 1)
            C["P"] += 1
        ln_bc = C["P"]

        for oc in range(8):
            @block.vector
            def _(v, tt=tt, oc=oc, ln_bc=ln_bc):
                wt(v, "v", "P", ln_bc)
                tenant = nrm_dmas5.get((tt, oc - 4)) if oc >= 4 else \
                    (nrm_dmas5.get((tt - 1, oc + 4)) if tt > 0 else None)
                if tenant is not None:
                    wt(v, "v", tenant[0], tenant[1])
                v.tensor_sub(tmpa[:, :], rbuf[:, oc, :], PS[:, 2048:2560])
                v.tensor_mul(nrm[:, oc % 4, :], tmpa[:, :],
                             PS[:, 2560:3072]).then_inc(SEMS["V"], 1)
                C["V"] += 1
            nrm_v = C["V"]

            @block.gpsimd
            def _(g, tt=tt, oc=oc, nrm_v=nrm_v):
                wt(g, "g", "V", nrm_v)
                key = "dN" + str(oc % 4)
                dma(g, key, d_yT[tt, oc], nrm[:, oc % 4, :])
                nrm_dmas5[(tt, oc)] = (key, C[key])

    @block.gpsimd
    def _(g):
        for dk in ("dN0", "dN1", "dN2", "dN3"):
            wt(g, "g", dk, C[dk])

    @block.scalar
    def _(a):
        for fn in SCALAR_OPS:
            fn(a)

    ctx.close()
    return nc


def _host_prep(inputs):
    f32 = np.float32
    x = np.asarray(inputs["L0_attn_x"], f32)
    xT = np.ascontiguousarray(x.transpose(0, 2, 1))

    half = 32
    inv = (1.0 / (np.float32(10000.0) ** (np.arange(half, dtype=f32) / np.float32(half)))).astype(f32)

    def tables(pos, scale):
        ang = (pos.astype(f32)[:, None] * inv[None, :]).astype(f32)
        cos = (np.cos(ang).astype(f32) * f32(scale)).astype(f32)
        sin = (np.sin(ang).astype(f32) * f32(scale)).astype(f32)
        out = np.zeros((4, 128, 1024), f32)
        for s in range(4):
            out[s, :, 0:512] = np.tile(cos[s * 128:(s + 1) * 128], (1, 16))
            out[s, :, 512:1024] = np.tile(sin[s * 128:(s + 1) * 128], (1, 16))
        return out

    wq = np.asarray(inputs["L0_attn_W_q"], f32)
    wdkv = np.asarray(inputs["L0_attn_W_dkv"], f32)
    wuk = np.asarray(inputs["L0_attn_W_uk"], f32)
    wuv = np.asarray(inputs["L0_attn_W_uv"], f32)
    wo = np.asarray(inputs["L0_attn_W_o"], f32)
    gam = np.asarray(inputs["L0_attn_ln_gamma"], f32)
    bet = np.asarray(inputs["L0_attn_ln_beta"], f32)
    fwT = np.ascontiguousarray(np.stack([
        np.asarray(inputs["L0_ffn_wf"], f32).T, np.asarray(inputs["L0_ffn_uf"], f32).T,
        np.asarray(inputs["L0_ffn_wi"], f32).T, np.asarray(inputs["L0_ffn_ui"], f32).T,
        np.asarray(inputs["L0_ffn_wc"], f32).T, np.asarray(inputs["L0_ffn_uc"], f32).T,
        np.asarray(inputs["L0_ffn_wo"], f32).T, np.asarray(inputs["L0_ffn_uo"], f32).T,
    ]))
    bias4 = np.stack([np.asarray(inputs["L0_ffn_" + k], f32) for k in ("bf", "bi", "bc", "bo")])
    bff = np.ascontiguousarray(bias4.reshape(4, 8, 128).transpose(2, 0, 1).reshape(128, 32))
    incol = np.ascontiguousarray(np.asarray(inputs["L0_ffn_in_c"], f32).reshape(8, 128).T)
    gcol = np.ascontiguousarray(gam.reshape(2, 128).T)
    bcol = np.ascontiguousarray(bet.reshape(2, 128).T)

    shared = dict(wq=wq, wdkv=wdkv, wuk=wuk, wuv=wuv, wo=wo, fwT=fwT,
                  bff=bff, incol=incol, gcol=gcol, bcol=bcol,
                  idt=np.eye(128, dtype=f32))

    in_maps = []
    for i in range(NC):
        xq_i = np.ascontiguousarray(xT[:, :, i::8])
        xkv_i = np.ascontiguousarray(xT[:, :, 512 * i:512 * (i + 1)])
        pos_q = np.arange(i, T, 8)
        pos_kv = np.arange(512 * i, 512 * (i + 1))
        k_loc = np.arange(128)
        j_loc = np.arange(16)
        mm_ = (k_loc[:, None] <= (8 * j_loc[None, :] + i))
        maskb = np.zeros((128, 2, 2, 32), np.float32)
        maskb[:, 0, :, 0:16] = mm_[:, None, :]
        maskb[:, 0, :, 16:32] = 1.0
        maskb[:, 1, :, 0:16] = 0.0
        maskb[:, 1, :, 16:32] = mm_[:, None, :]
        m = dict(shared)
        m["xq"] = xq_i
        m["xkv"] = xkv_i
        m["ropek"] = tables(pos_kv, 1.0)
        m["ropeq"] = tables(pos_q, 0.125)
        m["maskb"] = np.ascontiguousarray(maskb.reshape(128, 128)).astype(ml_dtypes.bfloat16)
        in_maps.append(m)
    return in_maps


def kernel(**inputs):
    in_maps = _host_prep(inputs)
    nc = build_program()
    res = run_bass_kernel_spmd(nc, in_maps, core_ids=list(range(NC)))
    outT = np.empty((B, D, T), np.float32)
    for i in range(NC):
        outT[:, :, i::8] = res.results[i]["yT"].reshape(B, D, 512)
    return np.ascontiguousarray(outT.transpose(0, 2, 1))


# version 23
# speedup vs baseline: 1.0698x; 1.0014x over previous
import numpy as np
import ml_dtypes
from contextlib import ExitStack

import concourse.bass as bass
import concourse.mybir as mybir
from concourse.bass_utils import run_bass_kernel_spmd

F32 = mybir.dt.float32
BF16 = mybir.dt.bfloat16
AF = mybir.ActivationFunctionType
ALU = mybir.AluOpType

B = 4
T = 4096
D = 1024
LT = 256
NC = 8
EPS = 1e-5


def build_program():
    nc = bass.Bass("TRN2", target_bir_lowering=False, debug=False,
                   detect_race_conditions=False)

    d_xkv = nc.dram_tensor("xkv", [B, D, 512], F32, kind="ExternalInput").ap()
    d_xq = nc.dram_tensor("xq", [B, D, 512], F32, kind="ExternalInput").ap()
    d_wq = nc.dram_tensor("wq", [D, D], F32, kind="ExternalInput").ap()
    d_wdkv = nc.dram_tensor("wdkv", [D, LT], F32, kind="ExternalInput").ap()
    d_wuk = nc.dram_tensor("wuk", [LT, D], F32, kind="ExternalInput").ap()
    d_wuv = nc.dram_tensor("wuv", [LT, D], F32, kind="ExternalInput").ap()
    d_wo = nc.dram_tensor("wo", [D, D], F32, kind="ExternalInput").ap()
    d_fwT = nc.dram_tensor("fwT", [8, D, D], F32, kind="ExternalInput").ap()
    d_bff = nc.dram_tensor("bff", [128, 32], F32, kind="ExternalInput").ap()
    d_incol = nc.dram_tensor("incol", [128, 8], F32, kind="ExternalInput").ap()
    d_gcol = nc.dram_tensor("gcol", [128, 2], F32, kind="ExternalInput").ap()
    d_bcol = nc.dram_tensor("bcol", [128, 2], F32, kind="ExternalInput").ap()
    d_ropek = nc.dram_tensor("ropek", [4, 128, 1024], F32, kind="ExternalInput").ap()
    d_ropeq = nc.dram_tensor("ropeq", [4, 128, 1024], F32, kind="ExternalInput").ap()
    d_maskb = nc.dram_tensor("maskb", [128, 128], BF16, kind="ExternalInput").ap()
    d_idt = nc.dram_tensor("idt", [128, 128], F32, kind="ExternalInput").ap()

    d_kvsh = nc.dram_tensor("kvsh", [B, 8, 2, 128, 4, 128], BF16).ap()
    d_kvfull = nc.dram_tensor("kvfull", [B, NC, 8, 2, 128, 4, 128], BF16,
                              addr_space="Shared").ap()
    d_stsh = nc.dram_tensor("stsh", [B, 128], F32).ap()
    d_stfull = nc.dram_tensor("stfull", [B, NC, 128], F32).ap()
    d_fcache = nc.dram_tensor("fcache", [8, 8, 128, 8, 128], BF16).ap()
    d_xmid32 = nc.dram_tensor("xmid32", [B, 8, 128, 512], F32).ap()
    d_xmid16 = nc.dram_tensor("xmid16", [B, 8, 128, 512], BF16).ap()
    d_yT = nc.dram_tensor("yT", [B, 8, 128, 512], F32, kind="ExternalOutput").ap()

    ctx = ExitStack()
    sb = lambda name, shape, dt: ctx.enter_context(nc.sbuf_tensor(name, shape, dt))
    ps = lambda name, shape: ctx.enter_context(nc.psum_tensor(name, shape, F32))

    on_mat = sb("on_mat", [128, 64], BF16)
    on_row = sb("on_row", [1, 128], F32)
    on_col = sb("on_col", [128, 1], F32)
    w_o = sb("w_o", [128, 8, 1024], BF16)
    bffs = sb("bffs", [128, 32], F32)
    incs = sb("incs", [128, 8], F32)
    gcs = sb("gcs", [128, 2], F32)
    bcs = sb("bcs", [128, 2], F32)
    masks = sb("masks", [128, 2, 2, 32], BF16)
    biasB = sb("biasB", [128, 32], F32)
    Mq = sb("Mq", [128, 32], F32)
    Mk = sb("Mk", [128, 4], F32)
    MqR = sb("MqR", [128, 32], F32)
    MkR = sb("MkR", [128, 4], F32)
    sqMq = sb("sqMq", [128, 32], F32)
    kmb = sb("kmb", [128, 4], F32)
    strd = sb("strd", [1, 64], F32)
    kmaxg = sb("kmaxg", [1, 8], F32)
    mrow = sb("mrow", [1, 40], F32)
    idt = sb("idt_s", [128, 128], F32)
    lnscr = sb("lnscr", [1, 1024], F32)
    tmpa = sb("tmpa", [128, 512], F32)
    tmpb = sb("tmpb", [128, 512], F32)
    cvals = sb("cvals", [128, 2], F32)
    zrow = sb("zrow", [1, 512], BF16)
    on_row16 = sb("on_row16", [1, 128], BF16)
    qhat = sb("qhat", [128, B, 8, 4, 128], BF16)

    bctx = ExitStack()
    bsb = lambda name, shape, dt: bctx.enter_context(nc.sbuf_tensor(name, shape, dt))
    w_q = bsb("w_q", [128, 8, 1024], BF16)
    w_dkv = bsb("w_dkv", [128, 8, 256], BF16)
    w_uk = bsb("w_uk", [128, 2, 1024], BF16)
    w_uv = bsb("w_uv", [128, 2, 1024], BF16)
    xkv_s = bsb("xkv_s", [128, 2, 8, 512], BF16)
    xq_s = bsb("xq_s", [128, 2, 8, 512], BF16)
    ltf = bsb("ltf", [128, 2, 512], F32)
    ltsq = bsb("ltsq", [128, 2, 512], F32)
    latn = bsb("latn", [128, 2, 2, 512], BF16)
    ropbk = bsb("ropbk", [128, 4, 1024], F32)
    ropbq = bsb("ropbq", [128, 4, 1024], F32)
    krop = bsb("krop", [128, 4, 1024], BF16)
    qrop = bsb("qrop", [128, 4, 1024], BF16)
    kts = bsb("kts", [128, 4, 1024], BF16)
    vst = bsb("vst", [128, 4, 1024], BF16)
    sqk = bsb("sqk", [128, 1024], F32)
    sqq = bsb("sqq", [128, 1024], F32)
    n16 = bsb("n16", [128, 16], F32)
    n8 = bsb("n8", [128, 8], F32)
    ncol = bsb("ncol", [128, 4], F32)

    PS = ps("PS", [128, 3072])
    PO = ps("PO", [128, 512])
    PD = ps("PD", [128, 512])

    SEMNAMES = [
        "dW",
        "dWa",
        "dX0", "dX1",
        "dC",
        "dT0", "dT1",
        "dK0", "dK1", "dK2", "dK3",
        "dQ0", "dQ1", "dQ2", "dQ3",
        "dH0", "dH1", "dH2", "dH3",
        "dSS",
        "dV0", "dV1",
        "dR0", "dR1",
        "dN0", "dN1", "dN2", "dN3",
        "dM",
        "dO0", "dO1",
        "dF0", "dF1",
        "P", "A", "V", "G", "CC",
    ]
    SEMS = {}
    for nm in SEMNAMES:
        SEMS[nm] = ctx.enter_context(nc.semaphore("s_" + nm))
    C = {nm: 0 for nm in SEMNAMES}
    lastw = {}

    def wt(eng, ename, key, val):
        if val <= 0:
            return
        if lastw.get((ename, key), -1) >= val:
            return
        eng.wait_ge(SEMS[key], val)
        lastw[(ename, key)] = val

    def dma(eng, key, out, in_, transpose=False):
        if transpose:
            eng.dma_start(out=out, in_=in_, transpose=True).then_inc(SEMS[key], 16)
        else:
            eng.dma_start(out=out, in_=in_).then_inc(SEMS[key], 16)
        C[key] += 16
        return C[key]

    block = ctx.enter_context(nc.Block())

    SCALAR_OPS = []

    def defer_scalar(fn):
        SCALAR_OPS.append(fn)
        return None

    @block.gpsimd
    def _(g):
        g.memset(on_mat[:, :], 1.0)
        g.memset(on_row[:, :], 1.0)
        g.memset(on_col[:, :], 1.0)
        g.memset(Mk[:, :], 0.0)
        g.memset(Mq[:, :], 0.0)
        g.memset(cvals[:, 0:1], EPS)
        g.memset(cvals[:, 1:2], 1e-30)
        g.memset(zrow[:, :], 0.0)
        g.memset(on_row16[:, :], 1.0).then_inc(SEMS["G"], 1)
        C["G"] += 1
        for dst, src in (
            (w_dkv[:, :, :], d_wdkv), (w_q[:, :, :], d_wq),
            (w_uk[:, :, :], d_wuk), (w_uv[:, :, :], d_wuv),
        ):
            dma(g, "dWa", dst, src.rearrange("(ic p) d -> p ic d", p=128))
        dma(g, "dW", w_o[:, :, :], d_wo.rearrange("(ic p) d -> p ic d", p=128))
        dma(g, "dW", masks[:, :, :, :], d_maskb.rearrange("p (par two c) -> p par two c", par=2, c=32))
    WA_DONE = C["dWa"]

    @block.sync
    def _(s):
        dma(s, "dC", idt[:, :], d_idt)
        dma(s, "dC", bffs[:, :], d_bff)
        dma(s, "dC", incs[:, :], d_incol)
        dma(s, "dC", gcs[:, :], d_gcol)
        dma(s, "dC", bcs[:, :], d_bcol)
        dma(s, "dC", ropbk[:, :, :], d_ropek.transpose([1, 0, 2]))
        dma(s, "dC", ropbq[:, :, :], d_ropeq.transpose([1, 0, 2]))
    CONST_DONE = C["dC"]

    @block.gpsimd
    def _(g):
        dma(g, "dX0", xkv_s[:, 0, :, :], d_xkv[0].rearrange("(ic p) t -> p ic t", p=128))
        dma(g, "dX0", xq_s[:, 0, :, :], d_xq[0].rearrange("(ic p) t -> p ic t", p=128))
    XLOAD = {0: ("dX0", C["dX0"])}

    slot_units = {}
    lat_units = {}
    lat_norm = 0
    ph2_V = {}
    SQMQ_B = {}
    CC_STATS = {}
    CC_KV = {}
    BIAS_B = {}

    mu_s = lnscr[0:1, 0:512]
    var_s = lnscr[0:1, 512:1024]
    m2_s = tmpa[0:1, 0:512]
    sd_s = tmpb[0:1, 0:512]

    def emit_lat_chain(b):
        bb = b % 2

        @block.tensor
        def _(t, b=b, bb=bb):
            wt(t, "t", "dWa", WA_DONE)
            wt(t, "t", "G", 1)
            wt(t, "t", XLOAD[b][0], XLOAD[b][1])
            if b > 0:
                wt(t, "t", "V", lat_units[b - 1]["norm_done"])
            for lh in range(2):
                for ic in range(8):
                    t.matmul(PS[:, 1024 + lh * 512:1024 + (lh + 1) * 512],
                             w_dkv[:, ic, lh * 128:(lh + 1) * 128],
                             xkv_s[:, bb, ic, :],
                             start=(ic == 0), stop=(ic == 7)).then_maybe_inc(
                        (SEMS["P"], 1) if (lh == 1 and ic == 7) else None)
            pass
            C["P"] += 1
        lat_mm = C["P"]

        @block.vector
        def _(v, lat_mm=lat_mm):
            wt(v, "v", "P", lat_mm)
            v.tensor_copy(ltf[:, 0, :], PS[:, 1024:1536])
            v.tensor_copy(ltf[:, 1, :], PS[:, 1536:2048]).then_inc(SEMS["V"], 1)
            C["V"] += 1

        @block.vector
        def _(v):
            v.tensor_mul(ltsq[:, 0, :], ltf[:, 0, :], ltf[:, 0, :])
            v.tensor_mul(ltsq[:, 1, :], ltf[:, 1, :],
                         ltf[:, 1, :]).then_inc(SEMS["V"], 1)
            C["V"] += 1
        lat_sq = C["V"]

        @block.tensor
        def _(t, lat_sq=lat_sq):
            wt(t, "t", "V", lat_sq)
            for lh in range(2):
                t.matmul(PS[0:1, 1024:1536], on_col[:, 0:1], ltf[:, lh, :],
                         start=(lh == 0), stop=(lh == 1))
            for lh in range(2):
                t.matmul(PS[0:1, 1536:2048], on_col[:, 0:1], ltsq[:, lh, :],
                         start=(lh == 0), stop=(lh == 1)).then_maybe_inc(
                    (SEMS["P"], 1) if lh == 1 else None)
            pass
            C["P"] += 1
        lat_stats = C["P"]

        @block.vector
        def _(v, lat_stats=lat_stats):
            wt(v, "v", "P", lat_stats)
            v.tensor_scalar(mu_s, PS[0:1, 1024:1536], 1.0 / LT, None, ALU.mult)
            v.tensor_scalar(var_s, PS[0:1, 1536:2048], 1.0 / LT, None, ALU.mult)
            v.tensor_mul(m2_s, mu_s, mu_s)
            v.tensor_sub(var_s, var_s, m2_s).then_inc(SEMS["V"], 1)
            C["V"] += 1
        lat_v1 = C["V"]

        @defer_scalar
        def _(a, lat_v1=lat_v1):
            wt(a, "a", "V", lat_v1)
            a.activation(sd_s, var_s, AF.Ln, bias=cvals[0:1, 0:1])
            a.activation(var_s, sd_s, AF.Exp, scale=-0.5).then_inc(SEMS["A"], 1)
        C["A"] += 1
        lat_rstd = C["A"]

        @block.tensor
        def _(t, lat_rstd=lat_rstd):
            wt(t, "t", "A", lat_rstd)
            t.matmul(PS[:, 1024:1536], on_row[0:1, 0:128], mu_s, start=True, stop=True)
            t.matmul(PS[:, 1536:2048], on_row[0:1, 0:128], var_s, start=True,
                     stop=True).then_inc(SEMS["P"], 1)
            C["P"] += 1
        lat_bc = C["P"]

        @block.vector
        def _(v, bb=bb, lat_bc=lat_bc):
            wt(v, "v", "P", lat_bc)
            wt(v, "v", "dC", CONST_DONE)
            for lh in range(2):
                v.tensor_sub(tmpa[:, :], ltf[:, lh, :], PS[:, 1024:1536])
                v.tensor_mul(tmpb[:, :], tmpa[:, :], PS[:, 1536:2048])
                v.tensor_scalar(latn[:, bb, lh, :], tmpb[:, :],
                                gcs[:, lh:lh + 1], bcs[:, lh:lh + 1],
                                ALU.mult, ALU.add).then_maybe_inc(
                    (SEMS["V"], 1) if lh == 1 else None)
            pass
            C["V"] += 1
        lat_units[b] = {"norm_done": C["V"]}

    for b in range(B):
        bb = b % 2

        if b + 1 < B:
            @block.gpsimd
            def _(g, b=b):
                nb = (b + 1) % 2
                key = "dX" + str(nb)
                if b + 1 >= 2:
                    wt(g, "g", "P", slot_units[(b - 1, 3)]["q_mm"])
                dma(g, key, xkv_s[:, nb, :, :],
                    d_xkv[b + 1].rearrange("(ic p) t -> p ic t", p=128))
                dma(g, key, xq_s[:, nb, :, :],
                    d_xq[b + 1].rearrange("(ic p) t -> p ic t", p=128))
                XLOAD[b + 1] = (key, C[key])

        if b == 0:
            emit_lat_chain(0)
        lat_norm = lat_units[b]["norm_done"]

        for s_ in range(4):
            par = s_
            prev = slot_units.get((b, s_ - 1)) if s_ > 0 else \
                (slot_units.get((b - 1, 3)) if b > 0 else None)
            prev2 = slot_units.get((b - 1, s_)) if b > 0 else None

            @block.tensor
            def _(t, bb=bb, s_=s_, prev=prev, lat_norm=lat_norm):
                wt(t, "t", "V", lat_norm)
                if prev is not None:
                    wt(t, "t", "V", prev["vevac_done"])
                for dh in range(2):
                    for lh in range(2):
                        t.matmul(PS[:, dh * 512:(dh + 1) * 512],
                                 latn[:, bb, lh, s_ * 128:s_ * 128 + 128],
                                 w_uk[:, lh, dh * 512:(dh + 1) * 512],
                                 start=(lh == 0), stop=(lh == 1))
                for dh in range(2):
                    dst = PO if dh == 0 else PD
                    for lh in range(2):
                        t.matmul(dst[:, :],
                                 latn[:, bb, lh, s_ * 128:s_ * 128 + 128],
                                 w_uv[:, lh, dh * 512:(dh + 1) * 512],
                                 start=(lh == 0), stop=(lh == 1)).then_maybe_inc(
                            (SEMS["P"], 1) if (dh == 1 and lh == 1) else None)
                pass
                C["P"] += 1
            kv_mm = C["P"]

            @block.tensor
            def _(t, b=b, bb=bb, s_=s_, prev=prev):
                if prev is not None:
                    wt(t, "t", "V", prev["ropeq_done"])
                if s_ == 0 and b > 0:
                    wt(t, "t", "V", ph2_V[b - 1])
                for dh in range(2):
                    for ic in range(8):
                        t.matmul(PS[:, 2048 + dh * 512:2048 + (dh + 1) * 512],
                                 xq_s[:, bb, ic, s_ * 128:s_ * 128 + 128],
                                 w_q[:, ic, dh * 512:(dh + 1) * 512],
                                 start=(ic == 0), stop=(ic == 7)).then_maybe_inc(
                            (SEMS["P"], 1) if (dh == 1 and ic == 7) else None)
                pass
                C["P"] += 1
            q_mm = C["P"]

            def rope_emit(v, psoff, tblbuf, tbl, dst, dstbuf):
                ap = PS[:, psoff:psoff + 1024].rearrange(
                    "p (h two q) -> p h two q", h=16, two=2)
                x1 = ap[:, :, 0, :]
                x2 = ap[:, :, 1, :]
                cos = tbl[:, tblbuf, 0:512].rearrange("p (h q) -> p h q", h=16)
                sin = tbl[:, tblbuf, 512:1024].rearrange("p (h q) -> p h q", h=16)
                ta = tmpa[:, :].rearrange("p (h q) -> p h q", h=16)
                tb = tmpb[:, :].rearrange("p (h q) -> p h q", h=16)
                dap = dst[:, dstbuf, :].rearrange("p (h two q) -> p h two q", h=16, two=2)
                v.tensor_mul(ta, x1, cos)
                v.tensor_mul(tb, x2, sin)
                v.tensor_sub(dap[:, :, 0, :], ta, tb)
                v.tensor_mul(ta, x1, sin)
                v.tensor_mul(tb, x2, cos)
                return v.tensor_add(dap[:, :, 1, :], ta, tb)

            @block.vector
            def _(v, s_=s_, par=par, kv_mm=kv_mm, prev2=prev2):
                wt(v, "v", "P", kv_mm)
                wt(v, "v", "dC", CONST_DONE)
                if prev2 is not None:
                    wt(v, "v", "dK" + str(par), prev2["ktr_done"])
                rope_emit(v, 0, s_, ropbk, krop, par).then_inc(SEMS["V"], 1)
                C["V"] += 1
            ropek_done = C["V"]

            @block.vector
            def _(v, par=par, prev2=prev2):
                if prev2 is not None:
                    wt(v, "v", "dH" + str(par), prev2["shard_done"])
                v.tensor_copy(vst[:, par, 0:512], PO[:, :])
                v.tensor_copy(vst[:, par, 512:1024], PD[:, :]).then_inc(SEMS["V"], 1)
                C["V"] += 1
            vevac_done = C["V"]

            @block.vector
            def _(v, s_=s_, par=par, q_mm=q_mm, prev2=prev2):
                wt(v, "v", "P", q_mm)
                if prev2 is not None:
                    wt(v, "v", "dQ" + str(par), prev2["qtr_done"])
                rope_emit(v, 2048, s_, ropbq, qrop, par).then_inc(SEMS["V"], 1)
                C["V"] += 1
            ropeq_done = C["V"]

            @block.vector
            def _(v, par=par):
                v.tensor_mul(sqk[:, :], krop[:, par, :], krop[:, par, :]).then_inc(SEMS["V"], 1)
                C["V"] += 1
                v.tensor_mul(sqq[:, :], qrop[:, par, :], qrop[:, par, :]).then_inc(SEMS["V"], 1)
                C["V"] += 1
            sq_done = C["V"]

            @block.vector
            def _(v, b=b, sq_done=sq_done):
                v.reduce_sum(n16[:, :], sqk[:, :].rearrange("p (h q) -> p h q", h=16),
                             axis=mybir.AxisListType.X)
                v.reduce_max(ncol[:, 0:1], n16[:, :], axis=mybir.AxisListType.X)
                v.tensor_max(Mk[:, b:b + 1], Mk[:, b:b + 1], ncol[:, 0:1])
                v.reduce_sum(n16[:, :], sqq[:, :].rearrange("p (h q) -> p h q", h=16),
                             axis=mybir.AxisListType.X)
                v.reduce_max(n8[:, :], n16[:, :].rearrange("p (pr two) -> p pr two", two=2),
                             axis=mybir.AxisListType.X)
                v.tensor_max(Mq[:, b * 8:(b + 1) * 8], Mq[:, b * 8:(b + 1) * 8],
                             n8[:, :]).then_inc(SEMS["V"], 1)
                C["V"] += 1
            norms_done = C["V"]

            @block.sync
            def _(s, par=par, ropek_done=ropek_done, prev2=prev2):
                wt(s, "s", "V", ropek_done)
                if prev2 is not None:
                    wt(s, "s", "dH" + str(par), prev2["shard_done"])
                for pr in range(8):
                    dma(s, "dK" + str(par), kts[:, par, pr * 128:(pr + 1) * 128],
                        krop[:, par, pr * 128:(pr + 1) * 128], transpose=True)
            ktr_done = C["dK" + str(par)]

            @block.sync
            def _(s, b=b, s_=s_, par=par, ropeq_done=ropeq_done):
                wt(s, "s", "V", ropeq_done)
                for pr in range(8):
                    dma(s, "dQ" + str(par), qhat[:, b, pr, s_, :],
                        qrop[:, par, pr * 128:(pr + 1) * 128], transpose=True)
            qtr_done = C["dQ" + str(par)]

            @block.gpsimd
            def _(g, b=b, s_=s_, par=par, ktr_done=ktr_done, vevac_done=vevac_done):
                wt(g, "g", "dK" + str(par), ktr_done)
                dma(g, "dH" + str(par), d_kvsh[b, :, 0, :, s_, :].transpose([1, 0, 2]),
                    kts[:, par, :].rearrange("p (pr c) -> p pr c", c=128))
                wt(g, "g", "V", vevac_done)
                dma(g, "dH" + str(par), d_kvsh[b, :, 1, :, s_, :].transpose([1, 0, 2]),
                    vst[:, par, :].rearrange("p (pr c) -> p pr c", c=128))
            shard_done = C["dH" + str(par)]

            slot_units[(b, s_)] = dict(
                ropek_done=ropek_done, ropeq_done=ropeq_done, vevac_done=vevac_done,
                norms_done=norms_done, ktr_done=ktr_done, qtr_done=qtr_done,
                shard_done=shard_done, q_mm=q_mm,
            )

            if s_ == 2 and b + 1 < B:
                emit_lat_chain(b + 1)

        nd = slot_units[(b, 3)]["norms_done"]

        @block.tensor
        def _(t, b=b, nd=nd):
            wt(t, "t", "V", nd)
            wt(t, "t", "dC", CONST_DONE)
            t.transpose(PS[0:8, 2048:2176], Mq[:, 8 * b:8 * b + 8], idt[:, :])
            t.transpose(PS[0:1, 2176:2304], Mk[:, b:b + 1],
                        idt[:, :]).then_inc(SEMS["P"], 1)
            C["P"] += 1
        tr1 = C["P"]

        @block.vector
        def _(v, b=b, tr1=tr1):
            wt(v, "v", "P", tr1)
            v.reduce_max(tmpa[0:8, 0:1], PS[0:8, 2048:2176], axis=mybir.AxisListType.X)
            v.reduce_max(tmpb[0:1, 0:1], PS[0:1, 2176:2304],
                         axis=mybir.AxisListType.X)
            v.tensor_copy(mrow[0:1, 32 + b:33 + b],
                          tmpb[0:1, 0:1]).then_inc(SEMS["V"], 1)
            C["V"] += 1
        rmx = C["V"]

        @block.tensor
        def _(t, rmx=rmx):
            wt(t, "t", "V", rmx)
            t.transpose(PS[0:1, 2560:2568], tmpa[0:8, 0:1],
                        idt[0:8, 0:8]).then_inc(SEMS["P"], 1)
            C["P"] += 1
        tr2 = C["P"]

        @block.vector
        def _(v, tr2=tr2):
            wt(v, "v", "P", tr2)
            v.tensor_copy(mrow[0:1, 0:8], PS[0:1, 2560:2568]).then_inc(SEMS["V"], 1)
            C["V"] += 1
        cpy = C["V"]

        @block.tensor
        def _(t, cpy=cpy):
            wt(t, "t", "V", cpy)
            t.matmul(PS[:, 2568:2576], on_row[0:1, 0:128], mrow[0:1, 0:8],
                     start=True, stop=True).then_inc(SEMS["P"], 1)
            C["P"] += 1
        bcm = C["P"]

        @block.vector
        def _(v, b=b, bcm=bcm):
            wt(v, "v", "P", bcm)
            v.tensor_copy(MqR[:, 8 * b:8 * b + 8],
                          PS[:, 2568:2576]).then_inc(SEMS["V"], 1)
            C["V"] += 1
        ph2_V[b] = C["V"]

        @defer_scalar
        def _(a, b=b, pv=ph2_V[b]):
            wt(a, "a", "V", pv)
            a.activation(sqMq[:, 8 * b:8 * b + 8], MqR[:, 8 * b:8 * b + 8],
                         AF.Ln, bias=cvals[:, 1:2])
            a.activation(sqMq[:, 8 * b:8 * b + 8], sqMq[:, 8 * b:8 * b + 8],
                         AF.Exp, scale=0.5).then_inc(SEMS["A"], 1)
        C["A"] += 1
        SQMQ_B[b] = C["A"]

        @block.gpsimd
        def _(g, b=b, rmx=rmx,
              kvdh=tuple((k, C[k]) for k in ("dH0", "dH1", "dH2", "dH3"))):
            wt(g, "g", "V", rmx)
            dma(g, "dSS", d_stsh[b:b + 1, 1:128], on_row[0:1, 1:128])
            dma(g, "dSS", d_stsh[b:b + 1, 0:1], mrow[0:1, 32 + b:33 + b])
            wt(g, "g", "dSS", C["dSS"])
            g.collective_compute(
                "AllGather", ALU.bypass, replica_groups=[list(range(NC))],
                ins=[d_stsh[b:b + 1, :]], outs=[d_stfull[b]],
            ).then_inc(SEMS["CC"], 1)
            C["CC"] += 1
            if b == 0:
                for hk, hv in kvdh:
                    wt(g, "g", hk, hv)
                g.collective_compute(
                    "AllGather", ALU.bypass, replica_groups=[list(range(NC))],
                    ins=[d_kvsh[0]], outs=[d_kvfull[0]],
                ).then_inc(SEMS["CC"], 1)
                C["CC"] += 1
        CC_STATS[b] = C["CC"] - (1 if b == 0 else 0)
        if b == 0:
            CC_KV[b] = C["CC"]

            @block.gpsimd
            def _(g):
                for fm in range(8):
                    for foc in range(8):
                        dma(g, "dW", d_fcache[fm, foc],
                            d_fwT[fm].rearrange("(ic p) (oc c) -> oc p ic c",
                                                p=128, c=128)[foc])

    @block.gpsimd
    def _(g):
        for hk in ("dH0", "dH1", "dH2", "dH3"):
            wt(g, "g", hk, C[hk])
        for bq in range(1, B):
            g.collective_compute(
                "AllGather", ALU.bypass, replica_groups=[list(range(NC))],
                ins=[d_kvsh[bq]], outs=[d_kvfull[bq]],
            ).then_inc(SEMS["CC"], 1)
            C["CC"] += 1
            CC_KV[bq] = C["CC"]

    W_DONE = C["dW"]
    LAST_NORMS = slot_units[(B - 1, 3)]["norms_done"]

    kmb_done = 0
    for b in range(B):
        @block.sync
        def _(s, b=b, cs=CC_STATS[b]):
            wt(s, "s", "CC", cs)
            s.dma_start(out=strd[0:1, b * 8:b * 8 + 8],
                        in_=d_stfull[b, :, 0:1].transpose([1, 0]),
                        allow_slow_non_contiguous=True).then_inc(SEMS["dC"], 16)
            C["dC"] += 16

        @block.vector
        def _(v, b=b, sl=C["dC"]):
            wt(v, "v", "dC", sl)
            v.reduce_max(kmaxg[0:1, b:b + 1], strd[0:1, b * 8:b * 8 + 8],
                         axis=mybir.AxisListType.X).then_inc(SEMS["V"], 1)
            C["V"] += 1
        kmg = C["V"]

        @defer_scalar
        def _(a, b=b, kmg=kmg):
            wt(a, "a", "V", kmg)
            a.activation(kmaxg[0:1, 4 + b:5 + b], kmaxg[0:1, b:b + 1],
                         AF.Ln, bias=cvals[0:1, 1:2])
            a.activation(kmaxg[0:1, b:b + 1], kmaxg[0:1, 4 + b:5 + b],
                         AF.Exp, scale=0.5).then_inc(SEMS["A"], 1)
        C["A"] += 1
        ksq = C["A"]

        @block.tensor
        def _(t, b=b, ksq=ksq, pk=kmb_done):
            wt(t, "t", "A", ksq)
            if b == 0:
                wt(t, "t", "V", ph2_V[B - 1])
            if pk:
                wt(t, "t", "V", pk)
            t.matmul(PS[:, 1024 + b:1025 + b], on_row[0:1, 0:128],
                     kmaxg[0:1, b:b + 1], start=True,
                     stop=True).then_inc(SEMS["P"], 1)
            C["P"] += 1
        kmm = C["P"]

        @block.vector
        def _(v, b=b, kmm=kmm, sq=SQMQ_B[b]):
            wt(v, "v", "P", kmm)
            v.tensor_copy(kmb[:, b:b + 1], PS[:, 1024 + b:1025 + b])
            wt(v, "v", "A", sq)
            v.tensor_scalar(biasB[:, b * 8:(b + 1) * 8], sqMq[:, b * 8:(b + 1) * 8],
                            kmb[:, b:b + 1], -1.0, ALU.mult,
                            ALU.mult).then_inc(SEMS["V"], 1)
            C["V"] += 1
        kmb_done = C["V"]
        BIAS_B[b] = C["V"]
    BIAS_DONE = BIAS_B[B - 1]

    BAR = dict(C)

    def barrier_waits(eng, ename):
        for k in SEMNAMES:
            if k != "CC":
                wt(eng, ename, k, BAR[k])

    @block.tensor
    def _(t):
        barrier_waits(t, "t")

    @block.vector
    def _(v):
        barrier_waits(v, "v")

    @defer_scalar
    def _(a):
        barrier_waits(a, "a")

    @block.sync
    def _(s):
        barrier_waits(s, "s")

    @block.gpsimd
    def _(g):
        barrier_waits(g, "g")

    bctx.close()

    kvK = sb("kvK", [128, 2, 32, 128], BF16)
    kvV = sb("kvV", [128, 2, 32, 128], BF16)
    pt = sb("pt", [128, 3, 1024], BF16)
    attnT = sb("attnT", [128, 8, 512], BF16)
    rcp = sb("rcp", [128, 512], F32)
    xrc = sb("xrc", [128, 2, 512], F32)
    rbuf = sb("rbuf", [128, 8, 512], F32)
    rsq = sb("rsq", [128, 512], F32)
    nrm = sb("nrm", [128, 4, 512], F32)
    xmt = sb("xmt", [128, 8, 512], BF16)
    xm32c = sb("xm32c", [128, 2, 512], F32)
    cT32 = sb("cT32", [128, 8, 512], F32)
    cT16 = sb("cT16", [128, 2, 8, 512], BF16)
    fwc = sb("fwc", [128, 2, 2, 8, 128], BF16)
    gf = sb("gf", [128, 512], BF16)
    gi = sb("gi", [128, 512], BF16)
    gg = sb("gg", [128, 512], BF16)
    hc = sb("hc", [128, 2, 512], F32)

    @defer_scalar
    def _(a):
        wt(a, "a", "dC", CONST_DONE)
        for j in range(8):
            a.activation(cT32[:, j, :], cT32[:, j, :], AF.Identity,
                         bias=incs[:, j:j + 1], scale=0.0).then_maybe_inc(
                (SEMS["A"], 1) if j == 7 else None)
        pass
    C["A"] += 1
    CINIT_A = C["A"]

    @block.vector
    def _(v):
        wt(v, "v", "A", CINIT_A)
        for j in range(8):
            v.tensor_copy(cT16[:, 0, j, :], cT32[:, j, :]).then_maybe_inc(
                (SEMS["V"], 1) if j == 7 else None)
        pass
        C["V"] += 1
    cready = C["V"]

    FCHUNKS = []
    for _t in range(B):
        for _mo in range(8):
            for _g in range(3):
                FCHUNKS.append((2 * _g, 2 * _g + 1, _mo))
    for _t in range(B):
        for _oc in range(8):
            FCHUNKS.append((6, 7, _oc))
    chunk_mm = {}
    chunk_ld = {}
    chunk_act = {}
    cupd_last = {-1: cready}
    ffn_state = {"last_cupd": cready}

    def emit_fwc_load(ci):
        if ci >= len(FCHUNKS):
            return
        wm, um, oc_ = FCHUNKS[ci]
        buf = ci % 2
        key = "dF" + str(buf)

        @block.gpsimd
        def _(g, ci=ci, wm=wm, um=um, oc_=oc_, buf=buf, key=key):
            wt(g, "g", "dW", W_DONE)
            if ci - 2 in chunk_mm:
                wt(g, "g", "P", chunk_mm[ci - 2])
            dma(g, key, fwc[:, buf, 0, :, :], d_fcache[wm, oc_])
            dma(g, key, fwc[:, buf, 1, :, :], d_fcache[um, oc_])
        chunk_ld[ci] = (key, C[key])

    emit_fwc_load(0)
    emit_fwc_load(1)

    def emit_xmt_load(step):
        @block.sync
        def _(s, step=step):
            for dk in ("dN0", "dN1", "dN2", "dN3"):
                wt(s, "s", dk, C[dk])
            if step > 0:
                wt(s, "s", "P", chunk_mm[step * 24 - 1])
            dma(s, "dM", xmt[:, :, :], d_xmid16[step].transpose([1, 0, 2]))
        return C["dM"]

    def emit_gate_part(ci, step, mo, gg_i, part, xmt_ld, extraV=0):
        buf = ci % 2
        bank = 4 + (ci % 2)
        func = (AF.Sigmoid, AF.Sigmoid, AF.Tanh)[gg_i]

        @block.tensor
        def _(t, ci=ci, buf=buf, bank=bank, xmt_ld=xmt_ld, step=step,
              part=part, extraV=extraV, cw=cupd_last[step - 1]):
            if part == 0:
                wt(t, "t", chunk_ld[ci][0], chunk_ld[ci][1])
                wt(t, "t", "dM", xmt_ld)
                if extraV:
                    wt(t, "t", "V", extraV)
                if ci - 2 in chunk_act:
                    wt(t, "t", "A", chunk_act[ci - 2])
            elif part == 2:
                wt(t, "t", "V", cw)
            w_u = part // 2
            src = xmt if w_u == 0 else None
            for j in range(4):
                ic = (part % 2) * 4 + j
                rhs = xmt[:, ic, :] if w_u == 0 else cT16[:, step % 2, ic, :]
                t.matmul(PS[:, bank * 512:(bank + 1) * 512],
                         fwc[:, buf, w_u, ic, :], rhs,
                         start=(part == 0 and j == 0),
                         stop=(part == 3 and j == 3)).then_maybe_inc(
                    (SEMS["P"], 1) if (part == 3 and j == 3) else None)
            pass
            if part == 3:
                C["P"] += 1
        if part == 3:
            chunk_mm[ci] = C["P"]
            emit_fwc_load(ci + 2)

            @defer_scalar
            def _(a, bank=bank, gg_i=gg_i, mo=mo, func=func, gmm=chunk_mm[ci],
                  pcu=(ffn_state["last_cupd"] if gg_i == 0 else 0)):
                wt(a, "a", "P", gmm)
                if pcu:
                    wt(a, "a", "V", pcu)
                dstb = (gf, gi, gg)[gg_i]
                a.activation(dstb[:, :], PS[:, bank * 512:(bank + 1) * 512],
                             func, bias=bffs[:, gg_i * 8 + mo:gg_i * 8 + mo + 1]
                             ).then_inc(SEMS["A"], 1)
            C["A"] += 1
            chunk_act[ci] = C["A"]

    FFN_TILES = [3, 5, 8, 10, 13, 15, 18, 20, 23, 25, 28, 30]
    FFN_TILES_P0 = [8, 10, 12, 14, 17, 19, 22, 24, 26, 28, 29, 30]

    def emit_gate_unit(ci, step, mo, gg_i, xmt_ld, extraV=0):
        for part in range(4):
            emit_gate_part(ci, step, mo, gg_i, part, xmt_ld,
                           extraV=(extraV if part == 0 else 0))

    def emit_cupdate(step, mo):
        nxt = (step + 1) % 2

        @block.vector
        def _(v, mo=mo, nxt=nxt, gact=chunk_act[step * 24 + mo * 3 + 2]):
            wt(v, "v", "A", gact)
            v.tensor_mul(tmpa[:, :], gf[:, :], cT32[:, mo, :])
            v.tensor_mul(tmpb[:, :], gi[:, :], gg[:, :])
            v.tensor_add(cT32[:, mo, :], tmpa[:, :], tmpb[:, :])
            v.tensor_copy(cT16[:, nxt, mo, :],
                          cT32[:, mo, :]).then_inc(SEMS["V"], 1)
            C["V"] += 1
        ffn_state["last_cupd"] = C["V"]

    xmt_lds = {}

    def kv_prefetch(s, b, pair, buf):
        key = "dV" + str(buf)
        for c in range(NC):
            dma(s, key, kvK[:, buf, 4 * c:4 * (c + 1), :], d_kvfull[b, c, pair, 0])
            dma(s, key, kvV[:, buf, 4 * c:4 * (c + 1), :], d_kvfull[b, c, pair, 1])
        return (key, C[key])

    kvready = {}

    @block.sync
    def _(s):
        wt(s, "s", "CC", CC_KV[0])
        kvready[(0, 0)] = kv_prefetch(s, 0, 0, 0)

    prev_epilogue_V = 0
    prev_pvden_P = 0
    wo_done_P = 0
    attn_done_V = {}
    wo_radds = {}
    wo_nrm_V = {}
    exp_slot_wait = {0: 0, 1: 0}

    for b in range(B):
        for pair in range(8):
            bp = b * 8 + pair
            buf = bp % 2

            @block.sync
            def _(s, bp=bp):
                if bp + 1 < 32:
                    nb, npair = divmod(bp + 1, 8)
                    if bp >= 1:
                        wt(s, "s", "P", prev_pvden_P)
                    wt(s, "s", "CC", CC_KV[nb])
                    kvready[(nb, npair)] = kv_prefetch(s, nb, npair, (bp + 1) % 2)

            if b >= 1 and pair == 0:
                xmt_lds[b - 1] = emit_xmt_load(b - 1)

            P_sT = {}
            A_exp = {}
            V_mask = {}

            def emit_dummies(epiV=prev_epilogue_V):
                @block.tensor
                def _(t, epiV=epiV):
                    wt(t, "t", "V", epiV)
                    t.matmul(PO[:, :], on_row16[0:1, 0:128], zrow[0:1, :],
                             start=True, stop=True)
                    t.matmul(PD[:, :], on_row16[0:1, 0:128], zrow[0:1, :],
                             start=True, stop=True)

            def emit_pv(tt, buf=buf):
                slot = tt % 3
                n0 = 32 * (tt // 2)
                N = 512 - n0

                @block.tensor
                def _(t, buf=buf, tt=tt, slot=slot, n0=n0, N=N, vm=V_mask[tt]):
                    wt(t, "t", "V", vm)
                    t.matmul(PO[0:64, n0:512], kvV[:, buf, tt, 0:64],
                             pt[:, slot, 0:N], start=False, stop=False,
                             tile_position=(0, 0), skip_group_check=True)
                    t.matmul(PO[64:128, n0:512], kvV[:, buf, tt, 64:128],
                             pt[:, slot, 512:512 + N], start=False, stop=False,
                             tile_position=(0, 64), skip_group_check=True)
                    t.matmul(PD[0:64, n0:512], on_mat[:, :],
                             pt[:, slot, 0:N], start=False, stop=False,
                             tile_position=(0, 0), skip_group_check=True)
                    t.matmul(PD[64:128, n0:512], on_mat[:, :],
                             pt[:, slot, 512:512 + N], start=False, stop=False,
                             tile_position=(0, 64), skip_group_check=True
                             ).then_inc(SEMS["P"], 1)
                C["P"] += 1

            for tt in range(32):
                slot = tt % 2
                n0 = 32 * (tt // 2)
                N = 512 - n0
                if tt == 0:
                    emit_dummies()
                if tt >= 3:
                    emit_pv(tt - 3)
                ftl = FFN_TILES_P0 if pair == 0 else FFN_TILES
                if b >= 1 and tt in ftl:
                    idx = ftl.index(tt)
                    g = idx // 4
                    part = idx % 4
                    ci = (b - 1) * 24 + pair * 3 + g
                    emit_gate_part(ci, b - 1, pair, g, part, xmt_lds[b - 1],
                                   extraV=(wo_nrm_V[b - 1]
                                           if (pair == 0 and g == 0 and part == 0)
                                           else 0))

                @block.tensor
                def _(t, b=b, pair=pair, buf=buf, tt=tt, slot=slot, n0=n0, N=N,
                      ew=exp_slot_wait[slot],
                      wonrm=(wo_radds[b - 1][7] if (pair == 0 and b > 0) else None)):
                    if tt == 0:
                        wt(t, "t", kvready[(b, pair)][0], kvready[(b, pair)][1])
                        wt(t, "t", "V", BIAS_B[b])
                        if wonrm is not None:
                            wt(t, "t", "V", wonrm)
                    if ew:
                        wt(t, "t", "A", ew)
                    qfe = qhat[0:64, b, pair, :, :].rearrange(
                        "p a c -> p (a c)")[:, n0:512]
                    qfo = qhat[64:128, b, pair, :, :].rearrange(
                        "p a c -> p (a c)")[:, n0:512]
                    t.matmul(PS[:, slot * 1024:slot * 1024 + N],
                             kvK[0:64, buf, tt, :], qfe,
                             start=True, stop=True)
                    t.matmul(PS[:, slot * 1024 + 512:slot * 1024 + 512 + N],
                             kvK[64:128, buf, tt, :], qfo,
                             start=True, stop=True,
                             tile_position=(64, 0)).then_inc(SEMS["P"], 1)
                C["P"] += 1
                P_sT[tt] = C["P"]

                @defer_scalar
                def _(a, b=b, pair=pair, slot=slot, pslot=tt % 3, N=N, stp=P_sT[tt]):
                    wt(a, "a", "P", stp)
                    inap = PS[:, slot * 1024:(slot + 1) * 1024].rearrange(
                        "p (two n) -> p two n", two=2)[:, :, 0:N]
                    outap = pt[:, pslot, :].rearrange(
                        "p (two n) -> p two n", two=2)[:, :, 0:N]
                    a.activation(outap, inap, AF.Exp,
                                 bias=biasB[:, b * 8 + pair:b * 8 + pair + 1]
                                 ).then_inc(SEMS["A"], 1)
                C["A"] += 1
                A_exp[tt] = C["A"]
                exp_slot_wait[slot] = C["A"]

                @block.vector
                def _(v, tt=tt, pslot=tt % 3, N=N, ae=A_exp[tt]):
                    if tt == 0:
                        wt(v, "v", "dW", W_DONE)
                    wt(v, "v", "A", ae)
                    pap = bass.AP(pt.handle if hasattr(pt, "handle") else pt,
                                  pslot * 1024, [[3072, 128], [512, 2], [1, 32]])
                    v.tensor_mul(pap, pap, masks[:, tt % 2, :, :]).then_inc(
                        SEMS["V"], 1)
                C["V"] += 1
                V_mask[tt] = C["V"]

            emit_pv(29)
            emit_pv(30)
            emit_pv(31)
            prev_pvden_P = C["P"]

            @defer_scalar
            def _(a, pv=prev_pvden_P):
                wt(a, "a", "P", pv)
                a.activation(rsq[:, :], PD[:, :], AF.Ln, bias=cvals[:, 1:2])
                a.activation(rcp[:, :], rsq[:, :], AF.Exp, scale=-1.0).then_inc(SEMS["A"], 1)
            C["A"] += 1
            epi_a = C["A"]

            @block.vector
            def _(v, pair=pair, epi_a=epi_a, b=b, wod=wo_done_P):
                wt(v, "v", "A", epi_a)
                if b > 0:
                    wt(v, "v", "P", wod)
                v.tensor_mul(attnT[:, pair, :], PO[:, :], rcp[:, :]).then_inc(SEMS["V"], 1)
                C["V"] += 1
            prev_epilogue_V = C["V"]

            if b >= 1:
                emit_cupdate(b - 1, pair)
                if pair == 7:
                    cupd_last[b - 1] = C["V"]

        attn_done_V[b] = prev_epilogue_V

        @block.sync
        def _(s, b=b):
            if b > 0:
                wt(s, "s", "V", wo_radds[b - 1][6])
                wt(s, "s", "V", wo_radds[b - 1][7])
            dma(s, "dR0", xrc[:, 0, :], d_xq[b, 0:128, :])
            dma(s, "dR1", xrc[:, 1, :], d_xq[b, 128:256, :])
        xrc_loads = {0: ("dR0", C["dR0"]), 1: ("dR1", C["dR1"])}

        wo_units = []
        radds = {}
        for oc in range(8):
            bank = oc % 2

            @block.tensor
            def _(t, b=b, oc=oc, bank=bank):
                wt(t, "t", "V", attn_done_V[b])
                if oc >= 2:
                    wt(t, "t", "V", wo_units[oc - 2]["radd"])
                for cc in range(8):
                    t.matmul(PS[:, bank * 512:(bank + 1) * 512],
                             w_o[:, cc, oc * 128:(oc + 1) * 128],
                             attnT[:, cc, :],
                             start=(cc == 0), stop=(cc == 7)).then_maybe_inc(
                        (SEMS["P"], 1) if cc == 7 else None)
                pass
                C["P"] += 1
            womm = C["P"]

            @block.vector
            def _(v, oc=oc, bank=bank, womm=womm, xl=xrc_loads[oc]):
                wt(v, "v", "P", womm)
                wt(v, "v", xl[0], xl[1])
                v.tensor_add(rbuf[:, oc, :], PS[:, bank * 512:(bank + 1) * 512],
                             xrc[:, oc % 2, :]).then_inc(SEMS["V"], 1)
                C["V"] += 1
            radd = C["V"]
            radds[oc] = radd

            @block.sync
            def _(s, b=b, oc=oc, radd=radd):
                if oc + 2 < 8:
                    wt(s, "s", "V", radd)
                    key = "dR" + str(oc % 2)
                    dma(s, key, xrc[:, oc % 2, :], d_xq[b, (oc + 2) * 128:(oc + 3) * 128, :])
                    xrc_loads[oc + 2] = (key, C[key])

            @block.vector
            def _(v, oc=oc):
                v.tensor_mul(rsq[:, :], rbuf[:, oc, :], rbuf[:, oc, :]).then_inc(SEMS["V"], 1)
                C["V"] += 1
            rsq_a = C["V"]

            @block.tensor
            def _(t, oc=oc, radd=radd, rsq_a=rsq_a):
                wt(t, "t", "V", rsq_a)
                t.matmul(PD[0:1, 0:512], on_col[:, 0:1], rbuf[:, oc, :],
                         start=(oc == 0), stop=(oc == 7))
                t.matmul(PO[0:1, 0:512], on_col[:, 0:1], rsq[:, :],
                         start=(oc == 0), stop=(oc == 7)).then_inc(SEMS["P"], 1)
                C["P"] += 1
            wo_units.append(dict(radd=radd, stats=C["P"]))

        wo_radds[b] = radds
        ln_stats = wo_units[-1]["stats"]

        @block.vector
        def _(v, ln_stats=ln_stats):
            wt(v, "v", "P", ln_stats)
            v.tensor_scalar(mu_s, PD[0:1, 0:512], 1.0 / D, None, ALU.mult)
            v.tensor_scalar(var_s, PO[0:1, 0:512], 1.0 / D, None, ALU.mult)
            v.tensor_mul(m2_s, mu_s, mu_s)
            v.tensor_sub(var_s, var_s, m2_s).then_inc(SEMS["V"], 1)
            C["V"] += 1
        ln_v = C["V"]

        @defer_scalar
        def _(a, ln_v=ln_v):
            wt(a, "a", "V", ln_v)
            a.activation(sd_s, var_s, AF.Ln, bias=cvals[0:1, 0:1])
            a.activation(var_s, sd_s, AF.Exp, scale=-0.5).then_inc(SEMS["A"], 1)
        C["A"] += 1
        ln_r = C["A"]

        @block.tensor
        def _(t, ln_r=ln_r, ga=chunk_act.get(b * 24 - 1, 0)):
            wt(t, "t", "A", ln_r)
            if ga:
                wt(t, "t", "A", ga)
            t.matmul(PS[:, 2048:2560], on_row[0:1, 0:128], mu_s, start=True, stop=True)
            t.matmul(PS[:, 2560:3072], on_row[0:1, 0:128], var_s, start=True,
                     stop=True).then_inc(SEMS["P"], 1)
            C["P"] += 1
        ln_bc = C["P"]

        nrm_dmas = {}
        for oc in range(8):
            @block.vector
            def _(v, oc=oc, ln_bc=ln_bc, nrm_dmas=nrm_dmas):
                wt(v, "v", "P", ln_bc)
                if oc >= 4:
                    wt(v, "v", nrm_dmas[oc - 4][0], nrm_dmas[oc - 4][1])
                v.tensor_sub(tmpa[:, :], rbuf[:, oc, :], PS[:, 2048:2560])
                v.tensor_mul(nrm[:, oc % 4, :], tmpa[:, :],
                             PS[:, 2560:3072]).then_inc(SEMS["V"], 1)
                C["V"] += 1
            nrm_v = C["V"]

            @block.gpsimd
            def _(g, b=b, oc=oc, nrm_v=nrm_v, nrm_dmas=nrm_dmas):
                wt(g, "g", "V", nrm_v)
                key = "dN" + str(oc % 4)
                dma(g, key, d_xmid32[b, oc], nrm[:, oc % 4, :])
                dma(g, key, d_xmid16[b, oc], nrm[:, oc % 4, :])
                nrm_dmas[oc] = (key, C[key])
        wo_nrm_V[b] = C["V"]
        wo_done_P = ln_bc

    xmt_lds[3] = emit_xmt_load(3)
    for mo in range(8):
        for g in range(3):
            ci = 3 * 24 + mo * 3 + g
            emit_gate_unit(ci, 3, mo, g, xmt_lds[3],
                           extraV=(wo_nrm_V[3] if (mo == 0 and g == 0) else 0))
        emit_cupdate(3, mo)
    cupd_last[3] = C["V"]

    CFIN = B % 2
    ffn_c_done = C["V"]
    CI = 96

    h_radds = {}
    nrm_dmas5 = {}
    for tt in range(B):
        @block.sync
        def _(s, tt=tt, lastmm=chunk_mm[CI - 1],
              dn=tuple((dk, C[dk]) for dk in ("dN0", "dN1", "dN2", "dN3"))):
            for dk, dv in dn:
                wt(s, dk, dk, dv) if False else wt(s, "s", dk, dv)
            wt(s, "s", "P", lastmm)
            dma(s, "dM", xmt[:, :, :], d_xmid16[tt].transpose([1, 0, 2]))
        hx_load = C["dM"]

        hu = []
        for oc in range(8):
            ci = CI
            CI += 1
            buf = ci % 2

            @block.tensor
            def _(t, oc=oc, ci=ci, buf=buf, hx_load=hx_load, ffn_c_done=ffn_c_done):
                wt(t, "t", chunk_ld[ci][0], chunk_ld[ci][1])
                wt(t, "t", "dM", hx_load)
                wt(t, "t", "V", ffn_c_done)
                bank = oc % 2
                if oc >= 2:
                    wt(t, "t", "V", hu[oc - 2]["radd"])
                for ic in range(8):
                    t.matmul(PS[:, bank * 512:(bank + 1) * 512],
                             fwc[:, buf, 0, ic, :], xmt[:, ic, :],
                             start=(ic == 0), stop=False)
                for ic in range(8):
                    t.matmul(PS[:, bank * 512:(bank + 1) * 512],
                             fwc[:, buf, 1, ic, :], cT16[:, CFIN, ic, :],
                             start=False, stop=(ic == 7)).then_maybe_inc(
                        (SEMS["P"], 1) if ic == 7 else None)
                pass
                C["P"] += 1
            hmm_ = C["P"]
            chunk_mm[ci] = hmm_
            emit_fwc_load(ci + 2)

            @defer_scalar
            def _(a, oc=oc, hmm_=hmm_):
                wt(a, "a", "P", hmm_)
                a.activation(hc[:, oc % 2, :], PS[:, (oc % 2) * 512:(oc % 2 + 1) * 512],
                             AF.Tanh, bias=bffs[:, 24 + oc:25 + oc]).then_inc(SEMS["A"], 1)
            C["A"] += 1
            hact = C["A"]

            @block.sync
            def _(s, tt=tt, oc=oc):
                key = "dO" + str(oc % 2)
                tenant = h_radds.get((tt, oc - 2)) if oc >= 2 else \
                    (h_radds.get((tt - 1, oc + 6)) if tt > 0 else None)
                if tenant is not None:
                    wt(s, "s", "V", tenant)
                dma(s, key, xm32c[:, oc % 2, :], d_xmid32[tt, oc])
            xm_ld = (("dO" + str(oc % 2)), C["dO" + str(oc % 2)])

            @block.vector
            def _(v, oc=oc, hact=hact, xm_ld=xm_ld):
                wt(v, "v", "A", hact)
                wt(v, "v", xm_ld[0], xm_ld[1])
                v.tensor_add(rbuf[:, oc, :], hc[:, oc % 2, :],
                             xm32c[:, oc % 2, :]).then_inc(SEMS["V"], 1)
                C["V"] += 1
            radd = C["V"]
            h_radds[(tt, oc)] = radd

            @block.vector
            def _(v, oc=oc):
                v.tensor_mul(rsq[:, :], rbuf[:, oc, :], rbuf[:, oc, :]).then_inc(SEMS["V"], 1)
                C["V"] += 1
            rsq_a = C["V"]

            @block.tensor
            def _(t, oc=oc, radd=radd, rsq_a=rsq_a):
                wt(t, "t", "V", rsq_a)
                t.matmul(PD[0:1, 0:512], on_col[:, 0:1], rbuf[:, oc, :],
                         start=(oc == 0), stop=(oc == 7))
                t.matmul(PO[0:1, 0:512], on_col[:, 0:1], rsq[:, :],
                         start=(oc == 0), stop=(oc == 7)).then_inc(SEMS["P"], 1)
                C["P"] += 1
            hu.append(dict(radd=radd, stats=C["P"]))

        @block.vector
        def _(v, hu=hu):
            wt(v, "v", "P", hu[-1]["stats"])
            v.tensor_scalar(mu_s, PD[0:1, 0:512], 1.0 / D, None, ALU.mult)
            v.tensor_scalar(var_s, PO[0:1, 0:512], 1.0 / D, None, ALU.mult)
            v.tensor_mul(m2_s, mu_s, mu_s)
            v.tensor_sub(var_s, var_s, m2_s).then_inc(SEMS["V"], 1)
            C["V"] += 1
        ln_v = C["V"]

        @defer_scalar
        def _(a, ln_v=ln_v):
            wt(a, "a", "V", ln_v)
            a.activation(sd_s, var_s, AF.Ln, bias=cvals[0:1, 0:1])
            a.activation(var_s, sd_s, AF.Exp, scale=-0.5).then_inc(SEMS["A"], 1)
        C["A"] += 1
        ln_r = C["A"]

        @block.tensor
        def _(t, ln_r=ln_r):
            wt(t, "t", "A", ln_r)
            t.matmul(PS[:, 2048:2560], on_row[0:1, 0:128], mu_s, start=True, stop=True)
            t.matmul(PS[:, 2560:3072], on_row[0:1, 0:128], var_s, start=True,
                     stop=True).then_inc(SEMS["P"], 1)
            C["P"] += 1
        ln_bc = C["P"]

        for oc in range(8):
            @block.vector
            def _(v, tt=tt, oc=oc, ln_bc=ln_bc):
                wt(v, "v", "P", ln_bc)
                tenant = nrm_dmas5.get((tt, oc - 4)) if oc >= 4 else \
                    (nrm_dmas5.get((tt - 1, oc + 4)) if tt > 0 else None)
                if tenant is not None:
                    wt(v, "v", tenant[0], tenant[1])
                v.tensor_sub(tmpa[:, :], rbuf[:, oc, :], PS[:, 2048:2560])
                v.tensor_mul(nrm[:, oc % 4, :], tmpa[:, :],
                             PS[:, 2560:3072]).then_inc(SEMS["V"], 1)
                C["V"] += 1
            nrm_v = C["V"]

            @block.gpsimd
            def _(g, tt=tt, oc=oc, nrm_v=nrm_v):
                wt(g, "g", "V", nrm_v)
                key = "dN" + str(oc % 4)
                dma(g, key, d_yT[tt, oc], nrm[:, oc % 4, :])
                nrm_dmas5[(tt, oc)] = (key, C[key])

    @block.gpsimd
    def _(g):
        for dk in ("dN0", "dN1", "dN2", "dN3"):
            wt(g, "g", dk, C[dk])

    @block.scalar
    def _(a):
        for fn in SCALAR_OPS:
            fn(a)

    ctx.close()
    return nc


def _host_prep(inputs):
    f32 = np.float32
    x = np.asarray(inputs["L0_attn_x"], f32)
    xT = np.ascontiguousarray(x.transpose(0, 2, 1))

    half = 32
    inv = (1.0 / (np.float32(10000.0) ** (np.arange(half, dtype=f32) / np.float32(half)))).astype(f32)

    def tables(pos, scale):
        ang = (pos.astype(f32)[:, None] * inv[None, :]).astype(f32)
        cos = (np.cos(ang).astype(f32) * f32(scale)).astype(f32)
        sin = (np.sin(ang).astype(f32) * f32(scale)).astype(f32)
        out = np.zeros((4, 128, 1024), f32)
        for s in range(4):
            out[s, :, 0:512] = np.tile(cos[s * 128:(s + 1) * 128], (1, 16))
            out[s, :, 512:1024] = np.tile(sin[s * 128:(s + 1) * 128], (1, 16))
        return out

    wq = np.asarray(inputs["L0_attn_W_q"], f32)
    wdkv = np.asarray(inputs["L0_attn_W_dkv"], f32)
    wuk = np.asarray(inputs["L0_attn_W_uk"], f32)
    wuv = np.asarray(inputs["L0_attn_W_uv"], f32)
    wo = np.asarray(inputs["L0_attn_W_o"], f32)
    gam = np.asarray(inputs["L0_attn_ln_gamma"], f32)
    bet = np.asarray(inputs["L0_attn_ln_beta"], f32)
    fwT = np.ascontiguousarray(np.stack([
        np.asarray(inputs["L0_ffn_wf"], f32).T, np.asarray(inputs["L0_ffn_uf"], f32).T,
        np.asarray(inputs["L0_ffn_wi"], f32).T, np.asarray(inputs["L0_ffn_ui"], f32).T,
        np.asarray(inputs["L0_ffn_wc"], f32).T, np.asarray(inputs["L0_ffn_uc"], f32).T,
        np.asarray(inputs["L0_ffn_wo"], f32).T, np.asarray(inputs["L0_ffn_uo"], f32).T,
    ]))
    bias4 = np.stack([np.asarray(inputs["L0_ffn_" + k], f32) for k in ("bf", "bi", "bc", "bo")])
    bff = np.ascontiguousarray(bias4.reshape(4, 8, 128).transpose(2, 0, 1).reshape(128, 32))
    incol = np.ascontiguousarray(np.asarray(inputs["L0_ffn_in_c"], f32).reshape(8, 128).T)
    gcol = np.ascontiguousarray(gam.reshape(2, 128).T)
    bcol = np.ascontiguousarray(bet.reshape(2, 128).T)

    shared = dict(wq=wq, wdkv=wdkv, wuk=wuk, wuv=wuv, wo=wo, fwT=fwT,
                  bff=bff, incol=incol, gcol=gcol, bcol=bcol,
                  idt=np.eye(128, dtype=f32))

    in_maps = []
    for i in range(NC):
        xq_i = np.ascontiguousarray(xT[:, :, i::8])
        xkv_i = np.ascontiguousarray(xT[:, :, 512 * i:512 * (i + 1)])
        pos_q = np.arange(i, T, 8)
        pos_kv = np.arange(512 * i, 512 * (i + 1))
        k_loc = np.arange(128)
        j_loc = np.arange(16)
        mm_ = (k_loc[:, None] <= (8 * j_loc[None, :] + i))
        maskb = np.zeros((128, 2, 2, 32), np.float32)
        maskb[:, 0, :, 0:16] = mm_[:, None, :]
        maskb[:, 0, :, 16:32] = 1.0
        maskb[:, 1, :, 0:16] = 0.0
        maskb[:, 1, :, 16:32] = mm_[:, None, :]
        m = dict(shared)
        m["xq"] = xq_i
        m["xkv"] = xkv_i
        m["ropek"] = tables(pos_kv, 1.0)
        m["ropeq"] = tables(pos_q, 0.125)
        m["maskb"] = np.ascontiguousarray(maskb.reshape(128, 128)).astype(ml_dtypes.bfloat16)
        in_maps.append(m)
    return in_maps


def kernel(**inputs):
    in_maps = _host_prep(inputs)
    nc = build_program()
    res = run_bass_kernel_spmd(nc, in_maps, core_ids=list(range(NC)))
    outT = np.empty((B, D, T), np.float32)
    for i in range(NC):
        outT[:, :, i::8] = res.results[i]["yT"].reshape(B, D, 512)
    return np.ascontiguousarray(outT.transpose(0, 2, 1))


# version 24
# speedup vs baseline: 1.1423x; 1.0678x over previous
import numpy as np
import ml_dtypes
from contextlib import ExitStack

import concourse.bass as bass
import concourse.mybir as mybir
from concourse.bass_utils import run_bass_kernel_spmd

F32 = mybir.dt.float32
BF16 = mybir.dt.bfloat16
AF = mybir.ActivationFunctionType
ALU = mybir.AluOpType

B = 4
T = 4096
D = 1024
LT = 256
NC = 8
EPS = 1e-5


def build_program():
    nc = bass.Bass("TRN2", target_bir_lowering=False, debug=False,
                   detect_race_conditions=False)

    d_xkv = nc.dram_tensor("xkv", [B, D, 512], F32, kind="ExternalInput").ap()
    d_xq = nc.dram_tensor("xq", [B, D, 512], F32, kind="ExternalInput").ap()
    d_wq = nc.dram_tensor("wq", [D, D], F32, kind="ExternalInput").ap()
    d_wdkv = nc.dram_tensor("wdkv", [D, LT], F32, kind="ExternalInput").ap()
    d_wuk = nc.dram_tensor("wuk", [LT, D], F32, kind="ExternalInput").ap()
    d_wuv = nc.dram_tensor("wuv", [LT, D], F32, kind="ExternalInput").ap()
    d_wo = nc.dram_tensor("wo", [D, D], F32, kind="ExternalInput").ap()
    d_fwT = nc.dram_tensor("fwT", [8, D, D], F32, kind="ExternalInput").ap()
    d_bff = nc.dram_tensor("bff", [128, 32], F32, kind="ExternalInput").ap()
    d_incol = nc.dram_tensor("incol", [128, 8], F32, kind="ExternalInput").ap()
    d_gcol = nc.dram_tensor("gcol", [128, 2], F32, kind="ExternalInput").ap()
    d_bcol = nc.dram_tensor("bcol", [128, 2], F32, kind="ExternalInput").ap()
    d_ropek = nc.dram_tensor("ropek", [4, 128, 1024], F32, kind="ExternalInput").ap()
    d_ropeq = nc.dram_tensor("ropeq", [4, 128, 1024], F32, kind="ExternalInput").ap()
    d_maskb = nc.dram_tensor("maskb", [128, 128], BF16, kind="ExternalInput").ap()
    d_idt = nc.dram_tensor("idt", [128, 128], F32, kind="ExternalInput").ap()

    d_kvsh = nc.dram_tensor("kvsh", [B, 8, 2, 128, 4, 128], BF16).ap()
    d_kvfull = nc.dram_tensor("kvfull", [B, NC, 8, 2, 128, 4, 128], BF16,
                              addr_space="Shared").ap()
    d_stsh = nc.dram_tensor("stsh", [B, 128], F32).ap()
    d_stfull = nc.dram_tensor("stfull", [B, NC, 128], F32).ap()
    d_fcache = nc.dram_tensor("fcache", [8, 8, 128, 8, 128], BF16).ap()
    d_xmid32 = nc.dram_tensor("xmid32", [B, 8, 128, 512], F32).ap()
    d_xmid16 = nc.dram_tensor("xmid16", [B, 8, 128, 512], BF16).ap()
    d_yT = nc.dram_tensor("yT", [B, 8, 128, 512], F32, kind="ExternalOutput").ap()

    ctx = ExitStack()
    sb = lambda name, shape, dt: ctx.enter_context(nc.sbuf_tensor(name, shape, dt))
    ps = lambda name, shape: ctx.enter_context(nc.psum_tensor(name, shape, F32))

    on_mat = sb("on_mat", [128, 64], BF16)
    on_row = sb("on_row", [1, 128], F32)
    on_col = sb("on_col", [128, 1], F32)
    w_o = sb("w_o", [128, 8, 1024], BF16)
    bffs = sb("bffs", [128, 32], F32)
    incs = sb("incs", [128, 8], F32)
    gcs = sb("gcs", [128, 2], F32)
    bcs = sb("bcs", [128, 2], F32)
    masks = sb("masks", [128, 2, 2, 32], BF16)
    biasB = sb("biasB", [128, 32], F32)
    Mq = sb("Mq", [128, 32], F32)
    Mk = sb("Mk", [128, 4], F32)
    MqR = sb("MqR", [128, 32], F32)
    MkR = sb("MkR", [128, 4], F32)
    sqMq = sb("sqMq", [128, 32], F32)
    kmb = sb("kmb", [128, 4], F32)
    strd = sb("strd", [1, 64], F32)
    kmaxg = sb("kmaxg", [1, 8], F32)
    mrow = sb("mrow", [1, 40], F32)
    idt = sb("idt_s", [128, 128], F32)
    lnscr = sb("lnscr", [1, 1024], F32)
    tmpa = sb("tmpa", [128, 512], F32)
    tmpb = sb("tmpb", [128, 512], F32)
    cvals = sb("cvals", [128, 2], F32)
    zrow = sb("zrow", [1, 512], BF16)
    on_row16 = sb("on_row16", [1, 128], BF16)
    on_col16 = sb("on_col16", [128, 1], BF16)
    qhat = sb("qhat", [128, B, 8, 4, 128], BF16)

    bctx = ExitStack()
    bsb = lambda name, shape, dt: bctx.enter_context(nc.sbuf_tensor(name, shape, dt))
    w_q = bsb("w_q", [128, 8, 1024], BF16)
    w_dkv = bsb("w_dkv", [128, 8, 256], BF16)
    w_uk = bsb("w_uk", [128, 2, 1024], BF16)
    w_uv = bsb("w_uv", [128, 2, 1024], BF16)
    xkv_s = bsb("xkv_s", [128, 2, 8, 512], BF16)
    xq_s = bsb("xq_s", [128, 2, 8, 512], BF16)
    ltf = bsb("ltf", [128, 2, 512], F32)
    ltsq = bsb("ltsq", [128, 2, 512], F32)
    latn = bsb("latn", [128, 2, 2, 512], BF16)
    ropbk = bsb("ropbk", [128, 4, 1024], F32)
    ropbq = bsb("ropbq", [128, 4, 1024], F32)
    krop = bsb("krop", [128, 4, 1024], BF16)
    qrop = bsb("qrop", [128, 4, 1024], BF16)
    kts = bsb("kts", [128, 4, 1024], BF16)
    vst = bsb("vst", [128, 4, 1024], BF16)
    sqk = bsb("sqk", [128, 1024], F32)
    sqq = bsb("sqq", [128, 1024], F32)
    n16 = bsb("n16", [128, 16], F32)
    n8 = bsb("n8", [128, 8], F32)
    ncol = bsb("ncol", [128, 4], F32)

    PS = ps("PS", [128, 3072])
    PO = ps("PO", [128, 512])
    PD = ps("PD", [128, 512])

    SEMNAMES = [
        "dW",
        "dWa",
        "dX0", "dX1",
        "dC",
        "dT0", "dT1",
        "dK0", "dK1", "dK2", "dK3",
        "dQ0", "dQ1", "dQ2", "dQ3",
        "dH0", "dH1", "dH2", "dH3",
        "dSS",
        "dV0", "dV1",
        "dR0", "dR1",
        "dN0", "dN1", "dN2", "dN3",
        "dM",
        "dO0", "dO1",
        "dF0", "dF1",
        "P", "A", "V", "G", "CC",
    ]
    SEMS = {}
    for nm in SEMNAMES:
        SEMS[nm] = ctx.enter_context(nc.semaphore("s_" + nm))
    C = {nm: 0 for nm in SEMNAMES}
    lastw = {}

    def wt(eng, ename, key, val):
        if val <= 0:
            return
        if lastw.get((ename, key), -1) >= val:
            return
        eng.wait_ge(SEMS[key], val)
        lastw[(ename, key)] = val

    def dma(eng, key, out, in_, transpose=False):
        if transpose:
            eng.dma_start(out=out, in_=in_, transpose=True).then_inc(SEMS[key], 16)
        else:
            eng.dma_start(out=out, in_=in_).then_inc(SEMS[key], 16)
        C[key] += 16
        return C[key]

    block = ctx.enter_context(nc.Block())

    SCALAR_OPS = []

    def defer_scalar(fn):
        SCALAR_OPS.append(fn)
        return None

    @block.gpsimd
    def _(g):
        g.memset(on_mat[:, :], 1.0)
        g.memset(on_row[:, :], 1.0)
        g.memset(on_col[:, :], 1.0)
        g.memset(Mk[:, :], 0.0)
        g.memset(Mq[:, :], 0.0)
        g.memset(cvals[:, 0:1], EPS)
        g.memset(cvals[:, 1:2], 1e-30)
        g.memset(zrow[:, :], 0.0)
        g.memset(on_row16[:, :], 1.0)
        g.memset(on_col16[:, :], 1.0).then_inc(SEMS["G"], 1)
        C["G"] += 1
        for dst, src in (
            (w_dkv[:, :, :], d_wdkv), (w_q[:, :, :], d_wq),
            (w_uk[:, :, :], d_wuk), (w_uv[:, :, :], d_wuv),
        ):
            dma(g, "dWa", dst, src.rearrange("(ic p) d -> p ic d", p=128))
        dma(g, "dW", w_o[:, :, :], d_wo.rearrange("(ic p) d -> p ic d", p=128))
        dma(g, "dW", masks[:, :, :, :], d_maskb.rearrange("p (par two c) -> p par two c", par=2, c=32))
    WA_DONE = C["dWa"]

    @block.sync
    def _(s):
        dma(s, "dC", idt[:, :], d_idt)
        dma(s, "dC", bffs[:, :], d_bff)
        dma(s, "dC", incs[:, :], d_incol)
        dma(s, "dC", gcs[:, :], d_gcol)
        dma(s, "dC", bcs[:, :], d_bcol)
        dma(s, "dC", ropbk[:, :, :], d_ropek.transpose([1, 0, 2]))
        dma(s, "dC", ropbq[:, :, :], d_ropeq.transpose([1, 0, 2]))
    CONST_DONE = C["dC"]

    @block.gpsimd
    def _(g):
        dma(g, "dX0", xkv_s[:, 0, :, :], d_xkv[0].rearrange("(ic p) t -> p ic t", p=128))
        dma(g, "dX0", xq_s[:, 0, :, :], d_xq[0].rearrange("(ic p) t -> p ic t", p=128))
    XLOAD = {0: ("dX0", C["dX0"])}

    slot_units = {}
    lat_units = {}
    lat_norm = 0
    ph2_V = {}
    SQMQ_B = {}
    CC_STATS = {}
    CC_KV = {}
    BIAS_B = {}

    mu_s = lnscr[0:1, 0:512]
    var_s = lnscr[0:1, 512:1024]
    m2_s = tmpa[0:1, 0:512]
    sd_s = tmpb[0:1, 0:512]

    def emit_lat_chain(b):
        bb = b % 2

        @block.tensor
        def _(t, b=b, bb=bb):
            wt(t, "t", "dWa", WA_DONE)
            wt(t, "t", "G", 1)
            wt(t, "t", XLOAD[b][0], XLOAD[b][1])
            if b > 0:
                wt(t, "t", "V", lat_units[b - 1]["norm_done"])
            for lh in range(2):
                for ic in range(8):
                    t.matmul(PS[:, 1024 + lh * 512:1024 + (lh + 1) * 512],
                             w_dkv[:, ic, lh * 128:(lh + 1) * 128],
                             xkv_s[:, bb, ic, :],
                             start=(ic == 0), stop=(ic == 7)).then_maybe_inc(
                        (SEMS["P"], 1) if (lh == 1 and ic == 7) else None)
            pass
            C["P"] += 1
        lat_mm = C["P"]

        @block.vector
        def _(v, lat_mm=lat_mm):
            wt(v, "v", "P", lat_mm)
            v.tensor_copy(ltf[:, 0, :], PS[:, 1024:1536])
            v.tensor_copy(ltf[:, 1, :], PS[:, 1536:2048]).then_inc(SEMS["V"], 1)
            C["V"] += 1

        @block.vector
        def _(v):
            v.tensor_mul(ltsq[:, 0, :], ltf[:, 0, :], ltf[:, 0, :])
            v.tensor_mul(ltsq[:, 1, :], ltf[:, 1, :],
                         ltf[:, 1, :]).then_inc(SEMS["V"], 1)
            C["V"] += 1
        lat_sq = C["V"]

        @block.tensor
        def _(t, lat_sq=lat_sq):
            wt(t, "t", "V", lat_sq)
            for lh in range(2):
                t.matmul(PS[0:1, 1024:1536], on_col[:, 0:1], ltf[:, lh, :],
                         start=(lh == 0), stop=(lh == 1))
            for lh in range(2):
                t.matmul(PS[0:1, 1536:2048], on_col[:, 0:1], ltsq[:, lh, :],
                         start=(lh == 0), stop=(lh == 1)).then_maybe_inc(
                    (SEMS["P"], 1) if lh == 1 else None)
            pass
            C["P"] += 1
        lat_stats = C["P"]

        @block.vector
        def _(v, lat_stats=lat_stats):
            wt(v, "v", "P", lat_stats)
            v.tensor_scalar(mu_s, PS[0:1, 1024:1536], 1.0 / LT, None, ALU.mult)
            v.tensor_scalar(var_s, PS[0:1, 1536:2048], 1.0 / LT, None, ALU.mult)
            v.tensor_mul(m2_s, mu_s, mu_s)
            v.tensor_sub(var_s, var_s, m2_s).then_inc(SEMS["V"], 1)
            C["V"] += 1
        lat_v1 = C["V"]

        @defer_scalar
        def _(a, lat_v1=lat_v1):
            wt(a, "a", "V", lat_v1)
            a.activation(sd_s, var_s, AF.Ln, bias=cvals[0:1, 0:1])
            a.activation(var_s, sd_s, AF.Exp, scale=-0.5).then_inc(SEMS["A"], 1)
        C["A"] += 1
        lat_rstd = C["A"]

        @block.tensor
        def _(t, lat_rstd=lat_rstd):
            wt(t, "t", "A", lat_rstd)
            t.matmul(PS[:, 1024:1536], on_row[0:1, 0:128], mu_s, start=True, stop=True)
            t.matmul(PS[:, 1536:2048], on_row[0:1, 0:128], var_s, start=True,
                     stop=True).then_inc(SEMS["P"], 1)
            C["P"] += 1
        lat_bc = C["P"]

        @block.vector
        def _(v, bb=bb, lat_bc=lat_bc):
            wt(v, "v", "P", lat_bc)
            wt(v, "v", "dC", CONST_DONE)
            for lh in range(2):
                v.tensor_sub(tmpa[:, :], ltf[:, lh, :], PS[:, 1024:1536])
                v.tensor_mul(tmpb[:, :], tmpa[:, :], PS[:, 1536:2048])
                v.tensor_scalar(latn[:, bb, lh, :], tmpb[:, :],
                                gcs[:, lh:lh + 1], bcs[:, lh:lh + 1],
                                ALU.mult, ALU.add).then_maybe_inc(
                    (SEMS["V"], 1) if lh == 1 else None)
            pass
            C["V"] += 1
        lat_units[b] = {"norm_done": C["V"]}

    for b in range(B):
        bb = b % 2

        if b + 1 < B:
            @block.gpsimd
            def _(g, b=b):
                nb = (b + 1) % 2
                key = "dX" + str(nb)
                if b + 1 >= 2:
                    wt(g, "g", "P", slot_units[(b - 1, 3)]["q_mm"])
                dma(g, key, xkv_s[:, nb, :, :],
                    d_xkv[b + 1].rearrange("(ic p) t -> p ic t", p=128))
                dma(g, key, xq_s[:, nb, :, :],
                    d_xq[b + 1].rearrange("(ic p) t -> p ic t", p=128))
                XLOAD[b + 1] = (key, C[key])

        if b == 0:
            emit_lat_chain(0)
        lat_norm = lat_units[b]["norm_done"]

        for s_ in range(4):
            par = s_
            prev = slot_units.get((b, s_ - 1)) if s_ > 0 else \
                (slot_units.get((b - 1, 3)) if b > 0 else None)
            prev2 = slot_units.get((b - 1, s_)) if b > 0 else None

            @block.tensor
            def _(t, bb=bb, s_=s_, prev=prev, lat_norm=lat_norm):
                wt(t, "t", "V", lat_norm)
                if prev is not None:
                    wt(t, "t", "V", prev["vevac_done"])
                for dh in range(2):
                    for lh in range(2):
                        t.matmul(PS[:, dh * 512:(dh + 1) * 512],
                                 latn[:, bb, lh, s_ * 128:s_ * 128 + 128],
                                 w_uk[:, lh, dh * 512:(dh + 1) * 512],
                                 start=(lh == 0), stop=(lh == 1))
                for dh in range(2):
                    dst = PO if dh == 0 else PD
                    for lh in range(2):
                        t.matmul(dst[:, :],
                                 latn[:, bb, lh, s_ * 128:s_ * 128 + 128],
                                 w_uv[:, lh, dh * 512:(dh + 1) * 512],
                                 start=(lh == 0), stop=(lh == 1)).then_maybe_inc(
                            (SEMS["P"], 1) if (dh == 1 and lh == 1) else None)
                pass
                C["P"] += 1
            kv_mm = C["P"]

            @block.tensor
            def _(t, b=b, bb=bb, s_=s_, prev=prev):
                if prev is not None:
                    wt(t, "t", "V", prev["ropeq_done"])
                if s_ == 0 and b > 0:
                    wt(t, "t", "V", ph2_V[b - 1])
                for dh in range(2):
                    for ic in range(8):
                        t.matmul(PS[:, 2048 + dh * 512:2048 + (dh + 1) * 512],
                                 xq_s[:, bb, ic, s_ * 128:s_ * 128 + 128],
                                 w_q[:, ic, dh * 512:(dh + 1) * 512],
                                 start=(ic == 0), stop=(ic == 7)).then_maybe_inc(
                            (SEMS["P"], 1) if (dh == 1 and ic == 7) else None)
                pass
                C["P"] += 1
            q_mm = C["P"]

            def rope_emit(v, psoff, tblbuf, tbl, dst, dstbuf):
                ap = PS[:, psoff:psoff + 1024].rearrange(
                    "p (h two q) -> p h two q", h=16, two=2)
                x1 = ap[:, :, 0, :]
                x2 = ap[:, :, 1, :]
                cos = tbl[:, tblbuf, 0:512].rearrange("p (h q) -> p h q", h=16)
                sin = tbl[:, tblbuf, 512:1024].rearrange("p (h q) -> p h q", h=16)
                ta = tmpa[:, :].rearrange("p (h q) -> p h q", h=16)
                tb = tmpb[:, :].rearrange("p (h q) -> p h q", h=16)
                dap = dst[:, dstbuf, :].rearrange("p (h two q) -> p h two q", h=16, two=2)
                v.tensor_mul(ta, x1, cos)
                v.tensor_mul(tb, x2, sin)
                v.tensor_sub(dap[:, :, 0, :], ta, tb)
                v.tensor_mul(ta, x1, sin)
                v.tensor_mul(tb, x2, cos)
                return v.tensor_add(dap[:, :, 1, :], ta, tb)

            @block.vector
            def _(v, s_=s_, par=par, kv_mm=kv_mm, prev2=prev2):
                wt(v, "v", "P", kv_mm)
                wt(v, "v", "dC", CONST_DONE)
                if prev2 is not None:
                    wt(v, "v", "dK" + str(par), prev2["ktr_done"])
                rope_emit(v, 0, s_, ropbk, krop, par).then_inc(SEMS["V"], 1)
                C["V"] += 1
            ropek_done = C["V"]

            @block.vector
            def _(v, par=par, prev2=prev2):
                if prev2 is not None:
                    wt(v, "v", "dH" + str(par), prev2["shard_done"])
                v.tensor_copy(vst[:, par, 0:512], PO[:, :])
                v.tensor_copy(vst[:, par, 512:1024], PD[:, :]).then_inc(SEMS["V"], 1)
                C["V"] += 1
            vevac_done = C["V"]

            @block.vector
            def _(v, s_=s_, par=par, q_mm=q_mm, prev2=prev2):
                wt(v, "v", "P", q_mm)
                if prev2 is not None:
                    wt(v, "v", "dQ" + str(par), prev2["qtr_done"])
                rope_emit(v, 2048, s_, ropbq, qrop, par).then_inc(SEMS["V"], 1)
                C["V"] += 1
            ropeq_done = C["V"]

            @block.vector
            def _(v, par=par):
                v.tensor_mul(sqk[:, :], krop[:, par, :], krop[:, par, :]).then_inc(SEMS["V"], 1)
                C["V"] += 1
                v.tensor_mul(sqq[:, :], qrop[:, par, :], qrop[:, par, :]).then_inc(SEMS["V"], 1)
                C["V"] += 1
            sq_done = C["V"]

            @block.vector
            def _(v, b=b, sq_done=sq_done):
                v.reduce_sum(n16[:, :], sqk[:, :].rearrange("p (h q) -> p h q", h=16),
                             axis=mybir.AxisListType.X)
                v.reduce_max(ncol[:, 0:1], n16[:, :], axis=mybir.AxisListType.X)
                v.tensor_max(Mk[:, b:b + 1], Mk[:, b:b + 1], ncol[:, 0:1])
                v.reduce_sum(n16[:, :], sqq[:, :].rearrange("p (h q) -> p h q", h=16),
                             axis=mybir.AxisListType.X)
                v.reduce_max(n8[:, :], n16[:, :].rearrange("p (pr two) -> p pr two", two=2),
                             axis=mybir.AxisListType.X)
                v.tensor_max(Mq[:, b * 8:(b + 1) * 8], Mq[:, b * 8:(b + 1) * 8],
                             n8[:, :]).then_inc(SEMS["V"], 1)
                C["V"] += 1
            norms_done = C["V"]

            @block.sync
            def _(s, par=par, ropek_done=ropek_done, prev2=prev2):
                wt(s, "s", "V", ropek_done)
                if prev2 is not None:
                    wt(s, "s", "dH" + str(par), prev2["shard_done"])
                for pr in range(8):
                    dma(s, "dK" + str(par), kts[:, par, pr * 128:(pr + 1) * 128],
                        krop[:, par, pr * 128:(pr + 1) * 128], transpose=True)
            ktr_done = C["dK" + str(par)]

            @block.sync
            def _(s, b=b, s_=s_, par=par, ropeq_done=ropeq_done):
                wt(s, "s", "V", ropeq_done)
                for pr in range(8):
                    dma(s, "dQ" + str(par), qhat[:, b, pr, s_, :],
                        qrop[:, par, pr * 128:(pr + 1) * 128], transpose=True)
            qtr_done = C["dQ" + str(par)]

            @block.gpsimd
            def _(g, b=b, s_=s_, par=par, ktr_done=ktr_done, vevac_done=vevac_done):
                wt(g, "g", "dK" + str(par), ktr_done)
                dma(g, "dH" + str(par), d_kvsh[b, :, 0, :, s_, :].transpose([1, 0, 2]),
                    kts[:, par, :].rearrange("p (pr c) -> p pr c", c=128))
                wt(g, "g", "V", vevac_done)
                dma(g, "dH" + str(par), d_kvsh[b, :, 1, :, s_, :].transpose([1, 0, 2]),
                    vst[:, par, :].rearrange("p (pr c) -> p pr c", c=128))
            shard_done = C["dH" + str(par)]

            slot_units[(b, s_)] = dict(
                ropek_done=ropek_done, ropeq_done=ropeq_done, vevac_done=vevac_done,
                norms_done=norms_done, ktr_done=ktr_done, qtr_done=qtr_done,
                shard_done=shard_done, q_mm=q_mm,
            )

            if s_ == 2 and b + 1 < B:
                emit_lat_chain(b + 1)

        nd = slot_units[(b, 3)]["norms_done"]

        @block.tensor
        def _(t, b=b, nd=nd):
            wt(t, "t", "V", nd)
            wt(t, "t", "dC", CONST_DONE)
            t.transpose(PS[0:8, 2048:2176], Mq[:, 8 * b:8 * b + 8], idt[:, :])
            t.transpose(PS[0:1, 2176:2304], Mk[:, b:b + 1],
                        idt[:, :]).then_inc(SEMS["P"], 1)
            C["P"] += 1
        tr1 = C["P"]

        @block.vector
        def _(v, b=b, tr1=tr1):
            wt(v, "v", "P", tr1)
            v.reduce_max(tmpa[0:8, 0:1], PS[0:8, 2048:2176], axis=mybir.AxisListType.X)
            v.reduce_max(tmpb[0:1, 0:1], PS[0:1, 2176:2304],
                         axis=mybir.AxisListType.X)
            v.tensor_copy(mrow[0:1, 32 + b:33 + b],
                          tmpb[0:1, 0:1]).then_inc(SEMS["V"], 1)
            C["V"] += 1
        rmx = C["V"]

        @block.tensor
        def _(t, rmx=rmx):
            wt(t, "t", "V", rmx)
            t.transpose(PS[0:1, 2560:2568], tmpa[0:8, 0:1],
                        idt[0:8, 0:8]).then_inc(SEMS["P"], 1)
            C["P"] += 1
        tr2 = C["P"]

        @block.vector
        def _(v, tr2=tr2):
            wt(v, "v", "P", tr2)
            v.tensor_copy(mrow[0:1, 0:8], PS[0:1, 2560:2568]).then_inc(SEMS["V"], 1)
            C["V"] += 1
        cpy = C["V"]

        @block.tensor
        def _(t, cpy=cpy):
            wt(t, "t", "V", cpy)
            t.matmul(PS[:, 2568:2576], on_row[0:1, 0:128], mrow[0:1, 0:8],
                     start=True, stop=True).then_inc(SEMS["P"], 1)
            C["P"] += 1
        bcm = C["P"]

        @block.vector
        def _(v, b=b, bcm=bcm):
            wt(v, "v", "P", bcm)
            v.tensor_copy(MqR[:, 8 * b:8 * b + 8],
                          PS[:, 2568:2576]).then_inc(SEMS["V"], 1)
            C["V"] += 1
        ph2_V[b] = C["V"]

        @defer_scalar
        def _(a, b=b, pv=ph2_V[b]):
            wt(a, "a", "V", pv)
            a.activation(sqMq[:, 8 * b:8 * b + 8], MqR[:, 8 * b:8 * b + 8],
                         AF.Ln, bias=cvals[:, 1:2])
            a.activation(sqMq[:, 8 * b:8 * b + 8], sqMq[:, 8 * b:8 * b + 8],
                         AF.Exp, scale=0.5).then_inc(SEMS["A"], 1)
        C["A"] += 1
        SQMQ_B[b] = C["A"]

        @block.gpsimd
        def _(g, b=b, rmx=rmx,
              kvdh=tuple((k, C[k]) for k in ("dH0", "dH1", "dH2", "dH3"))):
            wt(g, "g", "V", rmx)
            dma(g, "dSS", d_stsh[b:b + 1, 1:128], on_row[0:1, 1:128])
            dma(g, "dSS", d_stsh[b:b + 1, 0:1], mrow[0:1, 32 + b:33 + b])
            wt(g, "g", "dSS", C["dSS"])
            g.collective_compute(
                "AllGather", ALU.bypass, replica_groups=[list(range(NC))],
                ins=[d_stsh[b:b + 1, :]], outs=[d_stfull[b]],
            ).then_inc(SEMS["CC"], 1)
            C["CC"] += 1
            if b == 0:
                for hk, hv in kvdh:
                    wt(g, "g", hk, hv)
                g.collective_compute(
                    "AllGather", ALU.bypass, replica_groups=[list(range(NC))],
                    ins=[d_kvsh[0]], outs=[d_kvfull[0]],
                ).then_inc(SEMS["CC"], 1)
                C["CC"] += 1
        CC_STATS[b] = C["CC"] - (1 if b == 0 else 0)
        if b == 0:
            CC_KV[b] = C["CC"]

            @block.gpsimd
            def _(g):
                for fm in range(8):
                    for foc in range(8):
                        dma(g, "dW", d_fcache[fm, foc],
                            d_fwT[fm].rearrange("(ic p) (oc c) -> oc p ic c",
                                                p=128, c=128)[foc])

    @block.gpsimd
    def _(g):
        for hk in ("dH0", "dH1", "dH2", "dH3"):
            wt(g, "g", hk, C[hk])
        for bq in range(1, B):
            g.collective_compute(
                "AllGather", ALU.bypass, replica_groups=[list(range(NC))],
                ins=[d_kvsh[bq]], outs=[d_kvfull[bq]],
            ).then_inc(SEMS["CC"], 1)
            C["CC"] += 1
            CC_KV[bq] = C["CC"]

    W_DONE = C["dW"]
    LAST_NORMS = slot_units[(B - 1, 3)]["norms_done"]

    kmb_done = 0
    for b in range(B):
        @block.sync
        def _(s, b=b, cs=CC_STATS[b]):
            wt(s, "s", "CC", cs)
            s.dma_start(out=strd[0:1, b * 8:b * 8 + 8],
                        in_=d_stfull[b, :, 0:1].transpose([1, 0]),
                        allow_slow_non_contiguous=True).then_inc(SEMS["dC"], 16)
            C["dC"] += 16

        @block.vector
        def _(v, b=b, sl=C["dC"]):
            wt(v, "v", "dC", sl)
            v.reduce_max(kmaxg[0:1, b:b + 1], strd[0:1, b * 8:b * 8 + 8],
                         axis=mybir.AxisListType.X).then_inc(SEMS["V"], 1)
            C["V"] += 1
        kmg = C["V"]

        @defer_scalar
        def _(a, b=b, kmg=kmg):
            wt(a, "a", "V", kmg)
            a.activation(kmaxg[0:1, 4 + b:5 + b], kmaxg[0:1, b:b + 1],
                         AF.Ln, bias=cvals[0:1, 1:2])
            a.activation(kmaxg[0:1, b:b + 1], kmaxg[0:1, 4 + b:5 + b],
                         AF.Exp, scale=0.5).then_inc(SEMS["A"], 1)
        C["A"] += 1
        ksq = C["A"]

        @block.tensor
        def _(t, b=b, ksq=ksq, pk=kmb_done):
            wt(t, "t", "A", ksq)
            if b == 0:
                wt(t, "t", "V", ph2_V[B - 1])
            if pk:
                wt(t, "t", "V", pk)
            t.matmul(PS[:, 1024 + b:1025 + b], on_row[0:1, 0:128],
                     kmaxg[0:1, b:b + 1], start=True,
                     stop=True).then_inc(SEMS["P"], 1)
            C["P"] += 1
        kmm = C["P"]

        @block.vector
        def _(v, b=b, kmm=kmm, sq=SQMQ_B[b]):
            wt(v, "v", "P", kmm)
            v.tensor_copy(kmb[:, b:b + 1], PS[:, 1024 + b:1025 + b])
            wt(v, "v", "A", sq)
            v.tensor_scalar(biasB[:, b * 8:(b + 1) * 8], sqMq[:, b * 8:(b + 1) * 8],
                            kmb[:, b:b + 1], -1.0, ALU.mult,
                            ALU.mult).then_inc(SEMS["V"], 1)
            C["V"] += 1
        kmb_done = C["V"]
        BIAS_B[b] = C["V"]
    BIAS_DONE = BIAS_B[B - 1]

    BAR = dict(C)

    def barrier_waits(eng, ename):
        for k in SEMNAMES:
            if k != "CC":
                wt(eng, ename, k, BAR[k])

    @block.tensor
    def _(t):
        barrier_waits(t, "t")

    @block.vector
    def _(v):
        barrier_waits(v, "v")

    @defer_scalar
    def _(a):
        barrier_waits(a, "a")

    @block.sync
    def _(s):
        barrier_waits(s, "s")

    @block.gpsimd
    def _(g):
        barrier_waits(g, "g")

    bctx.close()

    kvK = sb("kvK", [128, 2, 32, 128], BF16)
    kvV = sb("kvV", [128, 2, 32, 128], BF16)
    pt = sb("pt", [128, 3, 1024], BF16)
    attnT = sb("attnT", [128, 8, 512], BF16)
    rcp = sb("rcp", [128, 512], F32)
    xrc = sb("xrc", [128, 2, 512], F32)
    rbuf = sb("rbuf", [128, 8, 512], F32)
    rsq = sb("rsq", [128, 512], F32)
    rbuf16 = sb("rbuf16", [128, 8, 512], BF16)
    rsq16 = sb("rsq16", [128, 2, 512], BF16)
    nrm = sb("nrm", [128, 4, 512], F32)
    xmt = sb("xmt", [128, 8, 512], BF16)
    xm32c = sb("xm32c", [128, 2, 512], F32)
    cT32 = sb("cT32", [128, 8, 512], F32)
    cT16 = sb("cT16", [128, 2, 8, 512], BF16)
    fwc = sb("fwc", [128, 2, 2, 8, 128], BF16)
    gf = sb("gf", [128, 512], BF16)
    gi = sb("gi", [128, 512], BF16)
    gg = sb("gg", [128, 512], BF16)
    hc = sb("hc", [128, 2, 512], F32)

    @defer_scalar
    def _(a):
        wt(a, "a", "dC", CONST_DONE)
        for j in range(8):
            a.activation(cT32[:, j, :], cT32[:, j, :], AF.Identity,
                         bias=incs[:, j:j + 1], scale=0.0).then_maybe_inc(
                (SEMS["A"], 1) if j == 7 else None)
        pass
    C["A"] += 1
    CINIT_A = C["A"]

    @block.vector
    def _(v):
        wt(v, "v", "A", CINIT_A)
        for j in range(8):
            v.tensor_copy(cT16[:, 0, j, :], cT32[:, j, :]).then_maybe_inc(
                (SEMS["V"], 1) if j == 7 else None)
        pass
        C["V"] += 1
    cready = C["V"]

    FCHUNKS = []
    for _t in range(B):
        for _mo in range(8):
            for _g in range(3):
                FCHUNKS.append((2 * _g, 2 * _g + 1, _mo))
    for _t in range(B):
        for _oc in range(8):
            FCHUNKS.append((6, 7, _oc))
    chunk_mm = {}
    chunk_ld = {}
    chunk_act = {}
    cupd_last = {-1: cready}
    ffn_state = {"last_cupd": cready}

    def emit_fwc_load(ci):
        if ci >= len(FCHUNKS):
            return
        wm, um, oc_ = FCHUNKS[ci]
        buf = ci % 2
        key = "dF" + str(buf)

        @block.gpsimd
        def _(g, ci=ci, wm=wm, um=um, oc_=oc_, buf=buf, key=key):
            wt(g, "g", "dW", W_DONE)
            if ci - 2 in chunk_mm:
                wt(g, "g", "P", chunk_mm[ci - 2])
            dma(g, key, fwc[:, buf, 0, :, :], d_fcache[wm, oc_])
            dma(g, key, fwc[:, buf, 1, :, :], d_fcache[um, oc_])
        chunk_ld[ci] = (key, C[key])

    emit_fwc_load(0)
    emit_fwc_load(1)

    def emit_xmt_load(step):
        @block.sync
        def _(s, step=step):
            for dk in ("dN0", "dN1", "dN2", "dN3"):
                wt(s, "s", dk, C[dk])
            if step > 0:
                wt(s, "s", "P", chunk_mm[step * 24 - 1])
            dma(s, "dM", xmt[:, :, :], d_xmid16[step].transpose([1, 0, 2]))
        return C["dM"]

    def emit_gate_part(ci, step, mo, gg_i, part, xmt_ld, extraV=0):
        buf = ci % 2
        bank = 4 + (ci % 2)
        func = (AF.Sigmoid, AF.Sigmoid, AF.Tanh)[gg_i]

        @block.tensor
        def _(t, ci=ci, buf=buf, bank=bank, xmt_ld=xmt_ld, step=step,
              part=part, extraV=extraV, cw=cupd_last[step - 1]):
            if part == 0:
                wt(t, "t", chunk_ld[ci][0], chunk_ld[ci][1])
                wt(t, "t", "dM", xmt_ld)
                if extraV:
                    wt(t, "t", "V", extraV)
                if ci - 2 in chunk_act:
                    wt(t, "t", "A", chunk_act[ci - 2])
            elif part == 2:
                wt(t, "t", "V", cw)
            w_u = part // 2
            src = xmt if w_u == 0 else None
            for j in range(4):
                ic = (part % 2) * 4 + j
                rhs = xmt[:, ic, :] if w_u == 0 else cT16[:, step % 2, ic, :]
                t.matmul(PS[:, bank * 512:(bank + 1) * 512],
                         fwc[:, buf, w_u, ic, :], rhs,
                         start=(part == 0 and j == 0),
                         stop=(part == 3 and j == 3)).then_maybe_inc(
                    (SEMS["P"], 1) if (part == 3 and j == 3) else None)
            pass
            if part == 3:
                C["P"] += 1
        if part == 3:
            chunk_mm[ci] = C["P"]
            emit_fwc_load(ci + 2)

            @defer_scalar
            def _(a, bank=bank, gg_i=gg_i, mo=mo, func=func, gmm=chunk_mm[ci],
                  pcu=(ffn_state["last_cupd"] if gg_i == 0 else 0)):
                wt(a, "a", "P", gmm)
                if pcu:
                    wt(a, "a", "V", pcu)
                dstb = (gf, gi, gg)[gg_i]
                a.activation(dstb[:, :], PS[:, bank * 512:(bank + 1) * 512],
                             func, bias=bffs[:, gg_i * 8 + mo:gg_i * 8 + mo + 1]
                             ).then_inc(SEMS["A"], 1)
            C["A"] += 1
            chunk_act[ci] = C["A"]

    FFN_TILES = [3, 5, 8, 10, 13, 15, 18, 20, 23, 25, 28, 30]
    FFN_TILES_P0 = [8, 10, 12, 14, 17, 19, 22, 24, 26, 28, 29, 30]

    def emit_gate_unit(ci, step, mo, gg_i, xmt_ld, extraV=0):
        for part in range(4):
            emit_gate_part(ci, step, mo, gg_i, part, xmt_ld,
                           extraV=(extraV if part == 0 else 0))

    def emit_cupdate(step, mo):
        nxt = (step + 1) % 2

        @block.vector
        def _(v, mo=mo, nxt=nxt, gact=chunk_act[step * 24 + mo * 3 + 2]):
            wt(v, "v", "A", gact)
            v.tensor_mul(tmpa[:, :], gf[:, :], cT32[:, mo, :])
            v.tensor_mul(tmpb[:, :], gi[:, :], gg[:, :])
            v.tensor_add(cT32[:, mo, :], tmpa[:, :], tmpb[:, :])
            v.tensor_copy(cT16[:, nxt, mo, :],
                          cT32[:, mo, :]).then_inc(SEMS["V"], 1)
            C["V"] += 1
        ffn_state["last_cupd"] = C["V"]

    xmt_lds = {}

    def kv_prefetch(s, b, pair, buf):
        key = "dV" + str(buf)
        for c in range(NC):
            dma(s, key, kvK[:, buf, 4 * c:4 * (c + 1), :], d_kvfull[b, c, pair, 0])
            dma(s, key, kvV[:, buf, 4 * c:4 * (c + 1), :], d_kvfull[b, c, pair, 1])
        return (key, C[key])

    kvready = {}

    @block.sync
    def _(s):
        wt(s, "s", "CC", CC_KV[0])
        kvready[(0, 0)] = kv_prefetch(s, 0, 0, 0)

    prev_epilogue_V = 0
    prev_pvden_P = 0
    wo_done_P = 0
    attn_done_V = {}
    wo_radds = {}
    wo_nrm_V = {}
    exp_slot_wait = {0: 0, 1: 0}

    for b in range(B):
        for pair in range(8):
            bp = b * 8 + pair
            buf = bp % 2

            @block.sync
            def _(s, bp=bp):
                if bp + 1 < 32:
                    nb, npair = divmod(bp + 1, 8)
                    if bp >= 1:
                        wt(s, "s", "P", prev_pvden_P)
                    wt(s, "s", "CC", CC_KV[nb])
                    kvready[(nb, npair)] = kv_prefetch(s, nb, npair, (bp + 1) % 2)

            if b >= 1 and pair == 0:
                xmt_lds[b - 1] = emit_xmt_load(b - 1)

            P_sT = {}
            A_exp = {}
            V_mask = {}

            def emit_dummies(epiV=prev_epilogue_V):
                @block.tensor
                def _(t, epiV=epiV):
                    wt(t, "t", "V", epiV)
                    t.matmul(PO[:, :], on_row16[0:1, 0:128], zrow[0:1, :],
                             start=True, stop=True)
                    t.matmul(PD[:, :], on_row16[0:1, 0:128], zrow[0:1, :],
                             start=True, stop=True)

            def emit_pv(tt, buf=buf):
                slot = tt % 3
                n0 = 32 * (tt // 2)
                N = 512 - n0

                @block.tensor
                def _(t, buf=buf, tt=tt, slot=slot, n0=n0, N=N, vm=V_mask[tt]):
                    wt(t, "t", "V", vm)
                    t.matmul(PO[0:64, n0:512], kvV[:, buf, tt, 0:64],
                             pt[:, slot, 0:N], start=False, stop=False,
                             tile_position=(0, 0), skip_group_check=True)
                    t.matmul(PO[64:128, n0:512], kvV[:, buf, tt, 64:128],
                             pt[:, slot, 512:512 + N], start=False, stop=False,
                             tile_position=(0, 64), skip_group_check=True)
                    t.matmul(PD[0:64, n0:512], on_mat[:, :],
                             pt[:, slot, 0:N], start=False, stop=False,
                             tile_position=(0, 0), skip_group_check=True)
                    t.matmul(PD[64:128, n0:512], on_mat[:, :],
                             pt[:, slot, 512:512 + N], start=False, stop=False,
                             tile_position=(0, 64), skip_group_check=True
                             ).then_inc(SEMS["P"], 1)
                C["P"] += 1

            for tt in range(32):
                slot = tt % 2
                n0 = 32 * (tt // 2)
                N = 512 - n0
                if tt == 0:
                    emit_dummies()
                if tt >= 3:
                    emit_pv(tt - 3)
                ftl = FFN_TILES_P0 if pair == 0 else FFN_TILES
                if b >= 1 and tt in ftl:
                    idx = ftl.index(tt)
                    g = idx // 4
                    part = idx % 4
                    ci = (b - 1) * 24 + pair * 3 + g
                    emit_gate_part(ci, b - 1, pair, g, part, xmt_lds[b - 1],
                                   extraV=(wo_nrm_V[b - 1]
                                           if (pair == 0 and g == 0 and part == 0)
                                           else 0))

                @block.tensor
                def _(t, b=b, pair=pair, buf=buf, tt=tt, slot=slot, n0=n0, N=N,
                      ew=exp_slot_wait[slot],
                      wonrm=(wo_radds[b - 1][7] if (pair == 0 and b > 0) else None)):
                    if tt == 0:
                        wt(t, "t", kvready[(b, pair)][0], kvready[(b, pair)][1])
                        wt(t, "t", "V", BIAS_B[b])
                        if wonrm is not None:
                            wt(t, "t", "V", wonrm)
                    if ew:
                        wt(t, "t", "A", ew)
                    qfe = qhat[0:64, b, pair, :, :].rearrange(
                        "p a c -> p (a c)")[:, n0:512]
                    qfo = qhat[64:128, b, pair, :, :].rearrange(
                        "p a c -> p (a c)")[:, n0:512]
                    t.matmul(PS[:, slot * 1024:slot * 1024 + N],
                             kvK[0:64, buf, tt, :], qfe,
                             start=True, stop=True)
                    t.matmul(PS[:, slot * 1024 + 512:slot * 1024 + 512 + N],
                             kvK[64:128, buf, tt, :], qfo,
                             start=True, stop=True,
                             tile_position=(64, 0)).then_inc(SEMS["P"], 1)
                C["P"] += 1
                P_sT[tt] = C["P"]

                @defer_scalar
                def _(a, b=b, pair=pair, slot=slot, pslot=tt % 3, N=N, stp=P_sT[tt]):
                    wt(a, "a", "P", stp)
                    inap = PS[:, slot * 1024:(slot + 1) * 1024].rearrange(
                        "p (two n) -> p two n", two=2)[:, :, 0:N]
                    outap = pt[:, pslot, :].rearrange(
                        "p (two n) -> p two n", two=2)[:, :, 0:N]
                    a.activation(outap, inap, AF.Exp,
                                 bias=biasB[:, b * 8 + pair:b * 8 + pair + 1]
                                 ).then_inc(SEMS["A"], 1)
                C["A"] += 1
                A_exp[tt] = C["A"]
                exp_slot_wait[slot] = C["A"]

                @block.vector
                def _(v, tt=tt, pslot=tt % 3, N=N, ae=A_exp[tt]):
                    if tt == 0:
                        wt(v, "v", "dW", W_DONE)
                    wt(v, "v", "A", ae)
                    pap = bass.AP(pt.handle if hasattr(pt, "handle") else pt,
                                  pslot * 1024, [[3072, 128], [512, 2], [1, 32]])
                    v.tensor_mul(pap, pap, masks[:, tt % 2, :, :]).then_inc(
                        SEMS["V"], 1)
                C["V"] += 1
                V_mask[tt] = C["V"]

            emit_pv(29)
            emit_pv(30)
            emit_pv(31)
            prev_pvden_P = C["P"]

            @defer_scalar
            def _(a, pv=prev_pvden_P):
                wt(a, "a", "P", pv)
                a.activation(rsq[:, :], PD[:, :], AF.Ln, bias=cvals[:, 1:2])
                a.activation(rcp[:, :], rsq[:, :], AF.Exp, scale=-1.0).then_inc(SEMS["A"], 1)
            C["A"] += 1
            epi_a = C["A"]

            @block.vector
            def _(v, pair=pair, epi_a=epi_a, b=b, wod=wo_done_P):
                wt(v, "v", "A", epi_a)
                if b > 0:
                    wt(v, "v", "P", wod)
                v.tensor_mul(attnT[:, pair, :], PO[:, :], rcp[:, :]).then_inc(SEMS["V"], 1)
                C["V"] += 1
            prev_epilogue_V = C["V"]

            if b >= 1:
                emit_cupdate(b - 1, pair)
                if pair == 7:
                    cupd_last[b - 1] = C["V"]

        attn_done_V[b] = prev_epilogue_V

        @block.sync
        def _(s, b=b):
            if b > 0:
                wt(s, "s", "V", wo_radds[b - 1][6])
                wt(s, "s", "V", wo_radds[b - 1][7])
            dma(s, "dR0", xrc[:, 0, :], d_xq[b, 0:128, :])
            dma(s, "dR1", xrc[:, 1, :], d_xq[b, 128:256, :])
        xrc_loads = {0: ("dR0", C["dR0"]), 1: ("dR1", C["dR1"])}

        wo_units = []
        radds = {}
        for oc in range(8):
            bank = oc % 2

            @block.tensor
            def _(t, b=b, oc=oc, bank=bank):
                wt(t, "t", "V", attn_done_V[b])
                if oc >= 2:
                    wt(t, "t", "V", wo_units[oc - 2]["radd"])
                for cc in range(8):
                    t.matmul(PS[:, bank * 512:(bank + 1) * 512],
                             w_o[:, cc, oc * 128:(oc + 1) * 128],
                             attnT[:, cc, :],
                             start=(cc == 0), stop=(cc == 7)).then_maybe_inc(
                        (SEMS["P"], 1) if cc == 7 else None)
                pass
                C["P"] += 1
            womm = C["P"]

            @block.vector
            def _(v, oc=oc, bank=bank, womm=womm, xl=xrc_loads[oc]):
                wt(v, "v", "P", womm)
                wt(v, "v", xl[0], xl[1])
                v.tensor_add(rbuf[:, oc, :], PS[:, bank * 512:(bank + 1) * 512],
                             xrc[:, oc % 2, :]).then_inc(SEMS["V"], 1)
                C["V"] += 1
            radd = C["V"]
            radds[oc] = radd

            @block.sync
            def _(s, b=b, oc=oc, radd=radd):
                if oc + 2 < 8:
                    wt(s, "s", "V", radd)
                    key = "dR" + str(oc % 2)
                    dma(s, key, xrc[:, oc % 2, :], d_xq[b, (oc + 2) * 128:(oc + 3) * 128, :])
                    xrc_loads[oc + 2] = (key, C[key])

            @defer_scalar
            def _(a, oc=oc, radd=radd):
                wt(a, "a", "V", radd)
                a.activation(rbuf16[:, oc, :], rbuf[:, oc, :],
                             AF.Identity).then_inc(SEMS["A"], 1)
            C["A"] += 1
            rb16_a = C["A"]

            @block.vector
            def _(v, oc=oc, pstat=(wo_units[oc - 2]["stats"] if oc >= 2 else 0)):
                if pstat:
                    wt(v, "v", "P", pstat)
                v.tensor_mul(rsq16[:, oc % 2, :], rbuf[:, oc, :],
                             rbuf[:, oc, :]).then_inc(SEMS["V"], 1)
                C["V"] += 1
            rsq_a = C["V"]

            def wo_stats_emit(oc=oc, rb16_a=rb16_a, rsq_a=rsq_a):
                @block.tensor
                def _(t, oc=oc, rb16_a=rb16_a, rsq_a=rsq_a):
                    wt(t, "t", "A", rb16_a)
                    wt(t, "t", "V", rsq_a)
                    t.matmul(PD[0:1, 0:512], on_col16[:, 0:1], rbuf16[:, oc, :],
                             start=(oc == 0), stop=(oc == 7))
                    t.matmul(PO[0:1, 0:512], on_col16[:, 0:1], rsq16[:, oc % 2, :],
                             start=(oc == 0), stop=(oc == 7)).then_inc(SEMS["P"], 1)
                    C["P"] += 1
            wo_units.append(dict(radd=radd, emit=wo_stats_emit))
            if oc > 0:
                wo_units[oc - 1]["emit"]()
                wo_units[oc - 1]["stats"] = C["P"]

        wo_units[7]["emit"]()
        wo_units[7]["stats"] = C["P"]

        wo_radds[b] = radds
        ln_stats = wo_units[-1]["stats"]

        @block.vector
        def _(v, ln_stats=ln_stats):
            wt(v, "v", "P", ln_stats)
            v.tensor_scalar(mu_s, PD[0:1, 0:512], 1.0 / D, None, ALU.mult)
            v.tensor_scalar(var_s, PO[0:1, 0:512], 1.0 / D, None, ALU.mult)
            v.tensor_mul(m2_s, mu_s, mu_s)
            v.tensor_sub(var_s, var_s, m2_s).then_inc(SEMS["V"], 1)
            C["V"] += 1
        ln_v = C["V"]

        @defer_scalar
        def _(a, ln_v=ln_v):
            wt(a, "a", "V", ln_v)
            a.activation(sd_s, var_s, AF.Ln, bias=cvals[0:1, 0:1])
            a.activation(var_s, sd_s, AF.Exp, scale=-0.5).then_inc(SEMS["A"], 1)
        C["A"] += 1
        ln_r = C["A"]

        @block.tensor
        def _(t, ln_r=ln_r, ga=chunk_act.get(b * 24 - 1, 0)):
            wt(t, "t", "A", ln_r)
            if ga:
                wt(t, "t", "A", ga)
            t.matmul(PS[:, 2048:2560], on_row[0:1, 0:128], mu_s, start=True, stop=True)
            t.matmul(PS[:, 2560:3072], on_row[0:1, 0:128], var_s, start=True,
                     stop=True).then_inc(SEMS["P"], 1)
            C["P"] += 1
        ln_bc = C["P"]

        nrm_dmas = {}
        for oc in range(8):
            @block.vector
            def _(v, oc=oc, ln_bc=ln_bc, nrm_dmas=nrm_dmas):
                wt(v, "v", "P", ln_bc)
                if oc >= 4:
                    wt(v, "v", nrm_dmas[oc - 4][0], nrm_dmas[oc - 4][1])
                v.tensor_sub(tmpa[:, :], rbuf[:, oc, :], PS[:, 2048:2560])
                v.tensor_mul(nrm[:, oc % 4, :], tmpa[:, :],
                             PS[:, 2560:3072]).then_inc(SEMS["V"], 1)
                C["V"] += 1
            nrm_v = C["V"]

            @block.gpsimd
            def _(g, b=b, oc=oc, nrm_v=nrm_v, nrm_dmas=nrm_dmas):
                wt(g, "g", "V", nrm_v)
                key = "dN" + str(oc % 4)
                dma(g, key, d_xmid32[b, oc], nrm[:, oc % 4, :])
                dma(g, key, d_xmid16[b, oc], nrm[:, oc % 4, :])
                nrm_dmas[oc] = (key, C[key])
        wo_nrm_V[b] = C["V"]
        wo_done_P = ln_bc

    xmt_lds[3] = emit_xmt_load(3)
    for mo in range(8):
        for g in range(3):
            ci = 3 * 24 + mo * 3 + g
            emit_gate_unit(ci, 3, mo, g, xmt_lds[3],
                           extraV=(wo_nrm_V[3] if (mo == 0 and g == 0) else 0))
        emit_cupdate(3, mo)
    cupd_last[3] = C["V"]

    CFIN = B % 2
    ffn_c_done = C["V"]
    CI = 96

    h_radds = {}
    nrm_dmas5 = {}
    for tt in range(B):
        @block.sync
        def _(s, tt=tt, lastmm=chunk_mm[CI - 1],
              dn=tuple((dk, C[dk]) for dk in ("dN0", "dN1", "dN2", "dN3"))):
            for dk, dv in dn:
                wt(s, dk, dk, dv) if False else wt(s, "s", dk, dv)
            wt(s, "s", "P", lastmm)
            dma(s, "dM", xmt[:, :, :], d_xmid16[tt].transpose([1, 0, 2]))
        hx_load = C["dM"]

        hu = []
        for oc in range(8):
            ci = CI
            CI += 1
            buf = ci % 2

            @block.tensor
            def _(t, oc=oc, ci=ci, buf=buf, hx_load=hx_load, ffn_c_done=ffn_c_done):
                wt(t, "t", chunk_ld[ci][0], chunk_ld[ci][1])
                wt(t, "t", "dM", hx_load)
                wt(t, "t", "V", ffn_c_done)
                bank = oc % 2
                if oc >= 2:
                    wt(t, "t", "V", hu[oc - 2]["radd"])
                for ic in range(8):
                    t.matmul(PS[:, bank * 512:(bank + 1) * 512],
                             fwc[:, buf, 0, ic, :], xmt[:, ic, :],
                             start=(ic == 0), stop=False)
                for ic in range(8):
                    t.matmul(PS[:, bank * 512:(bank + 1) * 512],
                             fwc[:, buf, 1, ic, :], cT16[:, CFIN, ic, :],
                             start=False, stop=(ic == 7)).then_maybe_inc(
                        (SEMS["P"], 1) if ic == 7 else None)
                pass
                C["P"] += 1
            hmm_ = C["P"]
            chunk_mm[ci] = hmm_
            emit_fwc_load(ci + 2)

            @defer_scalar
            def _(a, oc=oc, hmm_=hmm_):
                wt(a, "a", "P", hmm_)
                a.activation(hc[:, oc % 2, :], PS[:, (oc % 2) * 512:(oc % 2 + 1) * 512],
                             AF.Tanh, bias=bffs[:, 24 + oc:25 + oc]).then_inc(SEMS["A"], 1)
            C["A"] += 1
            hact = C["A"]

            @block.sync
            def _(s, tt=tt, oc=oc):
                key = "dO" + str(oc % 2)
                tenant = h_radds.get((tt, oc - 2)) if oc >= 2 else \
                    (h_radds.get((tt - 1, oc + 6)) if tt > 0 else None)
                if tenant is not None:
                    wt(s, "s", "V", tenant)
                dma(s, key, xm32c[:, oc % 2, :], d_xmid32[tt, oc])
            xm_ld = (("dO" + str(oc % 2)), C["dO" + str(oc % 2)])

            @block.vector
            def _(v, oc=oc, hact=hact, xm_ld=xm_ld):
                wt(v, "v", "A", hact)
                wt(v, "v", xm_ld[0], xm_ld[1])
                v.tensor_add(rbuf[:, oc, :], hc[:, oc % 2, :],
                             xm32c[:, oc % 2, :]).then_inc(SEMS["V"], 1)
                C["V"] += 1
            radd = C["V"]
            h_radds[(tt, oc)] = radd

            @defer_scalar
            def _(a, oc=oc, radd=radd):
                wt(a, "a", "V", radd)
                a.activation(rbuf16[:, oc, :], rbuf[:, oc, :],
                             AF.Identity).then_inc(SEMS["A"], 1)
            C["A"] += 1
            rb16_a = C["A"]

            @block.vector
            def _(v, oc=oc, pstat=(hu[oc - 2]["stats"] if oc >= 2 else 0)):
                if pstat:
                    wt(v, "v", "P", pstat)
                v.tensor_mul(rsq16[:, oc % 2, :], rbuf[:, oc, :],
                             rbuf[:, oc, :]).then_inc(SEMS["V"], 1)
                C["V"] += 1
            rsq_a = C["V"]

            def h_stats_emit(oc=oc, rb16_a=rb16_a, rsq_a=rsq_a):
                @block.tensor
                def _(t, oc=oc, rb16_a=rb16_a, rsq_a=rsq_a):
                    wt(t, "t", "A", rb16_a)
                    wt(t, "t", "V", rsq_a)
                    t.matmul(PD[0:1, 0:512], on_col16[:, 0:1], rbuf16[:, oc, :],
                             start=(oc == 0), stop=(oc == 7))
                    t.matmul(PO[0:1, 0:512], on_col16[:, 0:1], rsq16[:, oc % 2, :],
                             start=(oc == 0), stop=(oc == 7)).then_inc(SEMS["P"], 1)
                    C["P"] += 1
            hu.append(dict(radd=radd, emit=h_stats_emit))
            if oc > 0:
                hu[oc - 1]["emit"]()
                hu[oc - 1]["stats"] = C["P"]
        hu[7]["emit"]()
        hu[7]["stats"] = C["P"]

        @block.vector
        def _(v, hu=hu):
            wt(v, "v", "P", hu[-1]["stats"])
            v.tensor_scalar(mu_s, PD[0:1, 0:512], 1.0 / D, None, ALU.mult)
            v.tensor_scalar(var_s, PO[0:1, 0:512], 1.0 / D, None, ALU.mult)
            v.tensor_mul(m2_s, mu_s, mu_s)
            v.tensor_sub(var_s, var_s, m2_s).then_inc(SEMS["V"], 1)
            C["V"] += 1
        ln_v = C["V"]

        @defer_scalar
        def _(a, ln_v=ln_v):
            wt(a, "a", "V", ln_v)
            a.activation(sd_s, var_s, AF.Ln, bias=cvals[0:1, 0:1])
            a.activation(var_s, sd_s, AF.Exp, scale=-0.5).then_inc(SEMS["A"], 1)
        C["A"] += 1
        ln_r = C["A"]

        @block.tensor
        def _(t, ln_r=ln_r):
            wt(t, "t", "A", ln_r)
            t.matmul(PS[:, 2048:2560], on_row[0:1, 0:128], mu_s, start=True, stop=True)
            t.matmul(PS[:, 2560:3072], on_row[0:1, 0:128], var_s, start=True,
                     stop=True).then_inc(SEMS["P"], 1)
            C["P"] += 1
        ln_bc = C["P"]

        for oc in range(8):
            @block.vector
            def _(v, tt=tt, oc=oc, ln_bc=ln_bc):
                wt(v, "v", "P", ln_bc)
                tenant = nrm_dmas5.get((tt, oc - 4)) if oc >= 4 else \
                    (nrm_dmas5.get((tt - 1, oc + 4)) if tt > 0 else None)
                if tenant is not None:
                    wt(v, "v", tenant[0], tenant[1])
                v.tensor_sub(tmpa[:, :], rbuf[:, oc, :], PS[:, 2048:2560])
                v.tensor_mul(nrm[:, oc % 4, :], tmpa[:, :],
                             PS[:, 2560:3072]).then_inc(SEMS["V"], 1)
                C["V"] += 1
            nrm_v = C["V"]

            @block.gpsimd
            def _(g, tt=tt, oc=oc, nrm_v=nrm_v):
                wt(g, "g", "V", nrm_v)
                key = "dN" + str(oc % 4)
                dma(g, key, d_yT[tt, oc], nrm[:, oc % 4, :])
                nrm_dmas5[(tt, oc)] = (key, C[key])

    @block.gpsimd
    def _(g):
        for dk in ("dN0", "dN1", "dN2", "dN3"):
            wt(g, "g", dk, C[dk])

    @block.scalar
    def _(a):
        for fn in SCALAR_OPS:
            fn(a)

    ctx.close()
    return nc


def _host_prep(inputs):
    f32 = np.float32
    x = np.asarray(inputs["L0_attn_x"], f32)
    xT = np.ascontiguousarray(x.transpose(0, 2, 1))

    half = 32
    inv = (1.0 / (np.float32(10000.0) ** (np.arange(half, dtype=f32) / np.float32(half)))).astype(f32)

    def tables(pos, scale):
        ang = (pos.astype(f32)[:, None] * inv[None, :]).astype(f32)
        cos = (np.cos(ang).astype(f32) * f32(scale)).astype(f32)
        sin = (np.sin(ang).astype(f32) * f32(scale)).astype(f32)
        out = np.zeros((4, 128, 1024), f32)
        for s in range(4):
            out[s, :, 0:512] = np.tile(cos[s * 128:(s + 1) * 128], (1, 16))
            out[s, :, 512:1024] = np.tile(sin[s * 128:(s + 1) * 128], (1, 16))
        return out

    wq = np.asarray(inputs["L0_attn_W_q"], f32)
    wdkv = np.asarray(inputs["L0_attn_W_dkv"], f32)
    wuk = np.asarray(inputs["L0_attn_W_uk"], f32)
    wuv = np.asarray(inputs["L0_attn_W_uv"], f32)
    wo = np.asarray(inputs["L0_attn_W_o"], f32)
    gam = np.asarray(inputs["L0_attn_ln_gamma"], f32)
    bet = np.asarray(inputs["L0_attn_ln_beta"], f32)
    fwT = np.ascontiguousarray(np.stack([
        np.asarray(inputs["L0_ffn_wf"], f32).T, np.asarray(inputs["L0_ffn_uf"], f32).T,
        np.asarray(inputs["L0_ffn_wi"], f32).T, np.asarray(inputs["L0_ffn_ui"], f32).T,
        np.asarray(inputs["L0_ffn_wc"], f32).T, np.asarray(inputs["L0_ffn_uc"], f32).T,
        np.asarray(inputs["L0_ffn_wo"], f32).T, np.asarray(inputs["L0_ffn_uo"], f32).T,
    ]))
    bias4 = np.stack([np.asarray(inputs["L0_ffn_" + k], f32) for k in ("bf", "bi", "bc", "bo")])
    bff = np.ascontiguousarray(bias4.reshape(4, 8, 128).transpose(2, 0, 1).reshape(128, 32))
    incol = np.ascontiguousarray(np.asarray(inputs["L0_ffn_in_c"], f32).reshape(8, 128).T)
    gcol = np.ascontiguousarray(gam.reshape(2, 128).T)
    bcol = np.ascontiguousarray(bet.reshape(2, 128).T)

    shared = dict(wq=wq, wdkv=wdkv, wuk=wuk, wuv=wuv, wo=wo, fwT=fwT,
                  bff=bff, incol=incol, gcol=gcol, bcol=bcol,
                  idt=np.eye(128, dtype=f32))

    in_maps = []
    for i in range(NC):
        xq_i = np.ascontiguousarray(xT[:, :, i::8])
        xkv_i = np.ascontiguousarray(xT[:, :, 512 * i:512 * (i + 1)])
        pos_q = np.arange(i, T, 8)
        pos_kv = np.arange(512 * i, 512 * (i + 1))
        k_loc = np.arange(128)
        j_loc = np.arange(16)
        mm_ = (k_loc[:, None] <= (8 * j_loc[None, :] + i))
        maskb = np.zeros((128, 2, 2, 32), np.float32)
        maskb[:, 0, :, 0:16] = mm_[:, None, :]
        maskb[:, 0, :, 16:32] = 1.0
        maskb[:, 1, :, 0:16] = 0.0
        maskb[:, 1, :, 16:32] = mm_[:, None, :]
        m = dict(shared)
        m["xq"] = xq_i
        m["xkv"] = xkv_i
        m["ropek"] = tables(pos_kv, 1.0)
        m["ropeq"] = tables(pos_q, 0.125)
        m["maskb"] = np.ascontiguousarray(maskb.reshape(128, 128)).astype(ml_dtypes.bfloat16)
        in_maps.append(m)
    return in_maps


def kernel(**inputs):
    in_maps = _host_prep(inputs)
    nc = build_program()
    res = run_bass_kernel_spmd(nc, in_maps, core_ids=list(range(NC)))
    outT = np.empty((B, D, T), np.float32)
    for i in range(NC):
        outT[:, :, i::8] = res.results[i]["yT"].reshape(B, D, 512)
    return np.ascontiguousarray(outT.transpose(0, 2, 1))


# version 25
# speedup vs baseline: 1.1428x; 1.0005x over previous
import numpy as np
import ml_dtypes
from contextlib import ExitStack

import concourse.bass as bass
import concourse.mybir as mybir
from concourse.bass_utils import run_bass_kernel_spmd

F32 = mybir.dt.float32
BF16 = mybir.dt.bfloat16
AF = mybir.ActivationFunctionType
ALU = mybir.AluOpType

B = 4
T = 4096
D = 1024
LT = 256
NC = 8
EPS = 1e-5


def build_program():
    nc = bass.Bass("TRN2", target_bir_lowering=False, debug=False,
                   detect_race_conditions=False)

    d_xkv = nc.dram_tensor("xkv", [B, D, 512], F32, kind="ExternalInput").ap()
    d_xq = nc.dram_tensor("xq", [B, D, 512], F32, kind="ExternalInput").ap()
    d_wq = nc.dram_tensor("wq", [D, D], F32, kind="ExternalInput").ap()
    d_wdkv = nc.dram_tensor("wdkv", [D, LT], F32, kind="ExternalInput").ap()
    d_wuk = nc.dram_tensor("wuk", [LT, D], F32, kind="ExternalInput").ap()
    d_wuv = nc.dram_tensor("wuv", [LT, D], F32, kind="ExternalInput").ap()
    d_wo = nc.dram_tensor("wo", [D, D], F32, kind="ExternalInput").ap()
    d_fwT = nc.dram_tensor("fwT", [8, D, D], F32, kind="ExternalInput").ap()
    d_bff = nc.dram_tensor("bff", [128, 32], F32, kind="ExternalInput").ap()
    d_incol = nc.dram_tensor("incol", [128, 8], F32, kind="ExternalInput").ap()
    d_gcol = nc.dram_tensor("gcol", [128, 2], F32, kind="ExternalInput").ap()
    d_bcol = nc.dram_tensor("bcol", [128, 2], F32, kind="ExternalInput").ap()
    d_ropek = nc.dram_tensor("ropek", [4, 128, 1024], F32, kind="ExternalInput").ap()
    d_ropeq = nc.dram_tensor("ropeq", [4, 128, 1024], F32, kind="ExternalInput").ap()
    d_maskb = nc.dram_tensor("maskb", [128, 128], BF16, kind="ExternalInput").ap()
    d_idt = nc.dram_tensor("idt", [128, 128], F32, kind="ExternalInput").ap()

    d_kvsh = nc.dram_tensor("kvsh", [B, 8, 2, 128, 4, 128], BF16).ap()
    d_kvfull = nc.dram_tensor("kvfull", [B, NC, 8, 2, 128, 4, 128], BF16,
                              addr_space="Shared").ap()
    d_stsh = nc.dram_tensor("stsh", [B, 128], F32).ap()
    d_stfull = nc.dram_tensor("stfull", [B, NC, 128], F32).ap()
    d_fcache = nc.dram_tensor("fcache", [8, 8, 128, 8, 128], BF16).ap()
    d_xmid32 = nc.dram_tensor("xmid32", [B, 8, 128, 512], F32).ap()
    d_xmid16 = nc.dram_tensor("xmid16", [B, 8, 128, 512], BF16).ap()
    d_yT = nc.dram_tensor("yT", [B, 8, 128, 512], F32, kind="ExternalOutput").ap()

    ctx = ExitStack()
    sb = lambda name, shape, dt: ctx.enter_context(nc.sbuf_tensor(name, shape, dt))
    ps = lambda name, shape: ctx.enter_context(nc.psum_tensor(name, shape, F32))

    on_mat = sb("on_mat", [128, 64], BF16)
    on_row = sb("on_row", [1, 128], F32)
    on_col = sb("on_col", [128, 1], F32)
    w_o = sb("w_o", [128, 8, 1024], BF16)
    bffs = sb("bffs", [128, 32], F32)
    incs = sb("incs", [128, 8], F32)
    gcs = sb("gcs", [128, 2], F32)
    bcs = sb("bcs", [128, 2], F32)
    masks = sb("masks", [128, 2, 2, 32], BF16)
    biasB = sb("biasB", [128, 32], F32)
    Mq = sb("Mq", [128, 32], F32)
    Mk = sb("Mk", [128, 4], F32)
    MqR = sb("MqR", [128, 32], F32)
    MkR = sb("MkR", [128, 4], F32)
    sqMq = sb("sqMq", [128, 32], F32)
    kmb = sb("kmb", [128, 4], F32)
    strd = sb("strd", [1, 64], F32)
    kmaxg = sb("kmaxg", [1, 8], F32)
    mrow = sb("mrow", [1, 40], F32)
    idt = sb("idt_s", [128, 128], F32)
    lnscr = sb("lnscr", [1, 1024], F32)
    tmpa = sb("tmpa", [128, 512], F32)
    tmpb = sb("tmpb", [128, 512], F32)
    cvals = sb("cvals", [128, 2], F32)
    zrow = sb("zrow", [1, 512], BF16)
    on_row16 = sb("on_row16", [1, 128], BF16)
    on_col16 = sb("on_col16", [128, 1], BF16)
    qhat = sb("qhat", [128, B, 8, 4, 128], BF16)

    bctx = ExitStack()
    bsb = lambda name, shape, dt: bctx.enter_context(nc.sbuf_tensor(name, shape, dt))
    w_q = bsb("w_q", [128, 8, 1024], BF16)
    w_dkv = bsb("w_dkv", [128, 8, 256], BF16)
    w_uk = bsb("w_uk", [128, 2, 1024], BF16)
    w_uv = bsb("w_uv", [128, 2, 1024], BF16)
    xkv_s = bsb("xkv_s", [128, 2, 8, 512], BF16)
    xq_s = bsb("xq_s", [128, 2, 8, 512], BF16)
    ltf = bsb("ltf", [128, 2, 512], F32)
    ltsq = bsb("ltsq", [128, 2, 512], F32)
    latn = bsb("latn", [128, 2, 2, 512], BF16)
    ropbk = bsb("ropbk", [128, 4, 1024], F32)
    ropbq = bsb("ropbq", [128, 4, 1024], F32)
    krop = bsb("krop", [128, 4, 1024], BF16)
    qrop = bsb("qrop", [128, 4, 1024], BF16)
    kts = bsb("kts", [128, 4, 1024], BF16)
    vst = bsb("vst", [128, 4, 1024], BF16)
    sqk = bsb("sqk", [128, 1024], F32)
    sqq = bsb("sqq", [128, 1024], F32)
    n16 = bsb("n16", [128, 16], F32)
    n8 = bsb("n8", [128, 8], F32)
    ncol = bsb("ncol", [128, 4], F32)

    PS = ps("PS", [128, 3072])
    PO = ps("PO", [128, 512])
    PD = ps("PD", [128, 512])

    SEMNAMES = [
        "dW",
        "dWa",
        "dX0", "dX1",
        "dC",
        "dT0", "dT1",
        "dK0", "dK1", "dK2", "dK3",
        "dQ0", "dQ1", "dQ2", "dQ3",
        "dH0", "dH1", "dH2", "dH3",
        "dSS",
        "dV0", "dV1",
        "dR0", "dR1",
        "dN0", "dN1", "dN2", "dN3",
        "dM",
        "dO0", "dO1",
        "dF0", "dF1",
        "P", "A", "V", "G", "CC",
    ]
    SEMS = {}
    for nm in SEMNAMES:
        SEMS[nm] = ctx.enter_context(nc.semaphore("s_" + nm))
    C = {nm: 0 for nm in SEMNAMES}
    lastw = {}

    def wt(eng, ename, key, val):
        if val <= 0:
            return
        if lastw.get((ename, key), -1) >= val:
            return
        eng.wait_ge(SEMS[key], val)
        lastw[(ename, key)] = val

    def dma(eng, key, out, in_, transpose=False):
        if transpose:
            eng.dma_start(out=out, in_=in_, transpose=True).then_inc(SEMS[key], 16)
        else:
            eng.dma_start(out=out, in_=in_).then_inc(SEMS[key], 16)
        C[key] += 16
        return C[key]

    block = ctx.enter_context(nc.Block())

    SCALAR_OPS = []

    def defer_scalar(fn):
        SCALAR_OPS.append(fn)
        return None

    @block.gpsimd
    def _(g):
        g.memset(on_mat[:, :], 1.0)
        g.memset(on_row[:, :], 1.0)
        g.memset(on_col[:, :], 1.0)
        g.memset(Mk[:, :], 0.0)
        g.memset(Mq[:, :], 0.0)
        g.memset(cvals[:, 0:1], EPS)
        g.memset(cvals[:, 1:2], 1e-30)
        g.memset(zrow[:, :], 0.0)
        g.memset(on_row16[:, :], 1.0)
        g.memset(on_col16[:, :], 1.0).then_inc(SEMS["G"], 1)
        C["G"] += 1
        for dst, src in (
            (w_dkv[:, :, :], d_wdkv), (w_q[:, :, :], d_wq),
            (w_uk[:, :, :], d_wuk), (w_uv[:, :, :], d_wuv),
        ):
            dma(g, "dWa", dst, src.rearrange("(ic p) d -> p ic d", p=128))
        dma(g, "dW", w_o[:, :, :], d_wo.rearrange("(ic p) d -> p ic d", p=128))
        dma(g, "dW", masks[:, :, :, :], d_maskb.rearrange("p (par two c) -> p par two c", par=2, c=32))
    WA_DONE = C["dWa"]

    @block.sync
    def _(s):
        dma(s, "dC", idt[:, :], d_idt)
        dma(s, "dC", bffs[:, :], d_bff)
        dma(s, "dC", incs[:, :], d_incol)
        dma(s, "dC", gcs[:, :], d_gcol)
        dma(s, "dC", bcs[:, :], d_bcol)
        dma(s, "dC", ropbk[:, :, :], d_ropek.transpose([1, 0, 2]))
        dma(s, "dC", ropbq[:, :, :], d_ropeq.transpose([1, 0, 2]))
    CONST_DONE = C["dC"]

    @block.gpsimd
    def _(g):
        dma(g, "dX0", xkv_s[:, 0, :, :], d_xkv[0].rearrange("(ic p) t -> p ic t", p=128))
        dma(g, "dX0", xq_s[:, 0, :, :], d_xq[0].rearrange("(ic p) t -> p ic t", p=128))
    XLOAD = {0: ("dX0", C["dX0"])}

    slot_units = {}
    lat_units = {}
    lat_norm = 0
    ph2_V = {}
    SQMQ_B = {}
    CC_STATS = {}
    CC_KV = {}
    BIAS_B = {}

    mu_s = lnscr[0:1, 0:512]
    var_s = lnscr[0:1, 512:1024]
    m2_s = tmpa[0:1, 0:512]
    sd_s = tmpb[0:1, 0:512]

    def emit_lat_chain(b):
        bb = b % 2

        @block.tensor
        def _(t, b=b, bb=bb):
            wt(t, "t", "dWa", WA_DONE)
            wt(t, "t", "G", 1)
            wt(t, "t", XLOAD[b][0], XLOAD[b][1])
            if b > 0:
                wt(t, "t", "V", lat_units[b - 1]["norm_done"])
            for lh in range(2):
                for ic in range(8):
                    t.matmul(PS[:, 1024 + lh * 512:1024 + (lh + 1) * 512],
                             w_dkv[:, ic, lh * 128:(lh + 1) * 128],
                             xkv_s[:, bb, ic, :],
                             start=(ic == 0), stop=(ic == 7)).then_maybe_inc(
                        (SEMS["P"], 1) if (lh == 1 and ic == 7) else None)
            pass
            C["P"] += 1
        lat_mm = C["P"]

        @block.vector
        def _(v, lat_mm=lat_mm):
            wt(v, "v", "P", lat_mm)
            v.tensor_copy(ltf[:, 0, :], PS[:, 1024:1536])
            v.tensor_copy(ltf[:, 1, :], PS[:, 1536:2048]).then_inc(SEMS["V"], 1)
            C["V"] += 1

        @block.vector
        def _(v):
            v.tensor_mul(ltsq[:, 0, :], ltf[:, 0, :], ltf[:, 0, :])
            v.tensor_mul(ltsq[:, 1, :], ltf[:, 1, :],
                         ltf[:, 1, :]).then_inc(SEMS["V"], 1)
            C["V"] += 1
        lat_sq = C["V"]

        @block.tensor
        def _(t, lat_sq=lat_sq):
            wt(t, "t", "V", lat_sq)
            for lh in range(2):
                t.matmul(PS[0:1, 1024:1536], on_col[:, 0:1], ltf[:, lh, :],
                         start=(lh == 0), stop=(lh == 1))
            for lh in range(2):
                t.matmul(PS[0:1, 1536:2048], on_col[:, 0:1], ltsq[:, lh, :],
                         start=(lh == 0), stop=(lh == 1)).then_maybe_inc(
                    (SEMS["P"], 1) if lh == 1 else None)
            pass
            C["P"] += 1
        lat_stats = C["P"]

        @block.vector
        def _(v, lat_stats=lat_stats):
            wt(v, "v", "P", lat_stats)
            v.tensor_scalar(mu_s, PS[0:1, 1024:1536], 1.0 / LT, None, ALU.mult)
            v.tensor_scalar(var_s, PS[0:1, 1536:2048], 1.0 / LT, None, ALU.mult)
            v.tensor_mul(m2_s, mu_s, mu_s)
            v.tensor_sub(var_s, var_s, m2_s).then_inc(SEMS["V"], 1)
            C["V"] += 1
        lat_v1 = C["V"]

        @defer_scalar
        def _(a, lat_v1=lat_v1):
            wt(a, "a", "V", lat_v1)
            a.activation(sd_s, var_s, AF.Ln, bias=cvals[0:1, 0:1])
            a.activation(var_s, sd_s, AF.Exp, scale=-0.5).then_inc(SEMS["A"], 1)
        C["A"] += 1
        lat_rstd = C["A"]

        @block.tensor
        def _(t, lat_rstd=lat_rstd):
            wt(t, "t", "A", lat_rstd)
            t.matmul(PS[:, 1024:1536], on_row[0:1, 0:128], mu_s, start=True, stop=True)
            t.matmul(PS[:, 1536:2048], on_row[0:1, 0:128], var_s, start=True,
                     stop=True).then_inc(SEMS["P"], 1)
            C["P"] += 1
        lat_bc = C["P"]

        @block.vector
        def _(v, bb=bb, lat_bc=lat_bc):
            wt(v, "v", "P", lat_bc)
            wt(v, "v", "dC", CONST_DONE)
            for lh in range(2):
                v.tensor_sub(tmpa[:, :], ltf[:, lh, :], PS[:, 1024:1536])
                v.tensor_mul(tmpb[:, :], tmpa[:, :], PS[:, 1536:2048])
                v.tensor_scalar(latn[:, bb, lh, :], tmpb[:, :],
                                gcs[:, lh:lh + 1], bcs[:, lh:lh + 1],
                                ALU.mult, ALU.add).then_maybe_inc(
                    (SEMS["V"], 1) if lh == 1 else None)
            pass
            C["V"] += 1
        lat_units[b] = {"norm_done": C["V"]}

    for b in range(B):
        bb = b % 2

        if b + 1 < B:
            @block.gpsimd
            def _(g, b=b):
                nb = (b + 1) % 2
                key = "dX" + str(nb)
                if b + 1 >= 2:
                    wt(g, "g", "P", slot_units[(b - 1, 3)]["q_mm"])
                dma(g, key, xkv_s[:, nb, :, :],
                    d_xkv[b + 1].rearrange("(ic p) t -> p ic t", p=128))
                dma(g, key, xq_s[:, nb, :, :],
                    d_xq[b + 1].rearrange("(ic p) t -> p ic t", p=128))
                XLOAD[b + 1] = (key, C[key])

        if b == 0:
            emit_lat_chain(0)
        lat_norm = lat_units[b]["norm_done"]

        for s_ in range(4):
            par = s_
            prev = slot_units.get((b, s_ - 1)) if s_ > 0 else \
                (slot_units.get((b - 1, 3)) if b > 0 else None)
            prev2 = slot_units.get((b - 1, s_)) if b > 0 else None

            @block.tensor
            def _(t, bb=bb, s_=s_, prev=prev, lat_norm=lat_norm):
                wt(t, "t", "V", lat_norm)
                if prev is not None:
                    wt(t, "t", "V", prev["vevac_done"])
                for dh in range(2):
                    for lh in range(2):
                        t.matmul(PS[:, dh * 512:(dh + 1) * 512],
                                 latn[:, bb, lh, s_ * 128:s_ * 128 + 128],
                                 w_uk[:, lh, dh * 512:(dh + 1) * 512],
                                 start=(lh == 0), stop=(lh == 1))
                for dh in range(2):
                    dst = PO if dh == 0 else PD
                    for lh in range(2):
                        t.matmul(dst[:, :],
                                 latn[:, bb, lh, s_ * 128:s_ * 128 + 128],
                                 w_uv[:, lh, dh * 512:(dh + 1) * 512],
                                 start=(lh == 0), stop=(lh == 1)).then_maybe_inc(
                            (SEMS["P"], 1) if (dh == 1 and lh == 1) else None)
                pass
                C["P"] += 1
            kv_mm = C["P"]

            @block.tensor
            def _(t, b=b, bb=bb, s_=s_, prev=prev):
                if prev is not None:
                    wt(t, "t", "V", prev["ropeq_done"])
                if s_ == 0 and b > 0:
                    wt(t, "t", "V", ph2_V[b - 1])
                for dh in range(2):
                    for ic in range(8):
                        t.matmul(PS[:, 2048 + dh * 512:2048 + (dh + 1) * 512],
                                 xq_s[:, bb, ic, s_ * 128:s_ * 128 + 128],
                                 w_q[:, ic, dh * 512:(dh + 1) * 512],
                                 start=(ic == 0), stop=(ic == 7)).then_maybe_inc(
                            (SEMS["P"], 1) if (dh == 1 and ic == 7) else None)
                pass
                C["P"] += 1
            q_mm = C["P"]

            def rope_emit(v, psoff, tblbuf, tbl, dst, dstbuf):
                ap = PS[:, psoff:psoff + 1024].rearrange(
                    "p (h two q) -> p h two q", h=16, two=2)
                x1 = ap[:, :, 0, :]
                x2 = ap[:, :, 1, :]
                cos = tbl[:, tblbuf, 0:512].rearrange("p (h q) -> p h q", h=16)
                sin = tbl[:, tblbuf, 512:1024].rearrange("p (h q) -> p h q", h=16)
                ta = tmpa[:, :].rearrange("p (h q) -> p h q", h=16)
                tb = tmpb[:, :].rearrange("p (h q) -> p h q", h=16)
                dap = dst[:, dstbuf, :].rearrange("p (h two q) -> p h two q", h=16, two=2)
                v.tensor_mul(ta, x1, cos)
                v.tensor_mul(tb, x2, sin)
                v.tensor_sub(dap[:, :, 0, :], ta, tb)
                v.tensor_mul(ta, x1, sin)
                v.tensor_mul(tb, x2, cos)
                return v.tensor_add(dap[:, :, 1, :], ta, tb)

            @block.vector
            def _(v, s_=s_, par=par, kv_mm=kv_mm, prev2=prev2):
                wt(v, "v", "P", kv_mm)
                wt(v, "v", "dC", CONST_DONE)
                if prev2 is not None:
                    wt(v, "v", "dK" + str(par), prev2["ktr_done"])
                rope_emit(v, 0, s_, ropbk, krop, par).then_inc(SEMS["V"], 1)
                C["V"] += 1
            ropek_done = C["V"]

            @block.vector
            def _(v, par=par, prev2=prev2):
                if prev2 is not None:
                    wt(v, "v", "dH" + str(par), prev2["shard_done"])
                v.tensor_copy(vst[:, par, 0:512], PO[:, :])
                v.tensor_copy(vst[:, par, 512:1024], PD[:, :]).then_inc(SEMS["V"], 1)
                C["V"] += 1
            vevac_done = C["V"]

            @block.vector
            def _(v, s_=s_, par=par, q_mm=q_mm, prev2=prev2):
                wt(v, "v", "P", q_mm)
                if prev2 is not None:
                    wt(v, "v", "dQ" + str(par), prev2["qtr_done"])
                rope_emit(v, 2048, s_, ropbq, qrop, par).then_inc(SEMS["V"], 1)
                C["V"] += 1
            ropeq_done = C["V"]

            @block.vector
            def _(v, par=par):
                v.tensor_mul(sqk[:, :], krop[:, par, :], krop[:, par, :]).then_inc(SEMS["V"], 1)
                C["V"] += 1
                v.tensor_mul(sqq[:, :], qrop[:, par, :], qrop[:, par, :]).then_inc(SEMS["V"], 1)
                C["V"] += 1
            sq_done = C["V"]

            @block.vector
            def _(v, b=b, sq_done=sq_done):
                v.reduce_sum(n16[:, :], sqk[:, :].rearrange("p (h q) -> p h q", h=16),
                             axis=mybir.AxisListType.X)
                v.reduce_max(ncol[:, 0:1], n16[:, :], axis=mybir.AxisListType.X)
                v.tensor_max(Mk[:, b:b + 1], Mk[:, b:b + 1], ncol[:, 0:1])
                v.reduce_sum(n16[:, :], sqq[:, :].rearrange("p (h q) -> p h q", h=16),
                             axis=mybir.AxisListType.X)
                v.reduce_max(n8[:, :], n16[:, :].rearrange("p (pr two) -> p pr two", two=2),
                             axis=mybir.AxisListType.X)
                v.tensor_max(Mq[:, b * 8:(b + 1) * 8], Mq[:, b * 8:(b + 1) * 8],
                             n8[:, :]).then_inc(SEMS["V"], 1)
                C["V"] += 1
            norms_done = C["V"]

            @block.sync
            def _(s, par=par, ropek_done=ropek_done, prev2=prev2):
                wt(s, "s", "V", ropek_done)
                if prev2 is not None:
                    wt(s, "s", "dH" + str(par), prev2["shard_done"])
                for pr in range(8):
                    dma(s, "dK" + str(par), kts[:, par, pr * 128:(pr + 1) * 128],
                        krop[:, par, pr * 128:(pr + 1) * 128], transpose=True)
            ktr_done = C["dK" + str(par)]

            @block.sync
            def _(s, b=b, s_=s_, par=par, ropeq_done=ropeq_done):
                wt(s, "s", "V", ropeq_done)
                for pr in range(8):
                    dma(s, "dQ" + str(par), qhat[:, b, pr, s_, :],
                        qrop[:, par, pr * 128:(pr + 1) * 128], transpose=True)
            qtr_done = C["dQ" + str(par)]

            @block.gpsimd
            def _(g, b=b, s_=s_, par=par, ktr_done=ktr_done, vevac_done=vevac_done):
                wt(g, "g", "dK" + str(par), ktr_done)
                dma(g, "dH" + str(par), d_kvsh[b, :, 0, :, s_, :].transpose([1, 0, 2]),
                    kts[:, par, :].rearrange("p (pr c) -> p pr c", c=128))
                wt(g, "g", "V", vevac_done)
                dma(g, "dH" + str(par), d_kvsh[b, :, 1, :, s_, :].transpose([1, 0, 2]),
                    vst[:, par, :].rearrange("p (pr c) -> p pr c", c=128))
            shard_done = C["dH" + str(par)]

            slot_units[(b, s_)] = dict(
                ropek_done=ropek_done, ropeq_done=ropeq_done, vevac_done=vevac_done,
                norms_done=norms_done, ktr_done=ktr_done, qtr_done=qtr_done,
                shard_done=shard_done, q_mm=q_mm,
            )

            if s_ == 2 and b + 1 < B:
                emit_lat_chain(b + 1)

        nd = slot_units[(b, 3)]["norms_done"]

        @block.tensor
        def _(t, b=b, nd=nd):
            wt(t, "t", "V", nd)
            wt(t, "t", "dC", CONST_DONE)
            t.transpose(PS[0:8, 2048:2176], Mq[:, 8 * b:8 * b + 8], idt[:, :])
            t.transpose(PS[0:1, 2176:2304], Mk[:, b:b + 1],
                        idt[:, :]).then_inc(SEMS["P"], 1)
            C["P"] += 1
        tr1 = C["P"]

        @block.vector
        def _(v, b=b, tr1=tr1):
            wt(v, "v", "P", tr1)
            v.reduce_max(tmpa[0:8, 0:1], PS[0:8, 2048:2176], axis=mybir.AxisListType.X)
            v.reduce_max(tmpb[0:1, 0:1], PS[0:1, 2176:2304],
                         axis=mybir.AxisListType.X)
            v.tensor_copy(mrow[0:1, 32 + b:33 + b],
                          tmpb[0:1, 0:1]).then_inc(SEMS["V"], 1)
            C["V"] += 1
        rmx = C["V"]

        @block.tensor
        def _(t, rmx=rmx):
            wt(t, "t", "V", rmx)
            t.transpose(PS[0:1, 2560:2568], tmpa[0:8, 0:1],
                        idt[0:8, 0:8]).then_inc(SEMS["P"], 1)
            C["P"] += 1
        tr2 = C["P"]

        @block.vector
        def _(v, tr2=tr2):
            wt(v, "v", "P", tr2)
            v.tensor_copy(mrow[0:1, 0:8], PS[0:1, 2560:2568]).then_inc(SEMS["V"], 1)
            C["V"] += 1
        cpy = C["V"]

        @block.tensor
        def _(t, cpy=cpy):
            wt(t, "t", "V", cpy)
            t.matmul(PS[:, 2568:2576], on_row[0:1, 0:128], mrow[0:1, 0:8],
                     start=True, stop=True).then_inc(SEMS["P"], 1)
            C["P"] += 1
        bcm = C["P"]

        @block.vector
        def _(v, b=b, bcm=bcm):
            wt(v, "v", "P", bcm)
            v.tensor_copy(MqR[:, 8 * b:8 * b + 8],
                          PS[:, 2568:2576]).then_inc(SEMS["V"], 1)
            C["V"] += 1
        ph2_V[b] = C["V"]

        @defer_scalar
        def _(a, b=b, pv=ph2_V[b]):
            wt(a, "a", "V", pv)
            a.activation(sqMq[:, 8 * b:8 * b + 8], MqR[:, 8 * b:8 * b + 8],
                         AF.Ln, bias=cvals[:, 1:2])
            a.activation(sqMq[:, 8 * b:8 * b + 8], sqMq[:, 8 * b:8 * b + 8],
                         AF.Exp, scale=0.5).then_inc(SEMS["A"], 1)
        C["A"] += 1
        SQMQ_B[b] = C["A"]

        @block.gpsimd
        def _(g, b=b, rmx=rmx,
              kvdh=tuple((k, C[k]) for k in ("dH0", "dH1", "dH2", "dH3"))):
            wt(g, "g", "V", rmx)
            dma(g, "dSS", d_stsh[b:b + 1, 1:128], on_row[0:1, 1:128])
            dma(g, "dSS", d_stsh[b:b + 1, 0:1], mrow[0:1, 32 + b:33 + b])
            wt(g, "g", "dSS", C["dSS"])
            g.collective_compute(
                "AllGather", ALU.bypass, replica_groups=[list(range(NC))],
                ins=[d_stsh[b:b + 1, :]], outs=[d_stfull[b]],
            ).then_inc(SEMS["CC"], 1)
            C["CC"] += 1
            if b == 0:
                for hk, hv in kvdh:
                    wt(g, "g", hk, hv)
                g.collective_compute(
                    "AllGather", ALU.bypass, replica_groups=[list(range(NC))],
                    ins=[d_kvsh[0]], outs=[d_kvfull[0]],
                ).then_inc(SEMS["CC"], 1)
                C["CC"] += 1
        CC_STATS[b] = C["CC"] - (1 if b == 0 else 0)
        if b == 0:
            CC_KV[b] = C["CC"]

            @block.gpsimd
            def _(g):
                for fm in range(8):
                    for foc in range(8):
                        dma(g, "dW", d_fcache[fm, foc],
                            d_fwT[fm].rearrange("(ic p) (oc c) -> oc p ic c",
                                                p=128, c=128)[foc])

    @block.gpsimd
    def _(g):
        for hk in ("dH0", "dH1", "dH2", "dH3"):
            wt(g, "g", hk, C[hk])
        for bq in range(1, B):
            g.collective_compute(
                "AllGather", ALU.bypass, replica_groups=[list(range(NC))],
                ins=[d_kvsh[bq]], outs=[d_kvfull[bq]],
            ).then_inc(SEMS["CC"], 1)
            C["CC"] += 1
            CC_KV[bq] = C["CC"]

    W_DONE = C["dW"]
    LAST_NORMS = slot_units[(B - 1, 3)]["norms_done"]

    kmb_done = 0
    for b in range(B):
        @block.sync
        def _(s, b=b, cs=CC_STATS[b]):
            wt(s, "s", "CC", cs)
            s.dma_start(out=strd[0:1, b * 8:b * 8 + 8],
                        in_=d_stfull[b, :, 0:1].transpose([1, 0]),
                        allow_slow_non_contiguous=True).then_inc(SEMS["dC"], 16)
            C["dC"] += 16

        @block.vector
        def _(v, b=b, sl=C["dC"]):
            wt(v, "v", "dC", sl)
            v.reduce_max(kmaxg[0:1, b:b + 1], strd[0:1, b * 8:b * 8 + 8],
                         axis=mybir.AxisListType.X).then_inc(SEMS["V"], 1)
            C["V"] += 1
        kmg = C["V"]

        @defer_scalar
        def _(a, b=b, kmg=kmg):
            wt(a, "a", "V", kmg)
            a.activation(kmaxg[0:1, 4 + b:5 + b], kmaxg[0:1, b:b + 1],
                         AF.Ln, bias=cvals[0:1, 1:2])
            a.activation(kmaxg[0:1, b:b + 1], kmaxg[0:1, 4 + b:5 + b],
                         AF.Exp, scale=0.5).then_inc(SEMS["A"], 1)
        C["A"] += 1
        ksq = C["A"]

        @block.tensor
        def _(t, b=b, ksq=ksq, pk=kmb_done):
            wt(t, "t", "A", ksq)
            if b == 0:
                wt(t, "t", "V", ph2_V[B - 1])
            if pk:
                wt(t, "t", "V", pk)
            t.matmul(PS[:, 1024 + b:1025 + b], on_row[0:1, 0:128],
                     kmaxg[0:1, b:b + 1], start=True,
                     stop=True).then_inc(SEMS["P"], 1)
            C["P"] += 1
        kmm = C["P"]

        @block.vector
        def _(v, b=b, kmm=kmm, sq=SQMQ_B[b]):
            wt(v, "v", "P", kmm)
            v.tensor_copy(kmb[:, b:b + 1], PS[:, 1024 + b:1025 + b])
            wt(v, "v", "A", sq)
            v.tensor_scalar(biasB[:, b * 8:(b + 1) * 8], sqMq[:, b * 8:(b + 1) * 8],
                            kmb[:, b:b + 1], -1.0, ALU.mult,
                            ALU.mult).then_inc(SEMS["V"], 1)
            C["V"] += 1
        kmb_done = C["V"]
        BIAS_B[b] = C["V"]
    BIAS_DONE = BIAS_B[B - 1]

    BAR = dict(C)

    def barrier_waits(eng, ename):
        for k in SEMNAMES:
            if k != "CC":
                wt(eng, ename, k, BAR[k])

    @block.tensor
    def _(t):
        barrier_waits(t, "t")

    @block.vector
    def _(v):
        barrier_waits(v, "v")

    @defer_scalar
    def _(a):
        barrier_waits(a, "a")

    @block.sync
    def _(s):
        barrier_waits(s, "s")

    @block.gpsimd
    def _(g):
        barrier_waits(g, "g")

    bctx.close()

    kvK = sb("kvK", [128, 2, 32, 128], BF16)
    kvV = sb("kvV", [128, 2, 32, 128], BF16)
    pt = sb("pt", [128, 3, 1024], BF16)
    attnT = sb("attnT", [128, 8, 512], BF16)
    rcp = sb("rcp", [128, 512], F32)
    xrc = sb("xrc", [128, 2, 512], F32)
    rbuf = sb("rbuf", [128, 8, 512], F32)
    rsq = sb("rsq", [128, 512], F32)
    rbuf16 = sb("rbuf16", [128, 8, 512], BF16)
    rsq16 = sb("rsq16", [128, 2, 512], BF16)
    nrm = sb("nrm", [128, 4, 512], F32)
    xmt = sb("xmt", [128, 8, 512], BF16)
    xm32c = sb("xm32c", [128, 2, 512], F32)
    cT32 = sb("cT32", [128, 8, 512], F32)
    cT16 = sb("cT16", [128, 2, 8, 512], BF16)
    fwc = sb("fwc", [128, 2, 2, 8, 128], BF16)
    gf = sb("gf", [128, 512], BF16)
    gi = sb("gi", [128, 512], BF16)
    gg = sb("gg", [128, 512], BF16)
    hc = sb("hc", [128, 2, 512], F32)

    @defer_scalar
    def _(a):
        wt(a, "a", "dC", CONST_DONE)
        for j in range(8):
            a.activation(cT32[:, j, :], cT32[:, j, :], AF.Identity,
                         bias=incs[:, j:j + 1], scale=0.0).then_maybe_inc(
                (SEMS["A"], 1) if j == 7 else None)
        pass
    C["A"] += 1
    CINIT_A = C["A"]

    @block.vector
    def _(v):
        wt(v, "v", "A", CINIT_A)
        for j in range(8):
            v.tensor_copy(cT16[:, 0, j, :], cT32[:, j, :]).then_maybe_inc(
                (SEMS["V"], 1) if j == 7 else None)
        pass
        C["V"] += 1
    cready = C["V"]

    FCHUNKS = []
    for _t in range(B):
        for _mo in range(8):
            for _g in range(3):
                FCHUNKS.append((2 * _g, 2 * _g + 1, _mo))
    for _t in range(B):
        for _oc in range(8):
            FCHUNKS.append((6, 7, _oc))
    chunk_mm = {}
    chunk_ld = {}
    chunk_act = {}
    cupd_last = {-1: cready}
    ffn_state = {"last_cupd": cready}

    def emit_fwc_load(ci):
        if ci >= len(FCHUNKS):
            return
        wm, um, oc_ = FCHUNKS[ci]
        buf = ci % 2
        key = "dF" + str(buf)

        @block.gpsimd
        def _(g, ci=ci, wm=wm, um=um, oc_=oc_, buf=buf, key=key):
            wt(g, "g", "dW", W_DONE)
            if ci - 2 in chunk_mm:
                wt(g, "g", "P", chunk_mm[ci - 2])
            dma(g, key, fwc[:, buf, 0, :, :], d_fcache[wm, oc_])
            dma(g, key, fwc[:, buf, 1, :, :], d_fcache[um, oc_])
        chunk_ld[ci] = (key, C[key])

    emit_fwc_load(0)
    emit_fwc_load(1)

    def emit_xmt_load(step):
        @block.sync
        def _(s, step=step):
            for dk in ("dN0", "dN1", "dN2", "dN3"):
                wt(s, "s", dk, C[dk])
            if step > 0:
                wt(s, "s", "P", chunk_mm[step * 24 - 1])
            dma(s, "dM", xmt[:, :, :], d_xmid16[step].transpose([1, 0, 2]))
        return C["dM"]

    def emit_gate_part(ci, step, mo, gg_i, part, xmt_ld, extraV=0):
        buf = ci % 2
        bank = 4 + (ci % 2)
        func = (AF.Sigmoid, AF.Sigmoid, AF.Tanh)[gg_i]

        @block.tensor
        def _(t, ci=ci, buf=buf, bank=bank, xmt_ld=xmt_ld, step=step,
              part=part, extraV=extraV, cw=cupd_last[step - 1]):
            if part == 0:
                wt(t, "t", chunk_ld[ci][0], chunk_ld[ci][1])
                wt(t, "t", "dM", xmt_ld)
                if extraV:
                    wt(t, "t", "V", extraV)
                if ci - 2 in chunk_act:
                    wt(t, "t", "A", chunk_act[ci - 2])
            elif part == 2:
                wt(t, "t", "V", cw)
            w_u = part // 2
            src = xmt if w_u == 0 else None
            for j in range(4):
                ic = (part % 2) * 4 + j
                rhs = xmt[:, ic, :] if w_u == 0 else cT16[:, step % 2, ic, :]
                t.matmul(PS[:, bank * 512:(bank + 1) * 512],
                         fwc[:, buf, w_u, ic, :], rhs,
                         start=(part == 0 and j == 0),
                         stop=(part == 3 and j == 3)).then_maybe_inc(
                    (SEMS["P"], 1) if (part == 3 and j == 3) else None)
            pass
            if part == 3:
                C["P"] += 1
        if part == 3:
            chunk_mm[ci] = C["P"]
            emit_fwc_load(ci + 2)

            @defer_scalar
            def _(a, bank=bank, gg_i=gg_i, mo=mo, func=func, gmm=chunk_mm[ci],
                  pcu=(ffn_state["last_cupd"] if gg_i == 0 else 0)):
                wt(a, "a", "P", gmm)
                if pcu:
                    wt(a, "a", "V", pcu)
                dstb = (gf, gi, gg)[gg_i]
                a.activation(dstb[:, :], PS[:, bank * 512:(bank + 1) * 512],
                             func, bias=bffs[:, gg_i * 8 + mo:gg_i * 8 + mo + 1]
                             ).then_inc(SEMS["A"], 1)
            C["A"] += 1
            chunk_act[ci] = C["A"]

    FFN_TILES = [3, 5, 8, 10, 13, 15, 18, 20, 23, 25, 28, 30]
    FFN_TILES_P0 = [8, 10, 12, 14, 17, 19, 22, 24, 26, 28, 29, 30]

    def emit_gate_unit(ci, step, mo, gg_i, xmt_ld, extraV=0):
        for part in range(4):
            emit_gate_part(ci, step, mo, gg_i, part, xmt_ld,
                           extraV=(extraV if part == 0 else 0))

    def emit_cupdate(step, mo):
        nxt = (step + 1) % 2

        @block.vector
        def _(v, mo=mo, nxt=nxt, gact=chunk_act[step * 24 + mo * 3 + 2]):
            wt(v, "v", "A", gact)
            v.tensor_mul(tmpa[:, :], gf[:, :], cT32[:, mo, :])
            v.tensor_mul(tmpb[:, :], gi[:, :], gg[:, :])
            v.tensor_add(cT32[:, mo, :], tmpa[:, :], tmpb[:, :])
            v.tensor_copy(cT16[:, nxt, mo, :],
                          cT32[:, mo, :]).then_inc(SEMS["V"], 1)
            C["V"] += 1
        ffn_state["last_cupd"] = C["V"]

    xmt_lds = {}

    def kv_prefetch(s, b, pair, buf):
        key = "dV" + str(buf)
        for c in range(NC):
            dma(s, key, kvK[:, buf, 4 * c:4 * (c + 1), :], d_kvfull[b, c, pair, 0])
            dma(s, key, kvV[:, buf, 4 * c:4 * (c + 1), :], d_kvfull[b, c, pair, 1])
        return (key, C[key])

    kvready = {}

    @block.sync
    def _(s):
        wt(s, "s", "CC", CC_KV[0])
        kvready[(0, 0)] = kv_prefetch(s, 0, 0, 0)

    prev_epilogue_V = 0
    prev_pvden_P = 0
    wo_done_P = 0
    attn_done_V = {}
    wo_radds = {}
    wo_nrm_V = {}
    exp_slot_wait = {0: 0, 1: 0}

    for b in range(B):
        for pair in range(8):
            bp = b * 8 + pair
            buf = bp % 2

            @block.sync
            def _(s, bp=bp):
                if bp + 1 < 32:
                    nb, npair = divmod(bp + 1, 8)
                    if bp >= 1:
                        wt(s, "s", "P", prev_pvden_P)
                    wt(s, "s", "CC", CC_KV[nb])
                    kvready[(nb, npair)] = kv_prefetch(s, nb, npair, (bp + 1) % 2)

            if b >= 1 and pair == 0:
                xmt_lds[b - 1] = emit_xmt_load(b - 1)

            P_sT = {}
            A_exp = {}
            V_mask = {}

            def emit_dummies(epiV=prev_epilogue_V):
                @block.tensor
                def _(t, epiV=epiV):
                    wt(t, "t", "V", epiV)
                    t.matmul(PO[:, :], on_row16[0:1, 0:128], zrow[0:1, :],
                             start=True, stop=True)
                    t.matmul(PD[:, :], on_row16[0:1, 0:128], zrow[0:1, :],
                             start=True, stop=True)

            def emit_pv(tt, buf=buf):
                slot = tt % 3
                n0 = 32 * (tt // 2)
                N = 512 - n0

                @block.tensor
                def _(t, buf=buf, tt=tt, slot=slot, n0=n0, N=N, vm=V_mask[tt]):
                    wt(t, "t", "V", vm)
                    t.matmul(PO[0:64, n0:512], kvV[:, buf, tt, 0:64],
                             pt[:, slot, 0:N], start=False, stop=False,
                             tile_position=(0, 0), skip_group_check=True)
                    t.matmul(PO[64:128, n0:512], kvV[:, buf, tt, 64:128],
                             pt[:, slot, 512:512 + N], start=False, stop=False,
                             tile_position=(0, 64), skip_group_check=True)
                    t.matmul(PD[0:64, n0:512], on_mat[:, :],
                             pt[:, slot, 0:N], start=False, stop=False,
                             tile_position=(0, 0), skip_group_check=True)
                    t.matmul(PD[64:128, n0:512], on_mat[:, :],
                             pt[:, slot, 512:512 + N], start=False, stop=False,
                             tile_position=(0, 64), skip_group_check=True
                             ).then_inc(SEMS["P"], 1)
                C["P"] += 1

            for tt in range(32):
                slot = tt % 2
                n0 = 32 * (tt // 2)
                N = 512 - n0
                if tt == 0:
                    emit_dummies()
                if tt >= 3:
                    emit_pv(tt - 3)
                ftl = FFN_TILES_P0 if pair == 0 else FFN_TILES
                if b >= 1 and tt in ftl:
                    idx = ftl.index(tt)
                    g = idx // 4
                    part = idx % 4
                    ci = (b - 1) * 24 + pair * 3 + g
                    emit_gate_part(ci, b - 1, pair, g, part, xmt_lds[b - 1],
                                   extraV=(wo_nrm_V[b - 1]
                                           if (pair == 0 and g == 0 and part == 0)
                                           else 0))

                @block.tensor
                def _(t, b=b, pair=pair, buf=buf, tt=tt, slot=slot, n0=n0, N=N,
                      ew=exp_slot_wait[slot],
                      wonrm=(wo_radds[b - 1][7] if (pair == 0 and b > 0) else None)):
                    if tt == 0:
                        wt(t, "t", kvready[(b, pair)][0], kvready[(b, pair)][1])
                        wt(t, "t", "V", BIAS_B[b])
                        if wonrm is not None:
                            wt(t, "t", "V", wonrm)
                    if ew:
                        wt(t, "t", "A", ew)
                    qfe = qhat[0:64, b, pair, :, :].rearrange(
                        "p a c -> p (a c)")[:, n0:512]
                    qfo = qhat[64:128, b, pair, :, :].rearrange(
                        "p a c -> p (a c)")[:, n0:512]
                    t.matmul(PS[:, slot * 1024:slot * 1024 + N],
                             kvK[0:64, buf, tt, :], qfe,
                             start=True, stop=True)
                    t.matmul(PS[:, slot * 1024 + 512:slot * 1024 + 512 + N],
                             kvK[64:128, buf, tt, :], qfo,
                             start=True, stop=True,
                             tile_position=(64, 0)).then_inc(SEMS["P"], 1)
                C["P"] += 1
                P_sT[tt] = C["P"]

                @defer_scalar
                def _(a, b=b, pair=pair, slot=slot, pslot=tt % 3, N=N, stp=P_sT[tt]):
                    wt(a, "a", "P", stp)
                    inap = PS[:, slot * 1024:(slot + 1) * 1024].rearrange(
                        "p (two n) -> p two n", two=2)[:, :, 0:N]
                    outap = pt[:, pslot, :].rearrange(
                        "p (two n) -> p two n", two=2)[:, :, 0:N]
                    a.activation(outap, inap, AF.Exp,
                                 bias=biasB[:, b * 8 + pair:b * 8 + pair + 1]
                                 ).then_inc(SEMS["A"], 1)
                C["A"] += 1
                A_exp[tt] = C["A"]
                exp_slot_wait[slot] = C["A"]

                @block.vector
                def _(v, tt=tt, pslot=tt % 3, N=N, ae=A_exp[tt]):
                    if tt == 0:
                        wt(v, "v", "dW", W_DONE)
                    wt(v, "v", "A", ae)
                    pap = bass.AP(pt.handle if hasattr(pt, "handle") else pt,
                                  pslot * 1024, [[3072, 128], [512, 2], [1, 32]])
                    v.tensor_mul(pap, pap, masks[:, tt % 2, :, :]).then_inc(
                        SEMS["V"], 1)
                C["V"] += 1
                V_mask[tt] = C["V"]

            emit_pv(29)
            emit_pv(30)
            emit_pv(31)
            prev_pvden_P = C["P"]

            @defer_scalar
            def _(a, pv=prev_pvden_P):
                wt(a, "a", "P", pv)
                a.activation(rsq[:, :], PD[:, :], AF.Ln, bias=cvals[:, 1:2])
                a.activation(rcp[:, :], rsq[:, :], AF.Exp, scale=-1.0).then_inc(SEMS["A"], 1)
            C["A"] += 1
            epi_a = C["A"]

            @block.vector
            def _(v, pair=pair, epi_a=epi_a, b=b, wod=wo_done_P):
                wt(v, "v", "A", epi_a)
                if b > 0:
                    wt(v, "v", "P", wod)
                v.tensor_mul(attnT[:, pair, :], PO[:, :], rcp[:, :]).then_inc(SEMS["V"], 1)
                C["V"] += 1
            prev_epilogue_V = C["V"]

            if b >= 1:
                emit_cupdate(b - 1, pair)
                if pair == 7:
                    cupd_last[b - 1] = C["V"]

        attn_done_V[b] = prev_epilogue_V

        @block.sync
        def _(s, b=b):
            if b > 0:
                wt(s, "s", "V", wo_radds[b - 1][6])
                wt(s, "s", "V", wo_radds[b - 1][7])
            dma(s, "dR0", xrc[:, 0, :], d_xq[b, 0:128, :])
            dma(s, "dR1", xrc[:, 1, :], d_xq[b, 128:256, :])
        xrc_loads = {0: ("dR0", C["dR0"]), 1: ("dR1", C["dR1"])}

        wo_units = []
        radds = {}
        for oc in range(8):
            bank = oc % 2

            @block.tensor
            def _(t, b=b, oc=oc, bank=bank):
                wt(t, "t", "V", attn_done_V[b])
                if oc >= 2:
                    wt(t, "t", "V", wo_units[oc - 2]["radd"])
                for cc in range(8):
                    t.matmul(PS[:, bank * 512:(bank + 1) * 512],
                             w_o[:, cc, oc * 128:(oc + 1) * 128],
                             attnT[:, cc, :],
                             start=(cc == 0), stop=(cc == 7)).then_maybe_inc(
                        (SEMS["P"], 1) if cc == 7 else None)
                pass
                C["P"] += 1
            womm = C["P"]

            @block.vector
            def _(v, oc=oc, bank=bank, womm=womm, xl=xrc_loads[oc]):
                wt(v, "v", "P", womm)
                wt(v, "v", xl[0], xl[1])
                v.tensor_add(rbuf[:, oc, :], PS[:, bank * 512:(bank + 1) * 512],
                             xrc[:, oc % 2, :]).then_inc(SEMS["V"], 1)
                C["V"] += 1
            radd = C["V"]
            radds[oc] = radd

            @block.sync
            def _(s, b=b, oc=oc, radd=radd):
                if oc + 2 < 8:
                    wt(s, "s", "V", radd)
                    key = "dR" + str(oc % 2)
                    dma(s, key, xrc[:, oc % 2, :], d_xq[b, (oc + 2) * 128:(oc + 3) * 128, :])
                    xrc_loads[oc + 2] = (key, C[key])

            @defer_scalar
            def _(a, oc=oc, radd=radd):
                wt(a, "a", "V", radd)
                a.activation(rbuf16[:, oc, :], rbuf[:, oc, :],
                             AF.Identity).then_inc(SEMS["A"], 1)
            C["A"] += 1
            rb16_a = C["A"]

            @block.vector
            def _(v, oc=oc, pstat=(wo_units[oc - 2]["stats"] if oc >= 2 else 0)):
                if pstat:
                    wt(v, "v", "P", pstat)
                v.tensor_mul(rsq16[:, oc % 2, :], rbuf[:, oc, :],
                             rbuf[:, oc, :]).then_inc(SEMS["V"], 1)
                C["V"] += 1
            rsq_a = C["V"]

            def wo_stats_emit(oc=oc, rb16_a=rb16_a, rsq_a=rsq_a):
                @block.tensor
                def _(t, oc=oc, rb16_a=rb16_a, rsq_a=rsq_a):
                    wt(t, "t", "A", rb16_a)
                    wt(t, "t", "V", rsq_a)
                    t.matmul(PD[0:1, 0:512], on_col16[:, 0:1], rbuf16[:, oc, :],
                             start=(oc == 0), stop=(oc == 7))
                    t.matmul(PO[0:1, 0:512], on_col16[:, 0:1], rsq16[:, oc % 2, :],
                             start=(oc == 0), stop=(oc == 7)).then_inc(SEMS["P"], 1)
                    C["P"] += 1
            wo_units.append(dict(radd=radd, emit=wo_stats_emit))
            if oc > 0:
                wo_units[oc - 1]["emit"]()
                wo_units[oc - 1]["stats"] = C["P"]

        wo_units[7]["emit"]()
        wo_units[7]["stats"] = C["P"]

        wo_radds[b] = radds
        ln_stats = wo_units[-1]["stats"]

        @block.vector
        def _(v, ln_stats=ln_stats):
            wt(v, "v", "P", ln_stats)
            v.tensor_scalar(mu_s, PD[0:1, 0:512], 1.0 / D, None, ALU.mult)
            v.tensor_scalar(var_s, PO[0:1, 0:512], 1.0 / D, None, ALU.mult)
            v.tensor_mul(m2_s, mu_s, mu_s)
            v.tensor_sub(var_s, var_s, m2_s).then_inc(SEMS["V"], 1)
            C["V"] += 1
        ln_v = C["V"]

        @defer_scalar
        def _(a, ln_v=ln_v):
            wt(a, "a", "V", ln_v)
            a.activation(sd_s, var_s, AF.Ln, bias=cvals[0:1, 0:1])
            a.activation(var_s, sd_s, AF.Exp, scale=-0.5).then_inc(SEMS["A"], 1)
        C["A"] += 1
        ln_r = C["A"]

        @block.tensor
        def _(t, ln_r=ln_r, ga=chunk_act.get(b * 24 - 1, 0)):
            wt(t, "t", "A", ln_r)
            if ga:
                wt(t, "t", "A", ga)
            t.matmul(PS[:, 2048:2560], on_row[0:1, 0:128], mu_s, start=True, stop=True)
            t.matmul(PS[:, 2560:3072], on_row[0:1, 0:128], var_s, start=True,
                     stop=True).then_inc(SEMS["P"], 1)
            C["P"] += 1
        ln_bc = C["P"]

        nrm_dmas = {}
        for oc in range(8):
            @block.vector
            def _(v, oc=oc, ln_bc=ln_bc, nrm_dmas=nrm_dmas):
                wt(v, "v", "P", ln_bc)
                if oc >= 4:
                    wt(v, "v", nrm_dmas[oc - 4][0], nrm_dmas[oc - 4][1])
                v.tensor_sub(tmpa[:, :], rbuf[:, oc, :], PS[:, 2048:2560])
                v.tensor_mul(nrm[:, oc % 4, :], tmpa[:, :],
                             PS[:, 2560:3072]).then_inc(SEMS["V"], 1)
                C["V"] += 1
            nrm_v = C["V"]

            @block.gpsimd
            def _(g, b=b, oc=oc, nrm_v=nrm_v, nrm_dmas=nrm_dmas):
                wt(g, "g", "V", nrm_v)
                key = "dN" + str(oc % 4)
                dma(g, key, d_xmid32[b, oc], nrm[:, oc % 4, :])
                dma(g, key, d_xmid16[b, oc], nrm[:, oc % 4, :])
                nrm_dmas[oc] = (key, C[key])
        wo_nrm_V[b] = C["V"]
        wo_done_P = ln_bc

    xmt_lds[3] = emit_xmt_load(3)
    for mo in range(8):
        for g in range(3):
            ci = 3 * 24 + mo * 3 + g
            emit_gate_unit(ci, 3, mo, g, xmt_lds[3],
                           extraV=(wo_nrm_V[3] if (mo == 0 and g == 0) else 0))
        emit_cupdate(3, mo)
    cupd_last[3] = C["V"]

    CFIN = B % 2
    ffn_c_done = C["V"]
    CI = 96

    h_radds = {}
    nrm_dmas5 = {}
    WO_DN = tuple((dk, C[dk]) for dk in ("dN0", "dN1", "dN2", "dN3"))
    ph5_prev_lnv = 0
    ph5_prev_nrmv = 0
    for tt in range(B):
        @block.sync
        def _(s, tt=tt, lastmm=chunk_mm[CI - 1], dn=WO_DN):
            for dk, dv in dn:
                wt(s, "s", dk, dv)
            wt(s, "s", "P", lastmm)
            dma(s, "dM", xmt[:, :, :], d_xmid16[tt].transpose([1, 0, 2]))
        hx_load = C["dM"]

        hu = []
        for oc in range(8):
            ci = CI
            CI += 1
            buf = ci % 2

            @block.tensor
            def _(t, oc=oc, ci=ci, buf=buf, hx_load=hx_load, ffn_c_done=ffn_c_done):
                wt(t, "t", chunk_ld[ci][0], chunk_ld[ci][1])
                wt(t, "t", "dM", hx_load)
                wt(t, "t", "V", ffn_c_done)
                bank = oc % 2
                if oc >= 2:
                    wt(t, "t", "V", hu[oc - 2]["radd"])
                for ic in range(8):
                    t.matmul(PS[:, bank * 512:(bank + 1) * 512],
                             fwc[:, buf, 0, ic, :], xmt[:, ic, :],
                             start=(ic == 0), stop=False)
                for ic in range(8):
                    t.matmul(PS[:, bank * 512:(bank + 1) * 512],
                             fwc[:, buf, 1, ic, :], cT16[:, CFIN, ic, :],
                             start=False, stop=(ic == 7)).then_maybe_inc(
                        (SEMS["P"], 1) if ic == 7 else None)
                pass
                C["P"] += 1
            hmm_ = C["P"]
            chunk_mm[ci] = hmm_
            emit_fwc_load(ci + 2)

            @defer_scalar
            def _(a, oc=oc, hmm_=hmm_):
                wt(a, "a", "P", hmm_)
                a.activation(hc[:, oc % 2, :], PS[:, (oc % 2) * 512:(oc % 2 + 1) * 512],
                             AF.Tanh, bias=bffs[:, 24 + oc:25 + oc]).then_inc(SEMS["A"], 1)
            C["A"] += 1
            hact = C["A"]

            @block.sync
            def _(s, tt=tt, oc=oc):
                key = "dO" + str(oc % 2)
                tenant = h_radds.get((tt, oc - 2)) if oc >= 2 else \
                    (h_radds.get((tt - 1, oc + 6)) if tt > 0 else None)
                if tenant is not None:
                    wt(s, "s", "V", tenant)
                dma(s, key, xm32c[:, oc % 2, :], d_xmid32[tt, oc])
            xm_ld = (("dO" + str(oc % 2)), C["dO" + str(oc % 2)])

            @block.vector
            def _(v, oc=oc, hact=hact, xm_ld=xm_ld):
                wt(v, "v", "A", hact)
                wt(v, "v", xm_ld[0], xm_ld[1])
                v.tensor_add(rbuf[:, oc, :], hc[:, oc % 2, :],
                             xm32c[:, oc % 2, :]).then_inc(SEMS["V"], 1)
                C["V"] += 1
            radd = C["V"]
            h_radds[(tt, oc)] = radd

            @defer_scalar
            def _(a, oc=oc, radd=radd):
                wt(a, "a", "V", radd)
                a.activation(rbuf16[:, oc, :], rbuf[:, oc, :],
                             AF.Identity).then_inc(SEMS["A"], 1)
            C["A"] += 1
            rb16_a = C["A"]

            @block.vector
            def _(v, oc=oc, pstat=(hu[oc - 2]["stats"] if oc >= 2 else 0)):
                if pstat:
                    wt(v, "v", "P", pstat)
                v.tensor_mul(rsq16[:, oc % 2, :], rbuf[:, oc, :],
                             rbuf[:, oc, :]).then_inc(SEMS["V"], 1)
                C["V"] += 1
            rsq_a = C["V"]

            def h_stats_emit(oc=oc, rb16_a=rb16_a, rsq_a=rsq_a,
                             plnv=ph5_prev_lnv):
                @block.tensor
                def _(t, oc=oc, rb16_a=rb16_a, rsq_a=rsq_a, plnv=plnv):
                    if oc == 0 and plnv:
                        wt(t, "t", "V", plnv)
                    wt(t, "t", "A", rb16_a)
                    wt(t, "t", "V", rsq_a)
                    t.matmul(PD[0:1, 0:512], on_col16[:, 0:1], rbuf16[:, oc, :],
                             start=(oc == 0), stop=(oc == 7))
                    t.matmul(PO[0:1, 0:512], on_col16[:, 0:1], rsq16[:, oc % 2, :],
                             start=(oc == 0), stop=(oc == 7)).then_inc(SEMS["P"], 1)
                    C["P"] += 1
            hu.append(dict(radd=radd, emit=h_stats_emit))
            if oc > 0:
                hu[oc - 1]["emit"]()
                hu[oc - 1]["stats"] = C["P"]
        hu[7]["emit"]()
        hu[7]["stats"] = C["P"]

        @block.vector
        def _(v, hu=hu):
            wt(v, "v", "P", hu[-1]["stats"])
            v.tensor_scalar(mu_s, PD[0:1, 0:512], 1.0 / D, None, ALU.mult)
            v.tensor_scalar(var_s, PO[0:1, 0:512], 1.0 / D, None, ALU.mult)
            v.tensor_mul(m2_s, mu_s, mu_s)
            v.tensor_sub(var_s, var_s, m2_s).then_inc(SEMS["V"], 1)
            C["V"] += 1
        ln_v = C["V"]

        @defer_scalar
        def _(a, ln_v=ln_v):
            wt(a, "a", "V", ln_v)
            a.activation(sd_s, var_s, AF.Ln, bias=cvals[0:1, 0:1])
            a.activation(var_s, sd_s, AF.Exp, scale=-0.5).then_inc(SEMS["A"], 1)
        C["A"] += 1
        ln_r = C["A"]

        @block.tensor
        def _(t, ln_r=ln_r, pnv=ph5_prev_nrmv):
            wt(t, "t", "A", ln_r)
            if pnv:
                wt(t, "t", "V", pnv)
            t.matmul(PS[:, 2048:2560], on_row[0:1, 0:128], mu_s, start=True, stop=True)
            t.matmul(PS[:, 2560:3072], on_row[0:1, 0:128], var_s, start=True,
                     stop=True).then_inc(SEMS["P"], 1)
            C["P"] += 1
        ln_bc = C["P"]

        for oc in range(8):
            @block.vector
            def _(v, tt=tt, oc=oc, ln_bc=ln_bc):
                wt(v, "v", "P", ln_bc)
                tenant = nrm_dmas5.get((tt, oc - 4)) if oc >= 4 else \
                    (nrm_dmas5.get((tt - 1, oc + 4)) if tt > 0 else None)
                if tenant is not None:
                    wt(v, "v", tenant[0], tenant[1])
                v.tensor_sub(tmpa[:, :], rbuf[:, oc, :], PS[:, 2048:2560])
                v.tensor_mul(nrm[:, oc % 4, :], tmpa[:, :],
                             PS[:, 2560:3072]).then_inc(SEMS["V"], 1)
                C["V"] += 1
            nrm_v = C["V"]

            @block.gpsimd
            def _(g, tt=tt, oc=oc, nrm_v=nrm_v):
                wt(g, "g", "V", nrm_v)
                key = "dN" + str(oc % 4)
                dma(g, key, d_yT[tt, oc], nrm[:, oc % 4, :])
                nrm_dmas5[(tt, oc)] = (key, C[key])
        ph5_prev_lnv = ln_v
        ph5_prev_nrmv = C["V"]

    @block.gpsimd
    def _(g):
        for dk in ("dN0", "dN1", "dN2", "dN3"):
            wt(g, "g", dk, C[dk])

    @block.scalar
    def _(a):
        for fn in SCALAR_OPS:
            fn(a)

    ctx.close()
    return nc


def _host_prep(inputs):
    f32 = np.float32
    x = np.asarray(inputs["L0_attn_x"], f32)
    xT = np.ascontiguousarray(x.transpose(0, 2, 1))

    half = 32
    inv = (1.0 / (np.float32(10000.0) ** (np.arange(half, dtype=f32) / np.float32(half)))).astype(f32)

    def tables(pos, scale):
        ang = (pos.astype(f32)[:, None] * inv[None, :]).astype(f32)
        cos = (np.cos(ang).astype(f32) * f32(scale)).astype(f32)
        sin = (np.sin(ang).astype(f32) * f32(scale)).astype(f32)
        out = np.zeros((4, 128, 1024), f32)
        for s in range(4):
            out[s, :, 0:512] = np.tile(cos[s * 128:(s + 1) * 128], (1, 16))
            out[s, :, 512:1024] = np.tile(sin[s * 128:(s + 1) * 128], (1, 16))
        return out

    wq = np.asarray(inputs["L0_attn_W_q"], f32)
    wdkv = np.asarray(inputs["L0_attn_W_dkv"], f32)
    wuk = np.asarray(inputs["L0_attn_W_uk"], f32)
    wuv = np.asarray(inputs["L0_attn_W_uv"], f32)
    wo = np.asarray(inputs["L0_attn_W_o"], f32)
    gam = np.asarray(inputs["L0_attn_ln_gamma"], f32)
    bet = np.asarray(inputs["L0_attn_ln_beta"], f32)
    fwT = np.ascontiguousarray(np.stack([
        np.asarray(inputs["L0_ffn_wf"], f32).T, np.asarray(inputs["L0_ffn_uf"], f32).T,
        np.asarray(inputs["L0_ffn_wi"], f32).T, np.asarray(inputs["L0_ffn_ui"], f32).T,
        np.asarray(inputs["L0_ffn_wc"], f32).T, np.asarray(inputs["L0_ffn_uc"], f32).T,
        np.asarray(inputs["L0_ffn_wo"], f32).T, np.asarray(inputs["L0_ffn_uo"], f32).T,
    ]))
    bias4 = np.stack([np.asarray(inputs["L0_ffn_" + k], f32) for k in ("bf", "bi", "bc", "bo")])
    bff = np.ascontiguousarray(bias4.reshape(4, 8, 128).transpose(2, 0, 1).reshape(128, 32))
    incol = np.ascontiguousarray(np.asarray(inputs["L0_ffn_in_c"], f32).reshape(8, 128).T)
    gcol = np.ascontiguousarray(gam.reshape(2, 128).T)
    bcol = np.ascontiguousarray(bet.reshape(2, 128).T)

    shared = dict(wq=wq, wdkv=wdkv, wuk=wuk, wuv=wuv, wo=wo, fwT=fwT,
                  bff=bff, incol=incol, gcol=gcol, bcol=bcol,
                  idt=np.eye(128, dtype=f32))

    in_maps = []
    for i in range(NC):
        xq_i = np.ascontiguousarray(xT[:, :, i::8])
        xkv_i = np.ascontiguousarray(xT[:, :, 512 * i:512 * (i + 1)])
        pos_q = np.arange(i, T, 8)
        pos_kv = np.arange(512 * i, 512 * (i + 1))
        k_loc = np.arange(128)
        j_loc = np.arange(16)
        mm_ = (k_loc[:, None] <= (8 * j_loc[None, :] + i))
        maskb = np.zeros((128, 2, 2, 32), np.float32)
        maskb[:, 0, :, 0:16] = mm_[:, None, :]
        maskb[:, 0, :, 16:32] = 1.0
        maskb[:, 1, :, 0:16] = 0.0
        maskb[:, 1, :, 16:32] = mm_[:, None, :]
        m = dict(shared)
        m["xq"] = xq_i
        m["xkv"] = xkv_i
        m["ropek"] = tables(pos_kv, 1.0)
        m["ropeq"] = tables(pos_q, 0.125)
        m["maskb"] = np.ascontiguousarray(maskb.reshape(128, 128)).astype(ml_dtypes.bfloat16)
        in_maps.append(m)
    return in_maps


def kernel(**inputs):
    in_maps = _host_prep(inputs)
    nc = build_program()
    res = run_bass_kernel_spmd(nc, in_maps, core_ids=list(range(NC)))
    outT = np.empty((B, D, T), np.float32)
    for i in range(NC):
        outT[:, :, i::8] = res.results[i]["yT"].reshape(B, D, 512)
    return np.ascontiguousarray(outT.transpose(0, 2, 1))


# version 26 (final)
# speedup vs baseline: 1.1882x; 1.0397x over previous
import numpy as np
import ml_dtypes
from contextlib import ExitStack

import concourse.bass as bass
import concourse.mybir as mybir
from concourse.bass_utils import run_bass_kernel_spmd

F32 = mybir.dt.float32
BF16 = mybir.dt.bfloat16
AF = mybir.ActivationFunctionType
ALU = mybir.AluOpType

B = 4
T = 4096
D = 1024
LT = 256
NC = 8
EPS = 1e-5


def build_program():
    nc = bass.Bass("TRN2", target_bir_lowering=False, debug=False,
                   detect_race_conditions=False)

    d_xkv = nc.dram_tensor("xkv", [B, D, 512], F32, kind="ExternalInput").ap()
    d_xq = nc.dram_tensor("xq", [B, D, 512], F32, kind="ExternalInput").ap()
    d_wq = nc.dram_tensor("wq", [D, D], F32, kind="ExternalInput").ap()
    d_wdkv = nc.dram_tensor("wdkv", [D, LT], F32, kind="ExternalInput").ap()
    d_wuk = nc.dram_tensor("wuk", [LT, D], F32, kind="ExternalInput").ap()
    d_wuv = nc.dram_tensor("wuv", [LT, D], F32, kind="ExternalInput").ap()
    d_wo = nc.dram_tensor("wo", [D, D], F32, kind="ExternalInput").ap()
    d_fwT = nc.dram_tensor("fwT", [8, D, D], F32, kind="ExternalInput").ap()
    d_bff = nc.dram_tensor("bff", [128, 32], F32, kind="ExternalInput").ap()
    d_incol = nc.dram_tensor("incol", [128, 8], F32, kind="ExternalInput").ap()
    d_gcol = nc.dram_tensor("gcol", [128, 2], F32, kind="ExternalInput").ap()
    d_bcol = nc.dram_tensor("bcol", [128, 2], F32, kind="ExternalInput").ap()
    d_ropek = nc.dram_tensor("ropek", [4, 128, 1024], F32, kind="ExternalInput").ap()
    d_ropeq = nc.dram_tensor("ropeq", [4, 128, 1024], F32, kind="ExternalInput").ap()
    d_maskb = nc.dram_tensor("maskb", [128, 128], BF16, kind="ExternalInput").ap()
    d_idt = nc.dram_tensor("idt", [128, 128], F32, kind="ExternalInput").ap()

    d_kvsh = nc.dram_tensor("kvsh", [B, 8, 2, 128, 4, 128], BF16).ap()
    d_kvfull = nc.dram_tensor("kvfull", [B, NC, 8, 2, 128, 4, 128], BF16,
                              addr_space="Shared").ap()
    d_stsh = nc.dram_tensor("stsh", [B, 128], F32).ap()
    d_stfull = nc.dram_tensor("stfull", [B, NC, 128], F32).ap()
    d_fcache = nc.dram_tensor("fcache", [8, 8, 128, 8, 128], BF16).ap()
    d_xmid32 = nc.dram_tensor("xmid32", [B, 8, 128, 512], F32).ap()
    d_xmid16 = nc.dram_tensor("xmid16", [B, 8, 128, 512], BF16).ap()
    d_yT = nc.dram_tensor("yT", [B, 8, 128, 512], F32, kind="ExternalOutput").ap()

    ctx = ExitStack()
    sb = lambda name, shape, dt: ctx.enter_context(nc.sbuf_tensor(name, shape, dt))
    ps = lambda name, shape: ctx.enter_context(nc.psum_tensor(name, shape, F32))

    on_mat = sb("on_mat", [128, 64], BF16)
    on_row = sb("on_row", [1, 128], F32)
    on_col = sb("on_col", [128, 1], F32)
    w_o = sb("w_o", [128, 8, 1024], BF16)
    bffs = sb("bffs", [128, 32], F32)
    incs = sb("incs", [128, 8], F32)
    gcs = sb("gcs", [128, 2], F32)
    bcs = sb("bcs", [128, 2], F32)
    masks = sb("masks", [128, 2, 2, 32], BF16)
    biasB = sb("biasB", [128, 32], F32)
    Mq = sb("Mq", [128, 32], F32)
    Mk = sb("Mk", [128, 4], F32)
    MqR = sb("MqR", [128, 32], F32)
    MkR = sb("MkR", [128, 4], F32)
    sqMq = sb("sqMq", [128, 32], F32)
    kmb = sb("kmb", [128, 4], F32)
    strd = sb("strd", [1, 64], F32)
    kmaxg = sb("kmaxg", [1, 8], F32)
    mrow = sb("mrow", [1, 40], F32)
    idt = sb("idt_s", [128, 128], F32)
    lnscr = sb("lnscr", [1, 1024], F32)
    tmpa = sb("tmpa", [128, 512], F32)
    tmpb = sb("tmpb", [128, 512], F32)
    cvals = sb("cvals", [128, 2], F32)
    zrow = sb("zrow", [1, 512], BF16)
    on_row16 = sb("on_row16", [1, 128], BF16)
    on_col16 = sb("on_col16", [128, 1], BF16)
    qhat = sb("qhat", [128, B, 8, 4, 128], BF16)

    bctx = ExitStack()
    bsb = lambda name, shape, dt: bctx.enter_context(nc.sbuf_tensor(name, shape, dt))
    w_q = bsb("w_q", [128, 8, 1024], BF16)
    w_dkv = bsb("w_dkv", [128, 8, 256], BF16)
    w_uk = bsb("w_uk", [128, 2, 1024], BF16)
    w_uv = bsb("w_uv", [128, 2, 1024], BF16)
    xkv_s = bsb("xkv_s", [128, 2, 8, 512], BF16)
    xq_s = bsb("xq_s", [128, 2, 8, 512], BF16)
    ltf = bsb("ltf", [128, 2, 512], F32)
    ltsq = bsb("ltsq", [128, 2, 512], F32)
    latn = bsb("latn", [128, 2, 2, 512], BF16)
    ropbk = bsb("ropbk", [128, 4, 1024], F32)
    ropbq = bsb("ropbq", [128, 4, 1024], F32)
    krop = bsb("krop", [128, 4, 1024], BF16)
    qrop = bsb("qrop", [128, 4, 1024], BF16)
    kts = bsb("kts", [128, 4, 1024], BF16)
    vst = bsb("vst", [128, 4, 1024], BF16)
    sqk = bsb("sqk", [128, 1024], F32)
    sqq = bsb("sqq", [128, 1024], F32)
    n16 = bsb("n16", [128, 16], F32)
    n8 = bsb("n8", [128, 8], F32)
    ncol = bsb("ncol", [128, 4], F32)

    PS = ps("PS", [128, 3072])
    PO = ps("PO", [128, 512])
    PD = ps("PD", [128, 512])

    SEMNAMES = [
        "dW",
        "dWa",
        "dX0", "dX1",
        "dC",
        "dT0", "dT1",
        "dK0", "dK1", "dK2", "dK3",
        "dQ0", "dQ1", "dQ2", "dQ3",
        "dH0", "dH1", "dH2", "dH3",
        "dSS",
        "dV0", "dV1",
        "dR0", "dR1",
        "dN0", "dN1", "dN2", "dN3",
        "dM",
        "dO0", "dO1",
        "dF0", "dF1",
        "P", "A", "V", "G", "CC",
    ]
    SEMS = {}
    for nm in SEMNAMES:
        SEMS[nm] = ctx.enter_context(nc.semaphore("s_" + nm))
    C = {nm: 0 for nm in SEMNAMES}
    lastw = {}

    def wt(eng, ename, key, val):
        if val <= 0:
            return
        if lastw.get((ename, key), -1) >= val:
            return
        eng.wait_ge(SEMS[key], val)
        lastw[(ename, key)] = val

    def dma(eng, key, out, in_, transpose=False):
        if transpose:
            eng.dma_start(out=out, in_=in_, transpose=True).then_inc(SEMS[key], 16)
        else:
            eng.dma_start(out=out, in_=in_).then_inc(SEMS[key], 16)
        C[key] += 16
        return C[key]

    block = ctx.enter_context(nc.Block())

    SCALAR_OPS = []

    def defer_scalar(fn):
        SCALAR_OPS.append(fn)
        return None

    @block.gpsimd
    def _(g):
        g.memset(on_mat[:, :], 1.0)
        g.memset(on_row[:, :], 1.0)
        g.memset(on_col[:, :], 1.0)
        g.memset(Mk[:, :], 0.0)
        g.memset(Mq[:, :], 0.0)
        g.memset(cvals[:, 0:1], EPS)
        g.memset(cvals[:, 1:2], 1e-30)
        g.memset(zrow[:, :], 0.0)
        g.memset(on_row16[:, :], 1.0)
        g.memset(on_col16[:, :], 1.0).then_inc(SEMS["G"], 1)
        C["G"] += 1
        for dst, src in (
            (w_dkv[:, :, :], d_wdkv), (w_q[:, :, :], d_wq),
            (w_uk[:, :, :], d_wuk), (w_uv[:, :, :], d_wuv),
        ):
            dma(g, "dWa", dst, src.rearrange("(ic p) d -> p ic d", p=128))
        dma(g, "dW", w_o[:, :, :], d_wo.rearrange("(ic p) d -> p ic d", p=128))
        dma(g, "dW", masks[:, :, :, :], d_maskb.rearrange("p (par two c) -> p par two c", par=2, c=32))
    WA_DONE = C["dWa"]

    @block.sync
    def _(s):
        dma(s, "dC", idt[:, :], d_idt)
        dma(s, "dC", bffs[:, :], d_bff)
        dma(s, "dC", incs[:, :], d_incol)
        dma(s, "dC", gcs[:, :], d_gcol)
        dma(s, "dC", bcs[:, :], d_bcol)
        dma(s, "dC", ropbk[:, :, :], d_ropek.transpose([1, 0, 2]))
        dma(s, "dC", ropbq[:, :, :], d_ropeq.transpose([1, 0, 2]))
    CONST_DONE = C["dC"]

    @block.gpsimd
    def _(g):
        dma(g, "dX0", xkv_s[:, 0, :, :], d_xkv[0].rearrange("(ic p) t -> p ic t", p=128))
        dma(g, "dX0", xq_s[:, 0, :, :], d_xq[0].rearrange("(ic p) t -> p ic t", p=128))
    XLOAD = {0: ("dX0", C["dX0"])}

    slot_units = {}
    lat_units = {}
    lat_norm = 0
    ph2_V = {}
    SQMQ_B = {}
    CC_STATS = {}
    CC_KV = {}
    BIAS_B = {}

    mu_s = lnscr[0:1, 0:512]
    var_s = lnscr[0:1, 512:1024]
    m2_s = tmpa[0:1, 0:512]
    sd_s = tmpb[0:1, 0:512]

    def emit_lat_chain(b):
        bb = b % 2

        @block.tensor
        def _(t, b=b, bb=bb):
            wt(t, "t", "dWa", WA_DONE)
            wt(t, "t", "G", 1)
            wt(t, "t", XLOAD[b][0], XLOAD[b][1])
            if b > 0:
                wt(t, "t", "V", lat_units[b - 1]["norm_done"])
            for lh in range(2):
                for ic in range(8):
                    t.matmul(PS[:, 1024 + lh * 512:1024 + (lh + 1) * 512],
                             w_dkv[:, ic, lh * 128:(lh + 1) * 128],
                             xkv_s[:, bb, ic, :],
                             start=(ic == 0), stop=(ic == 7)).then_maybe_inc(
                        (SEMS["P"], 1) if (lh == 1 and ic == 7) else None)
            pass
            C["P"] += 1
        lat_mm = C["P"]

        @block.vector
        def _(v, lat_mm=lat_mm):
            wt(v, "v", "P", lat_mm)
            v.tensor_copy(ltf[:, 0, :], PS[:, 1024:1536])
            v.tensor_copy(ltf[:, 1, :], PS[:, 1536:2048]).then_inc(SEMS["V"], 1)
            C["V"] += 1

        @block.vector
        def _(v):
            v.tensor_mul(ltsq[:, 0, :], ltf[:, 0, :], ltf[:, 0, :])
            v.tensor_mul(ltsq[:, 1, :], ltf[:, 1, :],
                         ltf[:, 1, :]).then_inc(SEMS["V"], 1)
            C["V"] += 1
        lat_sq = C["V"]

        @block.tensor
        def _(t, lat_sq=lat_sq):
            wt(t, "t", "V", lat_sq)
            for lh in range(2):
                t.matmul(PS[0:1, 1024:1536], on_col[:, 0:1], ltf[:, lh, :],
                         start=(lh == 0), stop=(lh == 1))
            for lh in range(2):
                t.matmul(PS[0:1, 1536:2048], on_col[:, 0:1], ltsq[:, lh, :],
                         start=(lh == 0), stop=(lh == 1)).then_maybe_inc(
                    (SEMS["P"], 1) if lh == 1 else None)
            pass
            C["P"] += 1
        lat_stats = C["P"]

        @block.vector
        def _(v, lat_stats=lat_stats):
            wt(v, "v", "P", lat_stats)
            v.tensor_scalar(mu_s, PS[0:1, 1024:1536], 1.0 / LT, None, ALU.mult)
            v.tensor_scalar(var_s, PS[0:1, 1536:2048], 1.0 / LT, None, ALU.mult)
            v.tensor_mul(m2_s, mu_s, mu_s)
            v.tensor_sub(var_s, var_s, m2_s).then_inc(SEMS["V"], 1)
            C["V"] += 1
        lat_v1 = C["V"]

        @defer_scalar
        def _(a, lat_v1=lat_v1):
            wt(a, "a", "V", lat_v1)
            a.activation(sd_s, var_s, AF.Ln, bias=cvals[0:1, 0:1])
            a.activation(var_s, sd_s, AF.Exp, scale=-0.5).then_inc(SEMS["A"], 1)
        C["A"] += 1
        lat_rstd = C["A"]

        @block.tensor
        def _(t, lat_rstd=lat_rstd):
            wt(t, "t", "A", lat_rstd)
            t.matmul(PS[:, 1024:1536], on_row[0:1, 0:128], mu_s, start=True, stop=True)
            t.matmul(PS[:, 1536:2048], on_row[0:1, 0:128], var_s, start=True,
                     stop=True).then_inc(SEMS["P"], 1)
            C["P"] += 1
        lat_bc = C["P"]

        @block.vector
        def _(v, bb=bb, lat_bc=lat_bc):
            wt(v, "v", "P", lat_bc)
            wt(v, "v", "dC", CONST_DONE)
            for lh in range(2):
                v.tensor_sub(tmpa[:, :], ltf[:, lh, :], PS[:, 1024:1536])
                v.tensor_mul(tmpb[:, :], tmpa[:, :], PS[:, 1536:2048])
                v.tensor_scalar(latn[:, bb, lh, :], tmpb[:, :],
                                gcs[:, lh:lh + 1], bcs[:, lh:lh + 1],
                                ALU.mult, ALU.add).then_maybe_inc(
                    (SEMS["V"], 1) if lh == 1 else None)
            pass
            C["V"] += 1
        lat_units[b] = {"norm_done": C["V"]}

    for b in range(B):
        bb = b % 2

        if b + 1 < B:
            @block.gpsimd
            def _(g, b=b):
                nb = (b + 1) % 2
                key = "dX" + str(nb)
                if b + 1 >= 2:
                    wt(g, "g", "P", slot_units[(b - 1, 3)]["q_mm"])
                dma(g, key, xkv_s[:, nb, :, :],
                    d_xkv[b + 1].rearrange("(ic p) t -> p ic t", p=128))
                dma(g, key, xq_s[:, nb, :, :],
                    d_xq[b + 1].rearrange("(ic p) t -> p ic t", p=128))
                XLOAD[b + 1] = (key, C[key])

        if b == 0:
            emit_lat_chain(0)
        lat_norm = lat_units[b]["norm_done"]

        for s_ in range(4):
            par = s_
            prev = slot_units.get((b, s_ - 1)) if s_ > 0 else \
                (slot_units.get((b - 1, 3)) if b > 0 else None)
            prev2 = slot_units.get((b - 1, s_)) if b > 0 else None

            @block.tensor
            def _(t, bb=bb, s_=s_, prev=prev, lat_norm=lat_norm):
                wt(t, "t", "V", lat_norm)
                if prev is not None:
                    wt(t, "t", "V", prev["vevac_done"])
                for dh in range(2):
                    for lh in range(2):
                        t.matmul(PS[:, dh * 512:(dh + 1) * 512],
                                 latn[:, bb, lh, s_ * 128:s_ * 128 + 128],
                                 w_uk[:, lh, dh * 512:(dh + 1) * 512],
                                 start=(lh == 0), stop=(lh == 1))
                for dh in range(2):
                    dst = PO if dh == 0 else PD
                    for lh in range(2):
                        t.matmul(dst[:, :],
                                 latn[:, bb, lh, s_ * 128:s_ * 128 + 128],
                                 w_uv[:, lh, dh * 512:(dh + 1) * 512],
                                 start=(lh == 0), stop=(lh == 1)).then_maybe_inc(
                            (SEMS["P"], 1) if (dh == 1 and lh == 1) else None)
                pass
                C["P"] += 1
            kv_mm = C["P"]

            @block.tensor
            def _(t, b=b, bb=bb, s_=s_, prev=prev):
                if prev is not None:
                    wt(t, "t", "V", prev["ropeq_done"])
                if s_ == 0 and b > 0:
                    wt(t, "t", "V", ph2_V[b - 1])
                for dh in range(2):
                    for ic in range(8):
                        t.matmul(PS[:, 2048 + dh * 512:2048 + (dh + 1) * 512],
                                 xq_s[:, bb, ic, s_ * 128:s_ * 128 + 128],
                                 w_q[:, ic, dh * 512:(dh + 1) * 512],
                                 start=(ic == 0), stop=(ic == 7)).then_maybe_inc(
                            (SEMS["P"], 1) if (dh == 1 and ic == 7) else None)
                pass
                C["P"] += 1
            q_mm = C["P"]

            def rope_emit(v, psoff, tblbuf, tbl, dst, dstbuf):
                ap = PS[:, psoff:psoff + 1024].rearrange(
                    "p (h two q) -> p h two q", h=16, two=2)
                x1 = ap[:, :, 0, :]
                x2 = ap[:, :, 1, :]
                cos = tbl[:, tblbuf, 0:512].rearrange("p (h q) -> p h q", h=16)
                sin = tbl[:, tblbuf, 512:1024].rearrange("p (h q) -> p h q", h=16)
                ta = tmpa[:, :].rearrange("p (h q) -> p h q", h=16)
                tb = tmpb[:, :].rearrange("p (h q) -> p h q", h=16)
                dap = dst[:, dstbuf, :].rearrange("p (h two q) -> p h two q", h=16, two=2)
                v.tensor_mul(ta, x1, cos)
                v.tensor_mul(tb, x2, sin)
                v.tensor_sub(dap[:, :, 0, :], ta, tb)
                v.tensor_mul(ta, x1, sin)
                v.tensor_mul(tb, x2, cos)
                return v.tensor_add(dap[:, :, 1, :], ta, tb)

            @block.vector
            def _(v, s_=s_, par=par, kv_mm=kv_mm, prev2=prev2):
                wt(v, "v", "P", kv_mm)
                wt(v, "v", "dC", CONST_DONE)
                if prev2 is not None:
                    wt(v, "v", "dK" + str(par), prev2["ktr_done"])
                rope_emit(v, 0, s_, ropbk, krop, par).then_inc(SEMS["V"], 1)
                C["V"] += 1
            ropek_done = C["V"]

            @block.vector
            def _(v, par=par, prev2=prev2):
                if prev2 is not None:
                    wt(v, "v", "dH" + str(par), prev2["shard_done"])
                v.tensor_copy(vst[:, par, 0:512], PO[:, :])
                v.tensor_copy(vst[:, par, 512:1024], PD[:, :]).then_inc(SEMS["V"], 1)
                C["V"] += 1
            vevac_done = C["V"]

            @block.vector
            def _(v, s_=s_, par=par, q_mm=q_mm, prev2=prev2):
                wt(v, "v", "P", q_mm)
                if prev2 is not None:
                    wt(v, "v", "dQ" + str(par), prev2["qtr_done"])
                rope_emit(v, 2048, s_, ropbq, qrop, par).then_inc(SEMS["V"], 1)
                C["V"] += 1
            ropeq_done = C["V"]

            @block.vector
            def _(v, par=par):
                v.tensor_mul(sqk[:, :], krop[:, par, :], krop[:, par, :]).then_inc(SEMS["V"], 1)
                C["V"] += 1
                v.tensor_mul(sqq[:, :], qrop[:, par, :], qrop[:, par, :]).then_inc(SEMS["V"], 1)
                C["V"] += 1
            sq_done = C["V"]

            @block.vector
            def _(v, b=b, sq_done=sq_done):
                v.reduce_sum(n16[:, :], sqk[:, :].rearrange("p (h q) -> p h q", h=16),
                             axis=mybir.AxisListType.X)
                v.reduce_max(ncol[:, 0:1], n16[:, :], axis=mybir.AxisListType.X)
                v.tensor_max(Mk[:, b:b + 1], Mk[:, b:b + 1], ncol[:, 0:1])
                v.reduce_sum(n16[:, :], sqq[:, :].rearrange("p (h q) -> p h q", h=16),
                             axis=mybir.AxisListType.X)
                v.reduce_max(n8[:, :], n16[:, :].rearrange("p (pr two) -> p pr two", two=2),
                             axis=mybir.AxisListType.X)
                v.tensor_max(Mq[:, b * 8:(b + 1) * 8], Mq[:, b * 8:(b + 1) * 8],
                             n8[:, :]).then_inc(SEMS["V"], 1)
                C["V"] += 1
            norms_done = C["V"]

            @block.sync
            def _(s, par=par, ropek_done=ropek_done, prev2=prev2):
                wt(s, "s", "V", ropek_done)
                if prev2 is not None:
                    wt(s, "s", "dH" + str(par), prev2["shard_done"])
                dma(s, "dK" + str(par),
                    kts[:, par, :].rearrange("p (pr c) -> p pr c", c=128),
                    krop[:, par, :].rearrange("p (pr c) -> p pr c", c=128),
                    transpose=True)
            ktr_done = C["dK" + str(par)]

            @block.sync
            def _(s, b=b, s_=s_, par=par, ropeq_done=ropeq_done):
                wt(s, "s", "V", ropeq_done)
                dma(s, "dQ" + str(par), qhat[:, b, :, s_, :],
                    qrop[:, par, :].rearrange("p (pr c) -> p pr c", c=128),
                    transpose=True)
            qtr_done = C["dQ" + str(par)]

            @block.gpsimd
            def _(g, b=b, s_=s_, par=par, ktr_done=ktr_done, vevac_done=vevac_done):
                wt(g, "g", "dK" + str(par), ktr_done)
                dma(g, "dH" + str(par), d_kvsh[b, :, 0, :, s_, :].transpose([1, 0, 2]),
                    kts[:, par, :].rearrange("p (pr c) -> p pr c", c=128))
                wt(g, "g", "V", vevac_done)
                dma(g, "dH" + str(par), d_kvsh[b, :, 1, :, s_, :].transpose([1, 0, 2]),
                    vst[:, par, :].rearrange("p (pr c) -> p pr c", c=128))
            shard_done = C["dH" + str(par)]

            slot_units[(b, s_)] = dict(
                ropek_done=ropek_done, ropeq_done=ropeq_done, vevac_done=vevac_done,
                norms_done=norms_done, ktr_done=ktr_done, qtr_done=qtr_done,
                shard_done=shard_done, q_mm=q_mm,
            )

            if s_ == 2 and b + 1 < B:
                emit_lat_chain(b + 1)

        nd = slot_units[(b, 3)]["norms_done"]

        @block.tensor
        def _(t, b=b, nd=nd):
            wt(t, "t", "V", nd)
            wt(t, "t", "dC", CONST_DONE)
            t.transpose(PS[0:8, 2048:2176], Mq[:, 8 * b:8 * b + 8], idt[:, :])
            t.transpose(PS[0:1, 2176:2304], Mk[:, b:b + 1],
                        idt[:, :]).then_inc(SEMS["P"], 1)
            C["P"] += 1
        tr1 = C["P"]

        @block.vector
        def _(v, b=b, tr1=tr1):
            wt(v, "v", "P", tr1)
            v.reduce_max(tmpa[0:8, 0:1], PS[0:8, 2048:2176], axis=mybir.AxisListType.X)
            v.reduce_max(tmpb[0:1, 0:1], PS[0:1, 2176:2304],
                         axis=mybir.AxisListType.X)
            v.tensor_copy(mrow[0:1, 32 + b:33 + b],
                          tmpb[0:1, 0:1]).then_inc(SEMS["V"], 1)
            C["V"] += 1
        rmx = C["V"]

        @block.tensor
        def _(t, rmx=rmx):
            wt(t, "t", "V", rmx)
            t.transpose(PS[0:1, 2560:2568], tmpa[0:8, 0:1],
                        idt[0:8, 0:8]).then_inc(SEMS["P"], 1)
            C["P"] += 1
        tr2 = C["P"]

        @block.vector
        def _(v, tr2=tr2):
            wt(v, "v", "P", tr2)
            v.tensor_copy(mrow[0:1, 0:8], PS[0:1, 2560:2568]).then_inc(SEMS["V"], 1)
            C["V"] += 1
        cpy = C["V"]

        @block.tensor
        def _(t, cpy=cpy):
            wt(t, "t", "V", cpy)
            t.matmul(PS[:, 2568:2576], on_row[0:1, 0:128], mrow[0:1, 0:8],
                     start=True, stop=True).then_inc(SEMS["P"], 1)
            C["P"] += 1
        bcm = C["P"]

        @block.vector
        def _(v, b=b, bcm=bcm):
            wt(v, "v", "P", bcm)
            v.tensor_copy(MqR[:, 8 * b:8 * b + 8],
                          PS[:, 2568:2576]).then_inc(SEMS["V"], 1)
            C["V"] += 1
        ph2_V[b] = C["V"]

        @defer_scalar
        def _(a, b=b, pv=ph2_V[b]):
            wt(a, "a", "V", pv)
            a.activation(sqMq[:, 8 * b:8 * b + 8], MqR[:, 8 * b:8 * b + 8],
                         AF.Ln, bias=cvals[:, 1:2])
            a.activation(sqMq[:, 8 * b:8 * b + 8], sqMq[:, 8 * b:8 * b + 8],
                         AF.Exp, scale=0.5).then_inc(SEMS["A"], 1)
        C["A"] += 1
        SQMQ_B[b] = C["A"]

        @block.gpsimd
        def _(g, b=b, rmx=rmx,
              kvdh=tuple((k, C[k]) for k in ("dH0", "dH1", "dH2", "dH3"))):
            wt(g, "g", "V", rmx)
            dma(g, "dSS", d_stsh[b:b + 1, 1:128], on_row[0:1, 1:128])
            dma(g, "dSS", d_stsh[b:b + 1, 0:1], mrow[0:1, 32 + b:33 + b])
            wt(g, "g", "dSS", C["dSS"])
            g.collective_compute(
                "AllGather", ALU.bypass, replica_groups=[list(range(NC))],
                ins=[d_stsh[b:b + 1, :]], outs=[d_stfull[b]],
            ).then_inc(SEMS["CC"], 1)
            C["CC"] += 1
            if b == 0:
                for hk, hv in kvdh:
                    wt(g, "g", hk, hv)
                g.collective_compute(
                    "AllGather", ALU.bypass, replica_groups=[list(range(NC))],
                    ins=[d_kvsh[0]], outs=[d_kvfull[0]],
                ).then_inc(SEMS["CC"], 1)
                C["CC"] += 1
        CC_STATS[b] = C["CC"] - (1 if b == 0 else 0)
        if b == 0:
            CC_KV[b] = C["CC"]

            @block.gpsimd
            def _(g):
                for fm in range(8):
                    for foc in range(8):
                        dma(g, "dW", d_fcache[fm, foc],
                            d_fwT[fm].rearrange("(ic p) (oc c) -> oc p ic c",
                                                p=128, c=128)[foc])

    @block.gpsimd
    def _(g):
        for hk in ("dH0", "dH1", "dH2", "dH3"):
            wt(g, "g", hk, C[hk])
        for bq in range(1, B):
            g.collective_compute(
                "AllGather", ALU.bypass, replica_groups=[list(range(NC))],
                ins=[d_kvsh[bq]], outs=[d_kvfull[bq]],
            ).then_inc(SEMS["CC"], 1)
            C["CC"] += 1
            CC_KV[bq] = C["CC"]

    W_DONE = C["dW"]
    LAST_NORMS = slot_units[(B - 1, 3)]["norms_done"]

    kmb_done = 0
    for b in range(B):
        @block.sync
        def _(s, b=b, cs=CC_STATS[b]):
            wt(s, "s", "CC", cs)
            s.dma_start(out=strd[0:1, b * 8:b * 8 + 8],
                        in_=d_stfull[b, :, 0:1].transpose([1, 0]),
                        allow_slow_non_contiguous=True).then_inc(SEMS["dC"], 16)
            C["dC"] += 16

        @block.vector
        def _(v, b=b, sl=C["dC"]):
            wt(v, "v", "dC", sl)
            v.reduce_max(kmaxg[0:1, b:b + 1], strd[0:1, b * 8:b * 8 + 8],
                         axis=mybir.AxisListType.X).then_inc(SEMS["V"], 1)
            C["V"] += 1
        kmg = C["V"]

        @defer_scalar
        def _(a, b=b, kmg=kmg):
            wt(a, "a", "V", kmg)
            a.activation(kmaxg[0:1, 4 + b:5 + b], kmaxg[0:1, b:b + 1],
                         AF.Ln, bias=cvals[0:1, 1:2])
            a.activation(kmaxg[0:1, b:b + 1], kmaxg[0:1, 4 + b:5 + b],
                         AF.Exp, scale=0.5).then_inc(SEMS["A"], 1)
        C["A"] += 1
        ksq = C["A"]

        @block.tensor
        def _(t, b=b, ksq=ksq, pk=kmb_done):
            wt(t, "t", "A", ksq)
            if b == 0:
                wt(t, "t", "V", ph2_V[B - 1])
            if pk:
                wt(t, "t", "V", pk)
            t.matmul(PS[:, 1024 + b:1025 + b], on_row[0:1, 0:128],
                     kmaxg[0:1, b:b + 1], start=True,
                     stop=True).then_inc(SEMS["P"], 1)
            C["P"] += 1
        kmm = C["P"]

        @block.vector
        def _(v, b=b, kmm=kmm, sq=SQMQ_B[b]):
            wt(v, "v", "P", kmm)
            v.tensor_copy(kmb[:, b:b + 1], PS[:, 1024 + b:1025 + b])
            wt(v, "v", "A", sq)
            v.tensor_scalar(biasB[:, b * 8:(b + 1) * 8], sqMq[:, b * 8:(b + 1) * 8],
                            kmb[:, b:b + 1], -1.0, ALU.mult,
                            ALU.mult).then_inc(SEMS["V"], 1)
            C["V"] += 1
        kmb_done = C["V"]
        BIAS_B[b] = C["V"]
    BIAS_DONE = BIAS_B[B - 1]

    BAR = dict(C)

    def barrier_waits(eng, ename):
        for k in SEMNAMES:
            if k != "CC":
                wt(eng, ename, k, BAR[k])

    @block.tensor
    def _(t):
        barrier_waits(t, "t")

    @block.vector
    def _(v):
        barrier_waits(v, "v")

    @defer_scalar
    def _(a):
        barrier_waits(a, "a")

    @block.sync
    def _(s):
        barrier_waits(s, "s")

    @block.gpsimd
    def _(g):
        barrier_waits(g, "g")

    bctx.close()

    kvK = sb("kvK", [128, 2, 32, 128], BF16)
    kvV = sb("kvV", [128, 2, 32, 128], BF16)
    pt = sb("pt", [128, 3, 1024], BF16)
    attnT = sb("attnT", [128, 8, 512], BF16)
    rcp = sb("rcp", [128, 512], F32)
    xrc = sb("xrc", [128, 2, 512], F32)
    rbuf = sb("rbuf", [128, 8, 512], F32)
    rsq = sb("rsq", [128, 512], F32)
    rbuf16 = sb("rbuf16", [128, 8, 512], BF16)
    rsq16 = sb("rsq16", [128, 2, 512], BF16)
    nrm = sb("nrm", [128, 4, 512], F32)
    xmt = sb("xmt", [128, 8, 512], BF16)
    xm32c = sb("xm32c", [128, 2, 512], F32)
    cT32 = sb("cT32", [128, 8, 512], F32)
    cT16 = sb("cT16", [128, 2, 8, 512], BF16)
    fwc = sb("fwc", [128, 2, 2, 8, 128], BF16)
    gf = sb("gf", [128, 512], BF16)
    gi = sb("gi", [128, 512], BF16)
    gg = sb("gg", [128, 512], BF16)
    hc = sb("hc", [128, 2, 512], F32)

    @defer_scalar
    def _(a):
        wt(a, "a", "dC", CONST_DONE)
        for j in range(8):
            a.activation(cT32[:, j, :], cT32[:, j, :], AF.Identity,
                         bias=incs[:, j:j + 1], scale=0.0).then_maybe_inc(
                (SEMS["A"], 1) if j == 7 else None)
        pass
    C["A"] += 1
    CINIT_A = C["A"]

    @block.vector
    def _(v):
        wt(v, "v", "A", CINIT_A)
        for j in range(8):
            v.tensor_copy(cT16[:, 0, j, :], cT32[:, j, :]).then_maybe_inc(
                (SEMS["V"], 1) if j == 7 else None)
        pass
        C["V"] += 1
    cready = C["V"]

    FCHUNKS = []
    for _t in range(B):
        for _mo in range(8):
            for _g in range(3):
                FCHUNKS.append((2 * _g, 2 * _g + 1, _mo))
    for _t in range(B):
        for _oc in range(8):
            FCHUNKS.append((6, 7, _oc))
    chunk_mm = {}
    chunk_ld = {}
    chunk_act = {}
    cupd_last = {-1: cready}
    ffn_state = {"last_cupd": cready}

    def emit_fwc_load(ci):
        if ci >= len(FCHUNKS):
            return
        wm, um, oc_ = FCHUNKS[ci]
        buf = ci % 2
        key = "dF" + str(buf)

        @block.gpsimd
        def _(g, ci=ci, wm=wm, um=um, oc_=oc_, buf=buf, key=key):
            wt(g, "g", "dW", W_DONE)
            if ci - 2 in chunk_mm:
                wt(g, "g", "P", chunk_mm[ci - 2])
            dma(g, key, fwc[:, buf, 0, :, :], d_fcache[wm, oc_])
            dma(g, key, fwc[:, buf, 1, :, :], d_fcache[um, oc_])
        chunk_ld[ci] = (key, C[key])

    emit_fwc_load(0)
    emit_fwc_load(1)

    def emit_xmt_load(step):
        @block.sync
        def _(s, step=step):
            for dk in ("dN0", "dN1", "dN2", "dN3"):
                wt(s, "s", dk, C[dk])
            if step > 0:
                wt(s, "s", "P", chunk_mm[step * 24 - 1])
            dma(s, "dM", xmt[:, :, :], d_xmid16[step].transpose([1, 0, 2]))
        return C["dM"]

    def emit_gate_part(ci, step, mo, gg_i, part, xmt_ld, extraV=0):
        buf = ci % 2
        bank = 4 + (ci % 2)
        func = (AF.Sigmoid, AF.Sigmoid, AF.Tanh)[gg_i]

        @block.tensor
        def _(t, ci=ci, buf=buf, bank=bank, xmt_ld=xmt_ld, step=step,
              part=part, extraV=extraV, cw=cupd_last[step - 1]):
            if part == 0:
                wt(t, "t", chunk_ld[ci][0], chunk_ld[ci][1])
                wt(t, "t", "dM", xmt_ld)
                if extraV:
                    wt(t, "t", "V", extraV)
                if ci - 2 in chunk_act:
                    wt(t, "t", "A", chunk_act[ci - 2])
            elif part == 2:
                wt(t, "t", "V", cw)
            w_u = part // 2
            src = xmt if w_u == 0 else None
            for j in range(4):
                ic = (part % 2) * 4 + j
                rhs = xmt[:, ic, :] if w_u == 0 else cT16[:, step % 2, ic, :]
                t.matmul(PS[:, bank * 512:(bank + 1) * 512],
                         fwc[:, buf, w_u, ic, :], rhs,
                         start=(part == 0 and j == 0),
                         stop=(part == 3 and j == 3)).then_maybe_inc(
                    (SEMS["P"], 1) if (part == 3 and j == 3) else None)
            pass
            if part == 3:
                C["P"] += 1
        if part == 3:
            chunk_mm[ci] = C["P"]
            emit_fwc_load(ci + 2)

            @defer_scalar
            def _(a, bank=bank, gg_i=gg_i, mo=mo, func=func, gmm=chunk_mm[ci],
                  pcu=(ffn_state["last_cupd"] if gg_i == 0 else 0)):
                wt(a, "a", "P", gmm)
                if pcu:
                    wt(a, "a", "V", pcu)
                dstb = (gf, gi, gg)[gg_i]
                a.activation(dstb[:, :], PS[:, bank * 512:(bank + 1) * 512],
                             func, bias=bffs[:, gg_i * 8 + mo:gg_i * 8 + mo + 1]
                             ).then_inc(SEMS["A"], 1)
            C["A"] += 1
            chunk_act[ci] = C["A"]

    FFN_TILES = [3, 5, 8, 10, 13, 15, 18, 20, 23, 25, 28, 30]
    FFN_TILES_P0 = [8, 10, 12, 14, 17, 19, 22, 24, 26, 28, 29, 30]

    def emit_gate_unit(ci, step, mo, gg_i, xmt_ld, extraV=0):
        for part in range(4):
            emit_gate_part(ci, step, mo, gg_i, part, xmt_ld,
                           extraV=(extraV if part == 0 else 0))

    def emit_cupdate(step, mo):
        nxt = (step + 1) % 2

        @block.vector
        def _(v, mo=mo, nxt=nxt, gact=chunk_act[step * 24 + mo * 3 + 2]):
            wt(v, "v", "A", gact)
            v.tensor_mul(tmpa[:, :], gf[:, :], cT32[:, mo, :])
            v.tensor_mul(tmpb[:, :], gi[:, :], gg[:, :])
            v.tensor_add(cT32[:, mo, :], tmpa[:, :], tmpb[:, :])
            v.tensor_copy(cT16[:, nxt, mo, :],
                          cT32[:, mo, :]).then_inc(SEMS["V"], 1)
            C["V"] += 1
        ffn_state["last_cupd"] = C["V"]

    xmt_lds = {}

    def kv_prefetch(s, b, pair, buf):
        key = "dV" + str(buf)
        for c in range(NC):
            dma(s, key, kvK[:, buf, 4 * c:4 * (c + 1), :], d_kvfull[b, c, pair, 0])
            dma(s, key, kvV[:, buf, 4 * c:4 * (c + 1), :], d_kvfull[b, c, pair, 1])
        return (key, C[key])

    kvready = {}

    @block.sync
    def _(s):
        wt(s, "s", "CC", CC_KV[0])
        kvready[(0, 0)] = kv_prefetch(s, 0, 0, 0)

    prev_epilogue_V = 0
    prev_pvden_P = 0
    wo_done_P = 0
    attn_done_V = {}
    wo_radds = {}
    wo_nrm_V = {}
    exp_slot_wait = {0: 0, 1: 0}

    for b in range(B):
        for pair in range(8):
            bp = b * 8 + pair
            buf = bp % 2

            @block.sync
            def _(s, bp=bp):
                if bp + 1 < 32:
                    nb, npair = divmod(bp + 1, 8)
                    if bp >= 1:
                        wt(s, "s", "P", prev_pvden_P)
                    wt(s, "s", "CC", CC_KV[nb])
                    kvready[(nb, npair)] = kv_prefetch(s, nb, npair, (bp + 1) % 2)

            if b >= 1 and pair == 0:
                xmt_lds[b - 1] = emit_xmt_load(b - 1)

            P_sT = {}
            A_exp = {}
            V_mask = {}

            def emit_dummies(epiV=prev_epilogue_V):
                @block.tensor
                def _(t, epiV=epiV):
                    wt(t, "t", "V", epiV)
                    t.matmul(PO[:, :], on_row16[0:1, 0:128], zrow[0:1, :],
                             start=True, stop=True)
                    t.matmul(PD[:, :], on_row16[0:1, 0:128], zrow[0:1, :],
                             start=True, stop=True)

            def emit_pv(tt, buf=buf):
                slot = tt % 3
                n0 = 32 * (tt // 2)
                N = 512 - n0

                @block.tensor
                def _(t, buf=buf, tt=tt, slot=slot, n0=n0, N=N, vm=V_mask[tt]):
                    wt(t, "t", "V", vm)
                    t.matmul(PO[0:64, n0:512], kvV[:, buf, tt, 0:64],
                             pt[:, slot, 0:N], start=False, stop=False,
                             tile_position=(0, 0), skip_group_check=True)
                    t.matmul(PO[64:128, n0:512], kvV[:, buf, tt, 64:128],
                             pt[:, slot, 512:512 + N], start=False, stop=False,
                             tile_position=(0, 64), skip_group_check=True)
                    t.matmul(PD[0:64, n0:512], on_mat[:, :],
                             pt[:, slot, 0:N], start=False, stop=False,
                             tile_position=(0, 0), skip_group_check=True)
                    t.matmul(PD[64:128, n0:512], on_mat[:, :],
                             pt[:, slot, 512:512 + N], start=False, stop=False,
                             tile_position=(0, 64), skip_group_check=True
                             ).then_inc(SEMS["P"], 1)
                C["P"] += 1

            for tt in range(32):
                slot = tt % 2
                n0 = 32 * (tt // 2)
                N = 512 - n0
                if tt == 0:
                    emit_dummies()
                if tt >= 3:
                    emit_pv(tt - 3)
                ftl = FFN_TILES_P0 if pair == 0 else FFN_TILES
                if b >= 1 and tt in ftl:
                    idx = ftl.index(tt)
                    g = idx // 4
                    part = idx % 4
                    ci = (b - 1) * 24 + pair * 3 + g
                    emit_gate_part(ci, b - 1, pair, g, part, xmt_lds[b - 1],
                                   extraV=(wo_nrm_V[b - 1]
                                           if (pair == 0 and g == 0 and part == 0)
                                           else 0))

                @block.tensor
                def _(t, b=b, pair=pair, buf=buf, tt=tt, slot=slot, n0=n0, N=N,
                      ew=exp_slot_wait[slot],
                      wonrm=(wo_radds[b - 1][7] if (pair == 0 and b > 0) else None)):
                    if tt == 0:
                        wt(t, "t", kvready[(b, pair)][0], kvready[(b, pair)][1])
                        wt(t, "t", "V", BIAS_B[b])
                        if wonrm is not None:
                            wt(t, "t", "V", wonrm)
                    if ew:
                        wt(t, "t", "A", ew)
                    qfe = qhat[0:64, b, pair, :, :].rearrange(
                        "p a c -> p (a c)")[:, n0:512]
                    qfo = qhat[64:128, b, pair, :, :].rearrange(
                        "p a c -> p (a c)")[:, n0:512]
                    t.matmul(PS[:, slot * 1024:slot * 1024 + N],
                             kvK[0:64, buf, tt, :], qfe,
                             start=True, stop=True)
                    t.matmul(PS[:, slot * 1024 + 512:slot * 1024 + 512 + N],
                             kvK[64:128, buf, tt, :], qfo,
                             start=True, stop=True,
                             tile_position=(64, 0)).then_inc(SEMS["P"], 1)
                C["P"] += 1
                P_sT[tt] = C["P"]

                @defer_scalar
                def _(a, b=b, pair=pair, slot=slot, pslot=tt % 3, N=N, stp=P_sT[tt]):
                    wt(a, "a", "P", stp)
                    inap = PS[:, slot * 1024:(slot + 1) * 1024].rearrange(
                        "p (two n) -> p two n", two=2)[:, :, 0:N]
                    outap = pt[:, pslot, :].rearrange(
                        "p (two n) -> p two n", two=2)[:, :, 0:N]
                    a.activation(outap, inap, AF.Exp,
                                 bias=biasB[:, b * 8 + pair:b * 8 + pair + 1]
                                 ).then_inc(SEMS["A"], 1)
                C["A"] += 1
                A_exp[tt] = C["A"]
                exp_slot_wait[slot] = C["A"]

                @block.vector
                def _(v, tt=tt, pslot=tt % 3, N=N, ae=A_exp[tt]):
                    if tt == 0:
                        wt(v, "v", "dW", W_DONE)
                    wt(v, "v", "A", ae)
                    pap = bass.AP(pt.handle if hasattr(pt, "handle") else pt,
                                  pslot * 1024, [[3072, 128], [512, 2], [1, 32]])
                    v.tensor_mul(pap, pap, masks[:, tt % 2, :, :]).then_inc(
                        SEMS["V"], 1)
                C["V"] += 1
                V_mask[tt] = C["V"]

            emit_pv(29)
            emit_pv(30)
            emit_pv(31)
            prev_pvden_P = C["P"]

            @defer_scalar
            def _(a, pv=prev_pvden_P):
                wt(a, "a", "P", pv)
                a.activation(rsq[:, :], PD[:, :], AF.Ln, bias=cvals[:, 1:2])
                a.activation(rcp[:, :], rsq[:, :], AF.Exp, scale=-1.0).then_inc(SEMS["A"], 1)
            C["A"] += 1
            epi_a = C["A"]

            @block.vector
            def _(v, pair=pair, epi_a=epi_a, b=b, wod=wo_done_P):
                wt(v, "v", "A", epi_a)
                if b > 0:
                    wt(v, "v", "P", wod)
                v.tensor_mul(attnT[:, pair, :], PO[:, :], rcp[:, :]).then_inc(SEMS["V"], 1)
                C["V"] += 1
            prev_epilogue_V = C["V"]

            if b >= 1:
                emit_cupdate(b - 1, pair)
                if pair == 7:
                    cupd_last[b - 1] = C["V"]

        attn_done_V[b] = prev_epilogue_V

        @block.sync
        def _(s, b=b):
            if b > 0:
                wt(s, "s", "V", wo_radds[b - 1][6])
                wt(s, "s", "V", wo_radds[b - 1][7])
            dma(s, "dR0", xrc[:, 0, :], d_xq[b, 0:128, :])
            dma(s, "dR1", xrc[:, 1, :], d_xq[b, 128:256, :])
        xrc_loads = {0: ("dR0", C["dR0"]), 1: ("dR1", C["dR1"])}

        wo_units = []
        radds = {}
        for oc in range(8):
            bank = oc % 2

            @block.tensor
            def _(t, b=b, oc=oc, bank=bank):
                wt(t, "t", "V", attn_done_V[b])
                if oc >= 2:
                    wt(t, "t", "V", wo_units[oc - 2]["radd"])
                for cc in range(8):
                    t.matmul(PS[:, bank * 512:(bank + 1) * 512],
                             w_o[:, cc, oc * 128:(oc + 1) * 128],
                             attnT[:, cc, :],
                             start=(cc == 0), stop=(cc == 7)).then_maybe_inc(
                        (SEMS["P"], 1) if cc == 7 else None)
                pass
                C["P"] += 1
            womm = C["P"]

            @block.vector
            def _(v, oc=oc, bank=bank, womm=womm, xl=xrc_loads[oc]):
                wt(v, "v", "P", womm)
                wt(v, "v", xl[0], xl[1])
                v.tensor_add(rbuf[:, oc, :], PS[:, bank * 512:(bank + 1) * 512],
                             xrc[:, oc % 2, :]).then_inc(SEMS["V"], 1)
                C["V"] += 1
            radd = C["V"]
            radds[oc] = radd

            @block.sync
            def _(s, b=b, oc=oc, radd=radd):
                if oc + 2 < 8:
                    wt(s, "s", "V", radd)
                    key = "dR" + str(oc % 2)
                    dma(s, key, xrc[:, oc % 2, :], d_xq[b, (oc + 2) * 128:(oc + 3) * 128, :])
                    xrc_loads[oc + 2] = (key, C[key])

            @defer_scalar
            def _(a, oc=oc, radd=radd):
                wt(a, "a", "V", radd)
                a.activation(rbuf16[:, oc, :], rbuf[:, oc, :],
                             AF.Identity).then_inc(SEMS["A"], 1)
            C["A"] += 1
            rb16_a = C["A"]

            @block.vector
            def _(v, oc=oc, pstat=(wo_units[oc - 2]["stats"] if oc >= 2 else 0)):
                if pstat:
                    wt(v, "v", "P", pstat)
                v.tensor_mul(rsq16[:, oc % 2, :], rbuf[:, oc, :],
                             rbuf[:, oc, :]).then_inc(SEMS["V"], 1)
                C["V"] += 1
            rsq_a = C["V"]

            def wo_stats_emit(oc=oc, rb16_a=rb16_a, rsq_a=rsq_a):
                @block.tensor
                def _(t, oc=oc, rb16_a=rb16_a, rsq_a=rsq_a):
                    wt(t, "t", "A", rb16_a)
                    wt(t, "t", "V", rsq_a)
                    t.matmul(PD[0:1, 0:512], on_col16[:, 0:1], rbuf16[:, oc, :],
                             start=(oc == 0), stop=(oc == 7))
                    t.matmul(PO[0:1, 0:512], on_col16[:, 0:1], rsq16[:, oc % 2, :],
                             start=(oc == 0), stop=(oc == 7)).then_inc(SEMS["P"], 1)
                    C["P"] += 1
            wo_units.append(dict(radd=radd, emit=wo_stats_emit))
            if oc > 0:
                wo_units[oc - 1]["emit"]()
                wo_units[oc - 1]["stats"] = C["P"]

        wo_units[7]["emit"]()
        wo_units[7]["stats"] = C["P"]

        wo_radds[b] = radds
        ln_stats = wo_units[-1]["stats"]

        @block.vector
        def _(v, ln_stats=ln_stats):
            wt(v, "v", "P", ln_stats)
            v.tensor_scalar(mu_s, PD[0:1, 0:512], 1.0 / D, None, ALU.mult)
            v.tensor_scalar(var_s, PO[0:1, 0:512], 1.0 / D, None, ALU.mult)
            v.tensor_mul(m2_s, mu_s, mu_s)
            v.tensor_sub(var_s, var_s, m2_s).then_inc(SEMS["V"], 1)
            C["V"] += 1
        ln_v = C["V"]

        @defer_scalar
        def _(a, ln_v=ln_v):
            wt(a, "a", "V", ln_v)
            a.activation(sd_s, var_s, AF.Ln, bias=cvals[0:1, 0:1])
            a.activation(var_s, sd_s, AF.Exp, scale=-0.5).then_inc(SEMS["A"], 1)
        C["A"] += 1
        ln_r = C["A"]

        @block.tensor
        def _(t, ln_r=ln_r, ga=chunk_act.get(b * 24 - 1, 0)):
            wt(t, "t", "A", ln_r)
            if ga:
                wt(t, "t", "A", ga)
            t.matmul(PS[:, 2048:2560], on_row[0:1, 0:128], mu_s, start=True, stop=True)
            t.matmul(PS[:, 2560:3072], on_row[0:1, 0:128], var_s, start=True,
                     stop=True).then_inc(SEMS["P"], 1)
            C["P"] += 1
        ln_bc = C["P"]

        nrm_dmas = {}
        for oc in range(8):
            @block.vector
            def _(v, oc=oc, ln_bc=ln_bc, nrm_dmas=nrm_dmas):
                wt(v, "v", "P", ln_bc)
                if oc >= 4:
                    wt(v, "v", nrm_dmas[oc - 4][0], nrm_dmas[oc - 4][1])
                v.tensor_sub(tmpa[:, :], rbuf[:, oc, :], PS[:, 2048:2560])
                v.tensor_mul(nrm[:, oc % 4, :], tmpa[:, :],
                             PS[:, 2560:3072]).then_inc(SEMS["V"], 1)
                C["V"] += 1
            nrm_v = C["V"]

            @block.gpsimd
            def _(g, b=b, oc=oc, nrm_v=nrm_v, nrm_dmas=nrm_dmas):
                wt(g, "g", "V", nrm_v)
                key = "dN" + str(oc % 4)
                dma(g, key, d_xmid32[b, oc], nrm[:, oc % 4, :])
                dma(g, key, d_xmid16[b, oc], nrm[:, oc % 4, :])
                nrm_dmas[oc] = (key, C[key])
        wo_nrm_V[b] = C["V"]
        wo_done_P = ln_bc

    xmt_lds[3] = emit_xmt_load(3)
    for mo in range(8):
        for g in range(3):
            ci = 3 * 24 + mo * 3 + g
            emit_gate_unit(ci, 3, mo, g, xmt_lds[3],
                           extraV=(wo_nrm_V[3] if (mo == 0 and g == 0) else 0))
        emit_cupdate(3, mo)
    cupd_last[3] = C["V"]

    CFIN = B % 2
    ffn_c_done = C["V"]
    CI = 96

    h_radds = {}
    nrm_dmas5 = {}
    WO_DN = tuple((dk, C[dk]) for dk in ("dN0", "dN1", "dN2", "dN3"))
    ph5_prev_lnv = 0
    ph5_prev_nrmv = 0
    for tt in range(B):
        @block.sync
        def _(s, tt=tt, lastmm=chunk_mm[CI - 1], dn=WO_DN):
            for dk, dv in dn:
                wt(s, "s", dk, dv)
            wt(s, "s", "P", lastmm)
            dma(s, "dM", xmt[:, :, :], d_xmid16[tt].transpose([1, 0, 2]))
        hx_load = C["dM"]

        hu = []
        for oc in range(8):
            ci = CI
            CI += 1
            buf = ci % 2

            @block.tensor
            def _(t, oc=oc, ci=ci, buf=buf, hx_load=hx_load, ffn_c_done=ffn_c_done):
                wt(t, "t", chunk_ld[ci][0], chunk_ld[ci][1])
                wt(t, "t", "dM", hx_load)
                wt(t, "t", "V", ffn_c_done)
                bank = oc % 2
                if oc >= 2:
                    wt(t, "t", "V", hu[oc - 2]["radd"])
                for ic in range(8):
                    t.matmul(PS[:, bank * 512:(bank + 1) * 512],
                             fwc[:, buf, 0, ic, :], xmt[:, ic, :],
                             start=(ic == 0), stop=False)
                for ic in range(8):
                    t.matmul(PS[:, bank * 512:(bank + 1) * 512],
                             fwc[:, buf, 1, ic, :], cT16[:, CFIN, ic, :],
                             start=False, stop=(ic == 7)).then_maybe_inc(
                        (SEMS["P"], 1) if ic == 7 else None)
                pass
                C["P"] += 1
            hmm_ = C["P"]
            chunk_mm[ci] = hmm_
            emit_fwc_load(ci + 2)

            @defer_scalar
            def _(a, oc=oc, hmm_=hmm_):
                wt(a, "a", "P", hmm_)
                a.activation(hc[:, oc % 2, :], PS[:, (oc % 2) * 512:(oc % 2 + 1) * 512],
                             AF.Tanh, bias=bffs[:, 24 + oc:25 + oc]).then_inc(SEMS["A"], 1)
            C["A"] += 1
            hact = C["A"]

            @block.sync
            def _(s, tt=tt, oc=oc):
                key = "dO" + str(oc % 2)
                tenant = h_radds.get((tt, oc - 2)) if oc >= 2 else \
                    (h_radds.get((tt - 1, oc + 6)) if tt > 0 else None)
                if tenant is not None:
                    wt(s, "s", "V", tenant)
                dma(s, key, xm32c[:, oc % 2, :], d_xmid32[tt, oc])
            xm_ld = (("dO" + str(oc % 2)), C["dO" + str(oc % 2)])

            @block.vector
            def _(v, oc=oc, hact=hact, xm_ld=xm_ld):
                wt(v, "v", "A", hact)
                wt(v, "v", xm_ld[0], xm_ld[1])
                v.tensor_add(rbuf[:, oc, :], hc[:, oc % 2, :],
                             xm32c[:, oc % 2, :]).then_inc(SEMS["V"], 1)
                C["V"] += 1
            radd = C["V"]
            h_radds[(tt, oc)] = radd

            @defer_scalar
            def _(a, oc=oc, radd=radd):
                wt(a, "a", "V", radd)
                a.activation(rbuf16[:, oc, :], rbuf[:, oc, :],
                             AF.Identity).then_inc(SEMS["A"], 1)
            C["A"] += 1
            rb16_a = C["A"]

            @block.vector
            def _(v, oc=oc, pstat=(hu[oc - 2]["stats"] if oc >= 2 else 0)):
                if pstat:
                    wt(v, "v", "P", pstat)
                v.tensor_mul(rsq16[:, oc % 2, :], rbuf[:, oc, :],
                             rbuf[:, oc, :]).then_inc(SEMS["V"], 1)
                C["V"] += 1
            rsq_a = C["V"]

            def h_stats_emit(oc=oc, rb16_a=rb16_a, rsq_a=rsq_a,
                             plnv=ph5_prev_lnv):
                @block.tensor
                def _(t, oc=oc, rb16_a=rb16_a, rsq_a=rsq_a, plnv=plnv):
                    if oc == 0 and plnv:
                        wt(t, "t", "V", plnv)
                    wt(t, "t", "A", rb16_a)
                    wt(t, "t", "V", rsq_a)
                    t.matmul(PD[0:1, 0:512], on_col16[:, 0:1], rbuf16[:, oc, :],
                             start=(oc == 0), stop=(oc == 7))
                    t.matmul(PO[0:1, 0:512], on_col16[:, 0:1], rsq16[:, oc % 2, :],
                             start=(oc == 0), stop=(oc == 7)).then_inc(SEMS["P"], 1)
                    C["P"] += 1
            hu.append(dict(radd=radd, emit=h_stats_emit))
            if oc > 0:
                hu[oc - 1]["emit"]()
                hu[oc - 1]["stats"] = C["P"]
        hu[7]["emit"]()
        hu[7]["stats"] = C["P"]

        @block.vector
        def _(v, hu=hu):
            wt(v, "v", "P", hu[-1]["stats"])
            v.tensor_scalar(mu_s, PD[0:1, 0:512], 1.0 / D, None, ALU.mult)
            v.tensor_scalar(var_s, PO[0:1, 0:512], 1.0 / D, None, ALU.mult)
            v.tensor_mul(m2_s, mu_s, mu_s)
            v.tensor_sub(var_s, var_s, m2_s).then_inc(SEMS["V"], 1)
            C["V"] += 1
        ln_v = C["V"]

        @defer_scalar
        def _(a, ln_v=ln_v):
            wt(a, "a", "V", ln_v)
            a.activation(sd_s, var_s, AF.Ln, bias=cvals[0:1, 0:1])
            a.activation(var_s, sd_s, AF.Exp, scale=-0.5).then_inc(SEMS["A"], 1)
        C["A"] += 1
        ln_r = C["A"]

        @block.tensor
        def _(t, ln_r=ln_r, pnv=ph5_prev_nrmv):
            wt(t, "t", "A", ln_r)
            if pnv:
                wt(t, "t", "V", pnv)
            t.matmul(PS[:, 2048:2560], on_row[0:1, 0:128], mu_s, start=True, stop=True)
            t.matmul(PS[:, 2560:3072], on_row[0:1, 0:128], var_s, start=True,
                     stop=True).then_inc(SEMS["P"], 1)
            C["P"] += 1
        ln_bc = C["P"]

        for oc in range(8):
            @block.vector
            def _(v, tt=tt, oc=oc, ln_bc=ln_bc):
                wt(v, "v", "P", ln_bc)
                tenant = nrm_dmas5.get((tt, oc - 4)) if oc >= 4 else \
                    (nrm_dmas5.get((tt - 1, oc + 4)) if tt > 0 else None)
                if tenant is not None:
                    wt(v, "v", tenant[0], tenant[1])
                v.tensor_sub(tmpa[:, :], rbuf[:, oc, :], PS[:, 2048:2560])
                v.tensor_mul(nrm[:, oc % 4, :], tmpa[:, :],
                             PS[:, 2560:3072]).then_inc(SEMS["V"], 1)
                C["V"] += 1
            nrm_v = C["V"]

            @block.gpsimd
            def _(g, tt=tt, oc=oc, nrm_v=nrm_v):
                wt(g, "g", "V", nrm_v)
                key = "dN" + str(oc % 4)
                dma(g, key, d_yT[tt, oc], nrm[:, oc % 4, :])
                nrm_dmas5[(tt, oc)] = (key, C[key])
        ph5_prev_lnv = ln_v
        ph5_prev_nrmv = C["V"]

    @block.gpsimd
    def _(g):
        for dk in ("dN0", "dN1", "dN2", "dN3"):
            wt(g, "g", dk, C[dk])

    @block.scalar
    def _(a):
        for fn in SCALAR_OPS:
            fn(a)

    ctx.close()
    return nc


def _host_prep(inputs):
    f32 = np.float32
    x = np.asarray(inputs["L0_attn_x"], f32)
    xT = np.ascontiguousarray(x.transpose(0, 2, 1))

    half = 32
    inv = (1.0 / (np.float32(10000.0) ** (np.arange(half, dtype=f32) / np.float32(half)))).astype(f32)

    def tables(pos, scale):
        ang = (pos.astype(f32)[:, None] * inv[None, :]).astype(f32)
        cos = (np.cos(ang).astype(f32) * f32(scale)).astype(f32)
        sin = (np.sin(ang).astype(f32) * f32(scale)).astype(f32)
        out = np.zeros((4, 128, 1024), f32)
        for s in range(4):
            out[s, :, 0:512] = np.tile(cos[s * 128:(s + 1) * 128], (1, 16))
            out[s, :, 512:1024] = np.tile(sin[s * 128:(s + 1) * 128], (1, 16))
        return out

    wq = np.asarray(inputs["L0_attn_W_q"], f32)
    wdkv = np.asarray(inputs["L0_attn_W_dkv"], f32)
    wuk = np.asarray(inputs["L0_attn_W_uk"], f32)
    wuv = np.asarray(inputs["L0_attn_W_uv"], f32)
    wo = np.asarray(inputs["L0_attn_W_o"], f32)
    gam = np.asarray(inputs["L0_attn_ln_gamma"], f32)
    bet = np.asarray(inputs["L0_attn_ln_beta"], f32)
    fwT = np.ascontiguousarray(np.stack([
        np.asarray(inputs["L0_ffn_wf"], f32).T, np.asarray(inputs["L0_ffn_uf"], f32).T,
        np.asarray(inputs["L0_ffn_wi"], f32).T, np.asarray(inputs["L0_ffn_ui"], f32).T,
        np.asarray(inputs["L0_ffn_wc"], f32).T, np.asarray(inputs["L0_ffn_uc"], f32).T,
        np.asarray(inputs["L0_ffn_wo"], f32).T, np.asarray(inputs["L0_ffn_uo"], f32).T,
    ]))
    bias4 = np.stack([np.asarray(inputs["L0_ffn_" + k], f32) for k in ("bf", "bi", "bc", "bo")])
    bff = np.ascontiguousarray(bias4.reshape(4, 8, 128).transpose(2, 0, 1).reshape(128, 32))
    incol = np.ascontiguousarray(np.asarray(inputs["L0_ffn_in_c"], f32).reshape(8, 128).T)
    gcol = np.ascontiguousarray(gam.reshape(2, 128).T)
    bcol = np.ascontiguousarray(bet.reshape(2, 128).T)

    shared = dict(wq=wq, wdkv=wdkv, wuk=wuk, wuv=wuv, wo=wo, fwT=fwT,
                  bff=bff, incol=incol, gcol=gcol, bcol=bcol,
                  idt=np.eye(128, dtype=f32))

    in_maps = []
    for i in range(NC):
        xq_i = np.ascontiguousarray(xT[:, :, i::8])
        xkv_i = np.ascontiguousarray(xT[:, :, 512 * i:512 * (i + 1)])
        pos_q = np.arange(i, T, 8)
        pos_kv = np.arange(512 * i, 512 * (i + 1))
        k_loc = np.arange(128)
        j_loc = np.arange(16)
        mm_ = (k_loc[:, None] <= (8 * j_loc[None, :] + i))
        maskb = np.zeros((128, 2, 2, 32), np.float32)
        maskb[:, 0, :, 0:16] = mm_[:, None, :]
        maskb[:, 0, :, 16:32] = 1.0
        maskb[:, 1, :, 0:16] = 0.0
        maskb[:, 1, :, 16:32] = mm_[:, None, :]
        m = dict(shared)
        m["xq"] = xq_i
        m["xkv"] = xkv_i
        m["ropek"] = tables(pos_kv, 1.0)
        m["ropeq"] = tables(pos_q, 0.125)
        m["maskb"] = np.ascontiguousarray(maskb.reshape(128, 128)).astype(ml_dtypes.bfloat16)
        in_maps.append(m)
    return in_maps


def kernel(**inputs):
    in_maps = _host_prep(inputs)
    nc = build_program()
    res = run_bass_kernel_spmd(nc, in_maps, core_ids=list(range(NC)))
    outT = np.empty((B, D, T), np.float32)
    for i in range(NC):
        outT[:, :, i::8] = res.results[i]["yT"].reshape(B, D, 512)
    return np.ascontiguousarray(outT.transpose(0, 2, 1))
